# version 17
# speedup vs baseline: 1.5579x; 1.0201x over previous
import numpy as np
import ml_dtypes
import concourse.bass as bass
import concourse.mybir as mybir
from concourse.bass_utils import run_bass_kernel_spmd

F32 = mybir.dt.float32
BF16 = mybir.dt.bfloat16
AF = mybir.ActivationFunctionType
ALU = mybir.AluOpType
ET = mybir.EngineType
BF = ml_dtypes.bfloat16

T = 4096
D = 1024
NCHUNK = 8
SL = 512
NSL = T // SL
DHID = 4096
EPS_LN = 1e-5
EPS_HLN = 1e-6


class Bld:

    def __init__(self, nc):
        self.nc = nc
        self.sems = {}
        self.cnt = {}

    def sem(self, name):
        if name not in self.sems:
            self.sems[name] = self.nc.alloc_semaphore(name=name)
            self.cnt[name] = 0
        return self.sems[name]

    def inc(self, ins, name, k=1):
        ins.then_inc(self.sem(name), k)
        self.cnt[name] += k

    def wait(self, eng, name):
        eng.wait_ge(self.sem(name), self.cnt[name])

    def wait_at(self, eng, name, v):
        eng.wait_ge(self.sem(name), v)

    def barrier(self):
        self.nc.all_engine_barrier()

    def setup_consts(self, vals):
        t = self.sb("consts", [128, len(vals)], F32)
        for i, v in enumerate(vals):
            self.nc.vector.memset(t[:, i : i + 1], v)
            self.nc.const_aps.register(F32, v, t[:, i : i + 1])
        self.barrier()

    def drain_inc(self, eng, name):
        ins = eng.drain()
        self.inc(ins, name, 1)

    def sb(self, name, shape, dtype):
        return self.nc.alloc_sbuf_tensor(name, shape, dtype)

    def ps(self, name, shape, dtype=F32):
        return self.nc.alloc_psum_tensor(name, shape, dtype)

    def mark(self):
        nc = self.nc
        return (nc.sbuf_base, nc.sbuf_top, nc.psum_base, nc.psum_top)

    def restore(self, m):
        nc = self.nc
        nc.sbuf_base, nc.sbuf_top, nc.psum_base, nc.psum_top = m



def emit_l0ffn(b, xs, wrT, wkT, wvT, murk, tag="", xsrc=None):
    nc = b.nc
    m0 = b.mark()
    wsb = b.sb(f"p0w{tag}", [128, 3, 8, 1024], BF16)
    mu = b.sb(f"p0mu{tag}", [128, 4, 8], F32)
    xsl = b.sb(f"p0x{tag}", [128, 8, SL], F32)
    xb = b.sb(f"p0xb{tag}", [128, 8, SL], BF16)
    sqb = b.sb(f"p0sq{tag}", [128, 8, SL], BF16)
    xn = b.sb(f"p0xn{tag}", [128, 8, SL + 1], BF16)
    xr = b.sb(f"p0xr{tag}", [128, 8, SL], BF16)
    xk = b.sb(f"p0xk{tag}", [128, 8, SL], BF16)
    rg = b.sb(f"p0rg{tag}", [128, 8, SL], BF16)
    kk = b.sb(f"p0kk{tag}", [128, 8, SL], BF16)
    tmp = b.sb(f"p0tmp{tag}", [128, 8, SL], F32)
    rows = b.sb(f"p0rows{tag}", [1, 8, SL], F32)
    onesb = b.sb(f"p0ones{tag}", [128, 1], BF16)
    ones1 = b.sb(f"p0one1{tag}", [1, 128], F32)
    ps_s = b.ps(f"p0pss{tag}", [1, SL], F32)
    ps_q = b.ps(f"p0psq{tag}", [1, SL], F32)
    ps_b = b.ps(f"p0psb{tag}", [128, 2, SL], F32)
    ps_y = b.ps(f"p0psy{tag}", [128, 2, SL], F32)

    MU, E2, VAR, RSTD, MR = 0, 1, 2, 3, 4
    if getattr(b, "wcs_flat", 0) > 0:
        b.wait_at(nc.sync, "wcs", b.wcs_flat)
    if getattr(b, "w2s_flat", 0) > 0:
        b.wait_at(nc.sync, "w2s", b.w2s_flat)
    for wi, w in enumerate((wrT, wkT, wvT)):
        ins = nc.sync.dma_start(wsb[:, wi, :, :], w.ap().rearrange("(c p) m -> p c m", p=128))
        b.inc(ins, "wld", 16)
    ins = nc.sync.dma_start(mu[:, 0:2, :], murk.ap())
    b.inc(ins, "wld", 16)
    b.wait(nc.vector, "wld")
    nc.vector.memset(onesb[:], 1.0)
    nc.vector.memset(ones1[:], 1.0)
    nc.vector.tensor_scalar(mu[:, 2:4, :], mu[:, 0:2, :], -1.0, 1.0, ALU.mult, ALU.add)
    nc.vector.memset(xn[:, :, 0:1], 0.0)
    b.barrier()

    for s in range(NSL):
        _p0_slice(b, xs, s, wsb, mu, xsl, xb, sqb, xn, xr, xk, rg, kk, tmp, rows,
                  onesb, ones1, ps_s, ps_q, ps_b, ps_y, xsrc=xsrc)
    b.restore(m0)


def _p0_slice(b, xs, s, wsb, mu, xsl, xb, sqb, xn, xr, xk, rg, kk, tmp, rows,
              onesb, ones1, ps_s, ps_q, ps_b, ps_y, xsrc=None):
    nc = b.nc
    MU, E2, VAR, RSTD, MR = 0, 1, 2, 3, 4
    t0 = s * SL
    xrd = xs if xsrc is None else xsrc
    ins = nc.sync.dma_start(xsl[:], xrd[:, :, t0 : t0 + SL].rearrange("c p t -> p c t"))
    b.inc(ins, "xld", 16)
    if s > 0:
        nc.vector.tensor_copy(xn[:, :, 0:1], xn[:, :, SL : SL + 1])
    b.wait(nc.vector, "xld")
    nc.vector.tensor_copy(xb[:], xsl[:])
    ins = nc.vector.tensor_tensor(sqb[:], xb[:], xb[:], ALU.mult)
    b.inc(ins, "dv", 1)
    b.wait(nc.tensor, "dv")
    for c in range(NCHUNK):
        nc.tensor.matmul(ps_s[:], onesb[:], xb[:, c, :], start=(c == 0), stop=(c == 7))
    for c in range(NCHUNK):
        ins = nc.tensor.matmul(ps_q[:], onesb[:], sqb[:, c, :], start=(c == 0), stop=(c == 7))
    b.inc(ins, "mm", 1)
    b.wait(nc.vector, "mm")
    nc.vector.tensor_scalar_mul(rows[:, MU, :], ps_s[:], 1.0 / D)
    nc.vector.tensor_scalar_mul(rows[:, E2, :], ps_q[:], 1.0 / D)
    nc.vector.tensor_tensor(rows[:, VAR, :], rows[:, MU, :], rows[:, MU, :], ALU.mult)
    ins = nc.vector.tensor_tensor(rows[:, VAR, :], rows[:, E2, :], rows[:, VAR, :], ALU.subtract)
    b.inc(ins, "dv", 1)
    b.wait(nc.scalar, "dv")
    ins = nc.scalar.activation(rows[:, RSTD, :], rows[:, VAR, :], AF.Sqrt, bias=EPS_LN)
    b.inc(ins, "ac", 1)
    b.wait(nc.vector, "ac")
    nc.vector.reciprocal(rows[:, RSTD, :], rows[:, RSTD, :])
    ins = nc.vector.tensor_tensor(rows[:, MR, :], rows[:, MU, :], rows[:, RSTD, :], ALU.mult)
    b.inc(ins, "dv", 1)
    b.wait(nc.tensor, "dv")
    nc.tensor.matmul(ps_b[:, 0, :], ones1[:], rows[:, RSTD, :], start=True, stop=True)
    ins = nc.tensor.matmul(ps_b[:, 1, :], ones1[:], rows[:, MR, :], start=True, stop=True)
    b.inc(ins, "mm", 1)
    b.wait(nc.vector, "mm")
    for c in range(NCHUNK):
        nc.vector.tensor_tensor(tmp[:, c, :], xsl[:, c, :], ps_b[:, 0, :], ALU.mult)
        nc.vector.tensor_tensor(xn[:, c, 1 : SL + 1], tmp[:, c, :], ps_b[:, 1, :], ALU.subtract)
    for c in range(NCHUNK):
        nc.vector.tensor_scalar_mul(xr[:, c, :], xn[:, c, 0:SL], mu[:, 2, c : c + 1])
        nc.vector.scalar_tensor_tensor(xr[:, c, :], xn[:, c, 1 : SL + 1], mu[:, 0, c : c + 1],
                                       xr[:, c, :], ALU.mult, ALU.add)
        nc.vector.tensor_scalar_mul(xk[:, c, :], xn[:, c, 0:SL], mu[:, 3, c : c + 1])
        ins = nc.vector.scalar_tensor_tensor(xk[:, c, :], xn[:, c, 1 : SL + 1], mu[:, 1, c : c + 1],
                                             xk[:, c, :], ALU.mult, ALU.add)
    b.inc(ins, "dv", 1)
    b.wait(nc.tensor, "dv")
    for ot in range(NCHUNK):
        par = ot % 2
        if ot >= 2:
            b.wait_at(nc.tensor, "ev", b.cnt["ev"] - 1)
        for c in range(NCHUNK):
            ins = nc.tensor.matmul(ps_y[:, par, :], wsb[:, 0, c, ot * 128 : (ot + 1) * 128],
                                   xr[:, c, :], start=(c == 0), stop=(c == 7))
        b.inc(ins, "mm", 1)
        b.wait(nc.scalar, "mm")
        ins = nc.scalar.activation(rg[:, ot, :], ps_y[:, par, :], AF.Sigmoid)
        b.inc(ins, "ev", 1)
    for ot in range(NCHUNK):
        par = ot % 2
        b.wait_at(nc.tensor, "ev", b.cnt["ev"] - 1)
        for c in range(NCHUNK):
            ins = nc.tensor.matmul(ps_y[:, par, :], wsb[:, 1, c, ot * 128 : (ot + 1) * 128],
                                   xk[:, c, :], start=(c == 0), stop=(c == 7))
        b.inc(ins, "mm", 1)
        b.wait(nc.vector, "mm")
        ins = nc.vector.tensor_scalar_max(tmp[:, ot, :], ps_y[:, par, :], 0.0)
        b.inc(ins, "ev", 1)
        ins = nc.vector.tensor_tensor(kk[:, ot, :], tmp[:, ot, :], tmp[:, ot, :], ALU.mult)
        b.inc(ins, "ac", 1)
    b.wait(nc.tensor, "ac")
    for ot in range(NCHUNK):
        par = ot % 2
        b.wait_at(nc.tensor, "ev", b.cnt["ev"] - 1)
        for c in range(NCHUNK):
            ins = nc.tensor.matmul(ps_y[:, par, :], wsb[:, 2, c, ot * 128 : (ot + 1) * 128],
                                   kk[:, c, :], start=(c == 0), stop=(c == 7))
        b.inc(ins, "mm", 1)
        b.wait(nc.vector, "mm")
        nc.vector.tensor_tensor(tmp[:, ot, :], ps_y[:, par, :], rg[:, ot, :], ALU.mult)
        ins = nc.vector.scalar_tensor_tensor(xsl[:, ot, :], tmp[:, ot, :], 0.5,
                                             xsl[:, ot, :], ALU.mult, ALU.add)
        b.inc(ins, "ev", 1)
    b.drain_inc(nc.vector, "vdr")
    b.wait(nc.gpsimd, "vdr")
    ins = nc.gpsimd.dma_start(xs[:, :, t0 : t0 + SL].rearrange("c p t -> p c t"), xsl[:])
    b.inc(ins, "xst", 16)
    b.wait(nc.gpsimd, "xst")
    b.barrier()



def emit_qkv(b, xs, wqT, wkT, wvT, qkvbias, qkln, bvrow, hpatd, qT, kT, vtok, tag=""):
    nc = b.nc
    m0 = b.mark()
    wsb = b.sb(f"p1w{tag}", [128, 3, 8, 1024], BF16)
    wf = b.sb(f"p1wf{tag}", [128, 8, 1024], F32)
    bias = b.sb(f"p1b{tag}", [128, 3, 8], F32)
    qkl = b.sb(f"p1l{tag}", [128, 4], F32)
    bvb = b.sb(f"p1bvb{tag}", [128, 1024], F32)
    bvr = b.sb(f"p1bvr{tag}", [1, 1024], F32)
    xsl = b.sb(f"p1x{tag}", [128, 8, SL], F32)
    xb = b.sb(f"p1xb{tag}", [128, 8, SL], BF16)
    sqb = b.sb(f"p1sq{tag}", [128, 8, SL], BF16)
    xn = b.sb(f"p1xn{tag}", [128, 8, SL], BF16)
    qb = b.sb(f"p1qb{tag}", [128, 8, SL], BF16)
    sqq = b.sb(f"p1sqq{tag}", [128, 2, SL], BF16)
    qn2 = b.sb(f"p1qn{tag}", [128, 2, 8, SL], BF16)
    vsl = b.sb(f"p1v{tag}", [128, 4, 1024], BF16)
    tmp = b.sb(f"p1tmp{tag}", [128, 4, SL], F32)
    rows = b.sb(f"p1rows{tag}", [1, 8, SL], F32)
    hrows = b.sb(f"p1hr{tag}", [2, 2, 5, SL], F32)
    onesb = b.sb(f"p1ones{tag}", [128, 1], BF16)
    ones1 = b.sb(f"p1one1{tag}", [1, 128], F32)
    blk2 = b.sb(f"p1blk{tag}", [128, 2], BF16)
    hpat = b.sb(f"p1hp{tag}", [2, 128], F32)
    ps_s = b.ps(f"p1pss{tag}", [1, SL], F32)
    ps_q = b.ps(f"p1psq{tag}", [1, SL], F32)
    ps_b = b.ps(f"p1psb{tag}", [128, 2, SL], F32)
    ps_y = b.ps(f"p1psy{tag}", [128, 2, SL], F32)
    ps_h = b.ps(f"p1psh{tag}", [2, 2, SL], F32)

    MU, E2, VAR, RSTD, MR = 0, 1, 2, 3, 4
    for wi, w in enumerate((wqT, wkT, wvT)):
        if wi > 0:
            b.wait(nc.sync, "wfree")
        ins = nc.sync.dma_start(wf[:], w.ap().rearrange("(c p) m -> p c m", p=128))
        b.inc(ins, "wld", 16)
        b.wait(nc.vector, "wld")
        ins = nc.vector.tensor_copy(wsb[:, wi, :, :], wf[:])
        b.inc(ins, "wfree", 1)
    for src, dst in ((qkvbias, bias), (qkln, qkl), (bvrow, bvr)):
        ins = nc.sync.dma_start(dst[:], src.ap())
        b.inc(ins, "wld", 16)
    b.wait(nc.vector, "wld")
    nc.vector.memset(onesb[:], 1.0)
    nc.vector.memset(ones1[:], 1.0)
    nc.vector.memset(blk2[:, :], 0.0)
    nc.vector.memset(blk2[0:64, 0:1], 1.0)
    nc.vector.memset(blk2[64:128, 1:2], 1.0)
    ins = nc.sync.dma_start(hpat[:], hpatd.ap())
    b.inc(ins, "wld", 16)
    b.wait(nc.vector, "wld")
    ins = nc.vector.memset(blk2[64:128, 0:1], 0.0)
    b.inc(ins, "dv", 1)
    b.wait(nc.tensor, "dv")
    nc.tensor.matmul(ps_b[:, 0, :], ones1[:], bvr[:, 0:SL], start=True, stop=True)
    ins = nc.tensor.matmul(ps_b[:, 1, :], ones1[:], bvr[:, SL:D], start=True, stop=True)
    b.inc(ins, "mm", 1)
    b.wait(nc.vector, "mm")
    nc.vector.tensor_copy(bvb[:], ps_b.ap().rearrange("p a t -> p (a t)"))
    b.barrier()

    for s in range(NSL):
        t0 = s * SL
        ins = nc.sync.dma_start(xsl[:], xs[:, :, t0 : t0 + SL].rearrange("c p t -> p c t"))
        b.inc(ins, "xld", 16)
        b.wait(nc.vector, "xld")
        nc.vector.tensor_copy(xb[:], xsl[:])
        ins = nc.vector.tensor_tensor(sqb[:], xb[:], xb[:], ALU.mult)
        b.inc(ins, "dv", 1)
        b.wait(nc.tensor, "dv")
        for c in range(NCHUNK):
            nc.tensor.matmul(ps_s[:], onesb[:], xb[:, c, :], start=(c == 0), stop=(c == 7))
        for c in range(NCHUNK):
            ins = nc.tensor.matmul(ps_q[:], onesb[:], sqb[:, c, :], start=(c == 0), stop=(c == 7))
        b.inc(ins, "mm", 1)
        b.wait(nc.vector, "mm")
        nc.vector.tensor_scalar_mul(rows[:, MU, :], ps_s[:], 1.0 / D)
        nc.vector.tensor_scalar_mul(rows[:, E2, :], ps_q[:], 1.0 / D)
        nc.vector.tensor_tensor(rows[:, VAR, :], rows[:, MU, :], rows[:, MU, :], ALU.mult)
        ins = nc.vector.tensor_tensor(rows[:, VAR, :], rows[:, E2, :], rows[:, VAR, :], ALU.subtract)
        b.inc(ins, "dv", 1)
        b.wait(nc.scalar, "dv")
        ins = nc.scalar.activation(rows[:, RSTD, :], rows[:, VAR, :], AF.Sqrt, bias=EPS_LN)
        b.inc(ins, "ac", 1)
        b.wait(nc.vector, "ac")
        nc.vector.reciprocal(rows[:, RSTD, :], rows[:, RSTD, :])
        ins = nc.vector.tensor_tensor(rows[:, MR, :], rows[:, MU, :], rows[:, RSTD, :], ALU.mult)
        b.inc(ins, "dv", 1)
        b.wait(nc.tensor, "dv")
        nc.tensor.matmul(ps_b[:, 0, :], ones1[:], rows[:, RSTD, :], start=True, stop=True)
        ins = nc.tensor.matmul(ps_b[:, 1, :], ones1[:], rows[:, MR, :], start=True, stop=True)
        b.inc(ins, "mm", 1)
        b.wait(nc.vector, "mm")
        for c in range(NCHUNK):
            nc.vector.tensor_tensor(tmp[:, 0, :], xsl[:, c, :], ps_b[:, 0, :], ALU.mult)
            ins = nc.vector.tensor_tensor(xn[:, c, :], tmp[:, 0, :], ps_b[:, 1, :], ALU.subtract)
        b.inc(ins, "dv", 1)
        b.wait(nc.tensor, "dv")
        for qk in range(2):
            SC, BI = (0, 1) if qk == 0 else (2, 3)
            dst = qT if qk == 0 else kT
            qn = qn2[:, qk, :, :]
            for ot in range(NCHUNK):
                par = ot % 2
                if b.cnt.get("ev", 0) > 0:
                    b.wait_at(nc.tensor, "ev", b.cnt["ev"] - 1)
                for c in range(NCHUNK):
                    ins = nc.tensor.matmul(ps_y[:, par, :], wsb[:, qk, c, ot * 128 : (ot + 1) * 128],
                                           xn[:, c, :], start=(c == 0), stop=(c == 7))
                b.inc(ins, "mm", 1)
                b.wait(nc.vector, "mm")
                ins = nc.vector.tensor_scalar_add(qb[:, ot, :], ps_y[:, par, :],
                                                  bias[:, qk, ot : ot + 1])
                b.inc(ins, "ev", 1)
                ins = nc.vector.tensor_tensor(sqq[:, par, :], qb[:, ot, :], qb[:, ot, :], ALU.mult)
                b.inc(ins, "ac", 1)
                b.wait(nc.tensor, "ac")
                b.wait(nc.tensor, "dv")
                nc.tensor.matmul(ps_h[:, 0, :], blk2[:], qb[:, ot, :], start=True, stop=True)
                ins = nc.tensor.matmul(ps_h[:, 1, :], blk2[:], sqq[:, par, :], start=True, stop=True)
                b.inc(ins, "mm", 1)
                b.wait(nc.vector, "mm")
                hr = hrows[:, par, :, :]
                nc.vector.tensor_scalar_mul(hr[:, MU, :], ps_h[:, 0, :], 1.0 / 64)
                nc.vector.tensor_scalar_mul(hr[:, E2, :], ps_h[:, 1, :], 1.0 / 64)
                nc.vector.tensor_tensor(hr[:, VAR, :], hr[:, MU, :], hr[:, MU, :], ALU.mult)
                ins = nc.vector.tensor_tensor(hr[:, VAR, :], hr[:, E2, :], hr[:, VAR, :], ALU.subtract)
                b.inc(ins, "dv", 1)
                b.wait(nc.scalar, "dv")
                ins = nc.scalar.activation(hr[:, RSTD, :], hr[:, VAR, :], AF.Sqrt, bias=EPS_HLN)
                b.inc(ins, "ac", 1)
                b.wait(nc.vector, "ac")
                nc.vector.reciprocal(hr[:, RSTD, :], hr[:, RSTD, :])
                ins = nc.vector.tensor_tensor(hr[:, MR, :], hr[:, MU, :], hr[:, RSTD, :], ALU.mult)
                b.inc(ins, "dv", 1)
                b.wait(nc.tensor, "dv")
                if b.cnt.get("qnev", 0) > 0:
                    b.wait_at(nc.tensor, "qnev", b.cnt["qnev"])
                nc.tensor.matmul(ps_b[:, 0, :], hpat[:], hr[:, RSTD, :], start=True, stop=True)
                ins = nc.tensor.matmul(ps_b[:, 1, :], hpat[:], hr[:, MR, :], start=True, stop=True)
                b.inc(ins, "mm", 1)
                b.wait(nc.vector, "mm")
                nc.vector.tensor_tensor(tmp[:, 2 * par, :], qb[:, ot, :], ps_b[:, 0, :], ALU.mult)
                nc.vector.tensor_tensor(tmp[:, 2 * par + 1, :], tmp[:, 2 * par, :],
                                        ps_b[:, 1, :], ALU.subtract)
                ins = nc.vector.tensor_scalar(qn[:, ot, :], tmp[:, 2 * par + 1, :],
                                              qkl[:, SC : SC + 1], qkl[:, BI : BI + 1],
                                              ALU.mult, ALU.add)
                b.inc(ins, "qnev", 1)
            b.drain_inc(nc.vector, "vdr")
            b.wait(nc.gpsimd, "vdr")
            ins = nc.gpsimd.dma_start(dst[:, :, t0 : t0 + SL].rearrange("c p t -> p c t"), qn[:])
            b.inc(ins, "qst", 16)
        b.wait(nc.tensor, "qnev")
        for tt in range(4):
            par = tt % 2
            pv = ps_y if par == 0 else ps_b
            if tt >= 2:
                b.wait_at(nc.tensor, "vev", b.cnt["vev"] - 1)
            for half in range(2):
                for c in range(NCHUNK):
                    ins = nc.tensor.matmul(pv[:, half, :],
                                           xn[:, c, tt * 128 : (tt + 1) * 128],
                                           wsb[:, 2, c, half * SL : (half + 1) * SL],
                                           start=(c == 0), stop=(c == 7))
            b.inc(ins, "mm", 1)
            b.wait(nc.vector, "mm")
            ins = nc.vector.tensor_tensor(vsl[:, tt, :], pv.ap().rearrange("p a t -> p (a t)"),
                                          bvb[:], ALU.add)
            b.inc(ins, "vev", 1)
        b.drain_inc(nc.vector, "vdr")
        b.wait(nc.gpsimd, "vdr")
        ins = nc.gpsimd.dma_start(vtok[t0 : t0 + SL, :].rearrange("(q p) d -> p q d", p=128), vsl[:])
        b.inc(ins, "qst", 16)
        b.wait(nc.gpsimd, "qst")
        b.barrier()
    b.restore(m0)


def build_p1():
    nc = bass.Bass("TRN2", target_bir_lowering=False, debug=False)
    xs = nc.dram_tensor("xs", [NCHUNK, 128, T], F32, kind="ExternalInput")
    wqT = nc.dram_tensor("wqT", [D, D], F32, kind="ExternalInput")
    wkTa = nc.dram_tensor("wkTa", [D, D], F32, kind="ExternalInput")
    wvTa = nc.dram_tensor("wvTa", [D, D], F32, kind="ExternalInput")
    qkvbias = nc.dram_tensor("qkvbias", [128, 3, NCHUNK], F32, kind="ExternalInput")
    qkln = nc.dram_tensor("qkln", [128, 4], F32, kind="ExternalInput")
    bvrow = nc.dram_tensor("bvrow", [1, D], F32, kind="ExternalInput")
    hpatd = nc.dram_tensor("hpatd", [2, 128], F32, kind="ExternalInput")
    qT = nc.dram_tensor("qT", [NCHUNK, 128, T], BF16, kind="ExternalOutput")
    kT = nc.dram_tensor("kT", [NCHUNK, 128, T], BF16, kind="ExternalOutput")
    vtok = nc.dram_tensor("vtok", [T, D], BF16, kind="ExternalOutput")
    b = Bld(nc)
    b.setup_consts([EPS_LN, EPS_HLN])
    emit_qkv(b, xs, wqT, wkTa, wvTa, qkvbias, qkln, bvrow, hpatd, qT, kT, vtok)
    return nc


def _p1_inputs(inputs, x_stage):
    def tile8(v):
        return np.ascontiguousarray(v.reshape(NCHUNK, 128).T)
    bias = np.stack([tile8(inputs["L0_attn_bq"]), tile8(inputs["L0_attn_bk"]),
                     tile8(inputs["L0_attn_bv"])], axis=1)
    qkln = np.stack([np.tile(inputs["L0_attn_qn_scale"], 2), np.tile(inputs["L0_attn_qn_bias"], 2),
                     np.tile(inputs["L0_attn_kn_scale"], 2), np.tile(inputs["L0_attn_kn_bias"], 2)],
                    axis=1)
    common = {
        "wqT": np.ascontiguousarray(inputs["L0_attn_Wq"]),
        "wkTa": np.ascontiguousarray(inputs["L0_attn_Wk"]),
        "wvTa": np.ascontiguousarray(inputs["L0_attn_Wv"]),
        "qkvbias": np.ascontiguousarray(bias),
        "qkln": np.ascontiguousarray(qkln),
        "bvrow": np.ascontiguousarray(inputs["L0_attn_bv"][None, :]),
        "hpatd": np.repeat(np.eye(2, dtype=np.float32), 64, axis=1),
    }
    maps = []
    for bc in range(8):
        d = dict(common)
        d["xs"] = np.ascontiguousarray(x_stage[bc].T).reshape(NCHUNK, 128, T)
        maps.append(d)
    return maps



def emit_qkv2(b, xs, wqR, wkR, wvR, qkvbias, qkln, bvrow, hpatd, qT, kT, vtok, tag=""):
    nc = b.nc
    m0 = b.mark()
    MU, E2i, VAR, RSTD, MR = 0, 1, 2, 3, 4
    wsb = b.sb(f"q2w{tag}", [128, 3, 8, 1024], BF16)
    bias = b.sb(f"q2b{tag}", [128, 3, 8], F32)
    qkl = b.sb(f"q2l{tag}", [128, 4], F32)
    bvb = b.sb(f"q2bvb{tag}", [128, 1024], F32)
    bvr = b.sb(f"q2bvr{tag}", [1, 1024], F32)
    xsl = b.sb(f"q2x{tag}", [128, 8, SL], F32)
    xb = b.sb(f"q2xb{tag}", [128, 8, SL], BF16)
    sqb = b.sb(f"q2sq{tag}", [128, 8, SL], BF16)
    xn = b.sb(f"q2xn{tag}", [128, 8, SL], BF16)
    qb = b.sb(f"q2qb{tag}", [128, 2, 8, SL], BF16)
    sqq = b.sb(f"q2sqq{tag}", [128, 2, SL], BF16)
    qn2 = b.sb(f"q2qn{tag}", [128, 2, 8, SL], BF16)
    vsl = b.sb(f"q2v{tag}", [128, 4, 1024], BF16)
    tmp = b.sb(f"q2tmp{tag}", [128, 4, SL], F32)
    rows = b.sb(f"q2rows{tag}", [1, 8, SL], F32)
    hrows = b.sb(f"q2hr{tag}", [2, 2, 5, SL], F32)
    onesb = b.sb(f"q2ones{tag}", [128, 1], BF16)
    ones1 = b.sb(f"q2one1{tag}", [1, 128], F32)
    blk2 = b.sb(f"q2blk{tag}", [128, 2], BF16)
    hpat = b.sb(f"q2hp{tag}", [2, 128], F32)
    ps_y = b.ps(f"q2psy{tag}", [128, 2, SL], F32)
    ps_h = b.ps(f"q2psh{tag}", [128, 2, SL], F32)
    ps_b = b.ps(f"q2psb{tag}", [128, 4, SL], F32)

    if getattr(b, "wcs_flat", 0) > 0:
        b.wait_at(nc.sync, "wcs", b.wcs_flat)
    if getattr(b, "w2s_flat", 0) > 0:
        b.wait_at(nc.sync, "w2s", b.w2s_flat)
    for wi, w in enumerate((wqR, wkR, wvR)):
        ins = nc.sync.dma_start(wsb[:, wi, :, :], w.ap().rearrange("(c p) m -> p c m", p=128))
        b.inc(ins, "wld", 16)
    for src, dst in ((qkvbias, bias), (qkln, qkl), (bvrow, bvr), (hpatd, hpat)):
        ins = nc.sync.dma_start(dst[:], src.ap())
        b.inc(ins, "wld", 16)
    b.wait(nc.vector, "wld")
    nc.vector.memset(onesb[:], 1.0)
    nc.vector.memset(ones1[:], 1.0)
    nc.vector.memset(blk2[:, :], 0.0)
    nc.vector.memset(blk2[0:64, 0:1], 1.0)
    ins = nc.vector.memset(blk2[64:128, 1:2], 1.0)
    b.inc(ins, "dv", 1)
    b.wait(nc.tensor, "dv")
    nc.tensor.matmul(ps_b[:, 0, :], ones1[:], bvr[:, 0:SL], start=True, stop=True)
    ins = nc.tensor.matmul(ps_b[:, 1, :], ones1[:], bvr[:, SL:D], start=True, stop=True)
    b.inc(ins, "mm", 1)
    b.wait(nc.vector, "mm")
    nc.vector.tensor_copy(bvb[:, 0:SL], ps_b[:, 0, :])
    ins = nc.vector.tensor_copy(bvb[:, SL:D], ps_b[:, 1, :])
    b.inc(ins, "dv", 1)
    b.barrier()

    for s in range(NSL):
        t0 = s * SL
        ins = nc.sync.dma_start(xsl[:], xs[:, :, t0 : t0 + SL].rearrange("c p t -> p c t"))
        b.inc(ins, "xld", 16)
        b.wait(nc.vector, "xld")
        _ln_to_xn(b, xsl, xn, xb, sqb, rows, onesb, ones1,
                  ps_h[0:1, 0, :], ps_h[64:65, 0, :], ps_b[:, 0:2, :], tmp[:, 0, :])
        b.wait(nc.tensor, "dv")
        qmm0 = b.cnt.get("qmm", 0)
        qbe0 = b.cnt.get("qbe", 0)
        qsq0 = b.cnt.get("qsq", 0)
        smm0 = b.cnt.get("smm", 0)
        dvv0 = b.cnt.get("dvv", 0)
        acs0 = b.cnt.get("acs", 0)
        hrm0 = b.cnt.get("hrm", 0)
        bmm0 = b.cnt.get("bmm", 0)
        apl0 = b.cnt.get("apl", 0)

        def P1(u):
            qk, ot = u // 8, u % 8
            if u >= 2:
                b.wait_at(nc.tensor, "qbe", qbe0 + u - 1)
            for c in range(NCHUNK):
                ins = nc.tensor.matmul(ps_y[:, u % 2, :], wsb[:, qk, c, ot * 128 : (ot + 1) * 128],
                                       xn[:, c, :], start=(c == 0), stop=(c == 7),
                                       skip_group_check=True)
            b.inc(ins, "qmm", 1)

        def P2(u):
            qk, ot = u // 8, u % 8
            b.wait_at(nc.tensor, "qsq", qsq0 + u + 1)
            if u >= 2:
                b.wait_at(nc.tensor, "dvv", dvv0 + u - 1)
            nc.tensor.matmul(ps_h[0:2, u % 2, :], blk2[:], qb[:, qk, ot, :],
                             start=True, stop=True, skip_group_check=True)
            ins = nc.tensor.matmul(ps_h[64:66, u % 2, :], blk2[:], sqq[:, u % 2, :],
                                   start=True, stop=True, skip_group_check=True)
            b.inc(ins, "smm", 1)

        def P3(u):
            hr = hrows[:, u % 2, :, :]
            b.wait_at(nc.tensor, "hrm", hrm0 + u + 1)
            if u >= 2:
                b.wait_at(nc.tensor, "apl", apl0 + u - 1)
            nc.tensor.matmul(ps_b[:, 2 * (u % 2), :], hpat[:], hr[:, RSTD, :],
                             start=True, stop=True, skip_group_check=True)
            ins = nc.tensor.matmul(ps_b[:, 2 * (u % 2) + 1, :], hpat[:], hr[:, MR, :],
                                   start=True, stop=True, skip_group_check=True)
            b.inc(ins, "bmm", 1)

        def E1(u):
            qk, ot = u // 8, u % 8
            b.wait_at(nc.vector, "qmm", qmm0 + u + 1)
            ins = nc.vector.tensor_scalar_add(qb[:, qk, ot, :], ps_y[:, u % 2, :],
                                              bias[:, qk, ot : ot + 1])
            b.inc(ins, "qbe", 1)

        def E2(u):
            qk, ot = u // 8, u % 8
            if u >= 2:
                b.wait_at(nc.vector, "smm", smm0 + u - 1)
            ins = nc.vector.tensor_tensor(sqq[:, u % 2, :], qb[:, qk, ot, :],
                                          qb[:, qk, ot, :], ALU.mult)
            b.inc(ins, "qsq", 1)

        def E3(u):
            hr = hrows[:, u % 2, :, :]
            b.wait_at(nc.vector, "smm", smm0 + u + 1)
            if u >= 2:
                b.wait_at(nc.vector, "bmm", bmm0 + u - 1)
            nc.vector.tensor_scalar_mul(hr[:, MU, :], ps_h[0:2, u % 2, :], 1.0 / 64)
            nc.vector.tensor_scalar_mul(hr[:, E2i, :], ps_h[64:66, u % 2, :], 1.0 / 64)
            nc.vector.tensor_tensor(hr[:, VAR, :], hr[:, MU, :], hr[:, MU, :], ALU.mult)
            ins = nc.vector.tensor_tensor(hr[:, VAR, :], hr[:, E2i, :], hr[:, VAR, :], ALU.subtract)
            b.inc(ins, "dvv", 1)

        def A1(u):
            hr = hrows[:, u % 2, :, :]
            b.wait_at(nc.scalar, "dvv", dvv0 + u + 1)
            ins = nc.scalar.activation(hr[:, RSTD, :], hr[:, VAR, :], AF.Sqrt, bias=EPS_HLN)
            b.inc(ins, "acs", 1)

        def E4(u):
            hr = hrows[:, u % 2, :, :]
            b.wait_at(nc.vector, "acs", acs0 + u + 1)
            nc.vector.reciprocal(hr[:, RSTD, :], hr[:, RSTD, :])
            ins = nc.vector.tensor_tensor(hr[:, MR, :], hr[:, MU, :], hr[:, RSTD, :], ALU.mult)
            b.inc(ins, "hrm", 1)

        def E5(u):
            qk, ot = u // 8, u % 8
            SC, BI = (0, 1) if qk == 0 else (2, 3)
            b.wait_at(nc.vector, "bmm", bmm0 + u + 1)
            nc.vector.tensor_tensor(tmp[:, 2, :], qb[:, qk, ot, :], ps_b[:, 2 * (u % 2), :],
                                    ALU.mult)
            nc.vector.tensor_tensor(tmp[:, 3, :], tmp[:, 2, :], ps_b[:, 2 * (u % 2) + 1, :],
                                    ALU.subtract)
            ins = nc.vector.tensor_scalar(qn2[:, qk, ot, :], tmp[:, 3, :],
                                          qkl[:, SC : SC + 1], qkl[:, BI : BI + 1],
                                          ALU.mult, ALU.add)
            b.inc(ins, "apl", 1)

        for u in range(18):
            if u < 16:
                P1(u)
                E1(u)
                E2(u)
            if 1 <= u <= 16:
                P2(u - 1)
                E3(u - 1)
                A1(u - 1)
                E4(u - 1)
            if u >= 2:
                P3(u - 2)
                E5(u - 2)

        b.wait(nc.tensor, "apl")
        for tt in range(4):
            pv = ps_y if tt % 2 == 0 else ps_b
            if tt >= 2:
                b.wait_at(nc.tensor, "vev", b.cnt["vev"] - 1)
            for half in range(2):
                for c in range(NCHUNK):
                    ins = nc.tensor.matmul(pv[:, half, :], xn[:, c, tt * 128 : (tt + 1) * 128],
                                           wsb[:, 2, c, half * SL : (half + 1) * SL],
                                           start=(c == 0), stop=(c == 7), skip_group_check=True)
            b.inc(ins, "mm", 1)
            b.wait(nc.vector, "mm")
            nc.vector.tensor_tensor(vsl[:, tt, 0:SL], pv[:, 0, :], bvb[:, 0:SL], ALU.add)
            ins = nc.vector.tensor_tensor(vsl[:, tt, SL:D], pv[:, 1, :], bvb[:, SL:D], ALU.add)
            b.inc(ins, "vev", 1)
        b.drain_inc(nc.vector, "vdr")
        b.wait(nc.gpsimd, "vdr")
        ins = nc.gpsimd.dma_start(qT[:, :, t0 : t0 + SL].rearrange("c p t -> p c t"), qn2[:, 0, :, :])
        b.inc(ins, "qst", 16)
        ins = nc.gpsimd.dma_start(kT[:, :, t0 : t0 + SL].rearrange("c p t -> p c t"), qn2[:, 1, :, :])
        b.inc(ins, "qst", 16)
        ins = nc.gpsimd.dma_start(vtok[t0 : t0 + SL, :].rearrange("(q p) d -> p q d", p=128), vsl[:])
        b.inc(ins, "qst", 16)
        b.wait(nc.gpsimd, "qst")
        b.barrier()
    b.restore(m0)


def emit_attn(b, xs, qT, kT, vtok, woT, bo8, gam8, ibt, tag=""):
    nc = b.nc
    m0 = b.mark()
    wsb = b.sb(f"p2w{tag}", [128, 8, 1024], BF16)
    wf = b.sb(f"p2wf{tag}", [128, 8, 1024], F32)
    kTs = b.sb(f"p2k{tag}", [128, 8, T], BF16)
    qTs = b.sb(f"p2q{tag}", [128, 8, SL], BF16)
    vts = b.sb(f"p2v{tag}", [128, 2, 1024], BF16)
    ssb = b.sb(f"p2s{tag}", [128, 2, SL], BF16)
    att = b.sb(f"p2a{tag}", [128, 8, SL], BF16)
    xsl = b.sb(f"p2x{tag}", [128, 8, SL], F32)
    tmp = b.sb(f"p2t{tag}", [128, SL], F32)
    bo = b.sb(f"p2bo{tag}", [128, 8], F32)
    gam = b.sb(f"p2g{tag}", [128, 8], F32)
    ibb = b.sb(f"p2ib{tag}", [128, 1], F32)
    ps_s = b.ps(f"p2pss{tag}", [128, 2, SL], F32)
    ps_o = b.ps(f"p2pso{tag}", [128, 4, SL], F32)
    ps_y = b.ps(f"p2psy{tag}", [128, 2, SL], F32)

    ins = nc.sync.dma_start(wf[:], woT.ap().rearrange("(c p) m -> p c m", p=128))
    b.inc(ins, "wld", 16)
    b.wait(nc.vector, "wld")
    ins = nc.vector.tensor_copy(wsb[:], wf[:])
    b.inc(ins, "wfree", 1)
    ins = nc.sync.dma_start(kTs[:], kT.ap().rearrange("c p t -> p c t"))
    b.inc(ins, "wld", 16)
    for src, dst in ((bo8, bo), (gam8, gam), (ibt, ibb)):
        ins = nc.sync.dma_start(dst[:], src.ap())
        b.inc(ins, "wld", 16)
    b.barrier()

    for s in range(NSL):
        t0 = s * SL
        ins = nc.sync.dma_start(qTs[:], qT[:, :, t0 : t0 + SL].rearrange("c p t -> p c t"))
        b.inc(ins, "xld", 16)
        ins = nc.sync.dma_start(xsl[:], xs[:, :, t0 : t0 + SL].rearrange("c p t -> p c t"))
        b.inc(ins, "xld", 16)
        b.wait(nc.tensor, "xld")
        for hp in range(2):
            if b.cnt.get("aev", 0) > 0:
                b.wait(nc.tensor, "aev")
            vld_at = {}
            for j in range(T // 128):
                par = j % 2
                if j == 0:
                    b.wait(nc.sync, "ov")
                    ins = nc.sync.dma_start(vts[:, 0, :], vtok[0:128, :])
                    b.inc(ins, "vld", 16)
                    vld_at[0] = b.cnt["vld"]
                if j + 1 < T // 128:
                    if j >= 1:
                        b.wait(nc.sync, "ov")
                    ins = nc.sync.dma_start(vts[:, (j + 1) % 2, :], vtok[(j + 1) * 128 : (j + 2) * 128, :])
                    b.inc(ins, "vld", 16)
                    vld_at[j + 1] = b.cnt["vld"]
                b.wait_at(nc.tensor, "vld", vld_at[j])
                sg0 = b.cnt.get("sg", 0)
                ov0 = b.cnt.get("ov", 0)
                for h8 in range(8):
                    h = hp * 8 + h8
                    hc, hr = h // 2, (h % 2) * 64
                    spar = h8 % 2
                    if sg0 + h8 - 1 > 0:
                        b.wait_at(nc.tensor, "sg", sg0 + h8 - 1)
                    ins = nc.tensor.matmul(ps_s[:, spar, :],
                                           kTs[hr : hr + 64, hc, j * 128 : (j + 1) * 128],
                                           qTs[hr : hr + 64, hc, :], start=True, stop=True,
                                           skip_group_check=True)
                    b.inc(ins, "sm", 1)
                    b.wait(nc.scalar, "sm")
                    if ov0 + h8 - 1 > 0:
                        b.wait_at(nc.scalar, "ov", ov0 + h8 - 1)
                    ins = nc.scalar.activation(ssb[:, spar, :], ps_s[:, spar, :], AF.Sigmoid,
                                               bias=ibb[:, 0:1])
                    b.inc(ins, "sg", 1)
                    b.wait(nc.tensor, "sg")
                    ins = nc.tensor.matmul(ps_o[(h8 % 2) * 64 : (h8 % 2) * 64 + 64, h8 // 2, :],
                                           vts[:, par, h * 64 : (h + 1) * 64],
                                           ssb[:, spar, :], start=(j == 0), stop=(j == T // 128 - 1),
                                           skip_group_check=True)
                    b.inc(ins, "ov", 1)
            b.wait(nc.vector, "ov")
            for h8 in range(8):
                h = hp * 8 + h8
                hc, hr = h // 2, (h % 2) * 64
                ins = nc.vector.tensor_copy(att[hr : hr + 64, hc, :],
                                            ps_o[(h8 % 2) * 64 : (h8 % 2) * 64 + 64, h8 // 2, :])
                b.inc(ins, "aev", 1)
        b.wait(nc.tensor, "aev")
        for ot in range(NCHUNK):
            par = ot % 2
            if b.cnt.get("pev", 0) >= 2:
                b.wait_at(nc.tensor, "pev", b.cnt["pev"] - 2)
            for c in range(NCHUNK):
                ins = nc.tensor.matmul(ps_y[:, par, :], wsb[:, c, ot * 128 : (ot + 1) * 128],
                                       att[:, c, :], start=(c == 0), stop=(c == 7))
            b.inc(ins, "mm", 1)
            b.wait(nc.vector, "mm")
            nc.vector.tensor_scalar(tmp[:], ps_y[:, par, :], bo[:, ot : ot + 1],
                                    gam[:, ot : ot + 1], ALU.add, ALU.mult)
            ins = nc.vector.tensor_tensor(xsl[:, ot, :], tmp[:], xsl[:, ot, :], ALU.add)
            b.inc(ins, "pev", 2)
        b.drain_inc(nc.vector, "vdr")
        b.wait(nc.gpsimd, "vdr")
        ins = nc.gpsimd.dma_start(xs[:, :, t0 : t0 + SL].rearrange("c p t -> p c t"), xsl[:])
        b.inc(ins, "xst", 16)
        b.wait(nc.gpsimd, "xst")
        b.barrier()
    b.restore(m0)


def build_p2():
    nc = bass.Bass("TRN2", target_bir_lowering=False, debug=False)
    xs = nc.dram_tensor("xs", [NCHUNK, 128, T], F32, kind="ExternalInput")
    xso = nc.dram_tensor("xso", [NCHUNK, 128, T], F32, kind="ExternalOutput")
    qT = nc.dram_tensor("qT", [NCHUNK, 128, T], BF16, kind="ExternalInput")
    kT = nc.dram_tensor("kT", [NCHUNK, 128, T], BF16, kind="ExternalInput")
    vtok = nc.dram_tensor("vtok", [T, D], BF16, kind="ExternalInput")
    woT = nc.dram_tensor("woT", [D, D], F32, kind="ExternalInput")
    bo8 = nc.dram_tensor("bo8", [128, NCHUNK], F32, kind="ExternalInput")
    gam8 = nc.dram_tensor("gam8", [128, NCHUNK], F32, kind="ExternalInput")
    ibt = nc.dram_tensor("ibt", [128, 1], F32, kind="ExternalInput")
    b = Bld(nc)
    b.setup_consts([EPS_LN, EPS_HLN])
    emit_dram_copy(b, xso, xs)
    emit_attn(b, xso, qT, kT, vtok, woT, bo8, gam8, ibt)
    return nc


def _p2_inputs(inputs, x_stage, qn, kn, v):
    def tile8(vv):
        return np.ascontiguousarray(vv.reshape(NCHUNK, 128).T)
    common = {
        "woT": np.ascontiguousarray(inputs["L0_attn_Wo"]),
        "bo8": tile8(inputs["L0_attn_bo"]),
        "gam8": tile8(inputs["L0_attn_gamma"]),
        "ibt": np.ascontiguousarray(np.tile(inputs["L0_attn_init_bias"].reshape(1, 1), (128, 1)).astype(np.float32)),
    }
    maps = []
    for bc in range(8):
        d = dict(common)
        d["xs"] = np.ascontiguousarray(x_stage[bc].T).reshape(NCHUNK, 128, T)
        d["qT"] = np.ascontiguousarray(qn[bc].T.astype(BF)).reshape(NCHUNK, 128, T)
        d["kT"] = np.ascontiguousarray(kn[bc].T.astype(BF)).reshape(NCHUNK, 128, T)
        d["vtok"] = np.ascontiguousarray(v[bc].astype(BF))
        maps.append(d)
    return maps



def _ln_to_xn(b, xsl, xn, xb, sqb, rows, onesb, ones1, ps_s, ps_q, ps_b, tmpc):
    nc = b.nc
    MU, E2, VAR, RSTD, MR = 0, 1, 2, 3, 4
    nc.vector.tensor_copy(xb[:], xsl[:])
    ins = nc.vector.tensor_tensor(sqb[:], xb[:], xb[:], ALU.mult)
    b.inc(ins, "dv", 1)
    b.wait(nc.tensor, "dv")
    for c in range(NCHUNK):
        nc.tensor.matmul(ps_s[:], onesb[:], xb[:, c, :], start=(c == 0), stop=(c == 7))
    for c in range(NCHUNK):
        ins = nc.tensor.matmul(ps_q[:], onesb[:], sqb[:, c, :], start=(c == 0), stop=(c == 7))
    b.inc(ins, "mm", 1)
    b.wait(nc.vector, "mm")
    nc.vector.tensor_scalar_mul(rows[:, MU, :], ps_s[:], 1.0 / D)
    nc.vector.tensor_scalar_mul(rows[:, E2, :], ps_q[:], 1.0 / D)
    nc.vector.tensor_tensor(rows[:, VAR, :], rows[:, MU, :], rows[:, MU, :], ALU.mult)
    ins = nc.vector.tensor_tensor(rows[:, VAR, :], rows[:, E2, :], rows[:, VAR, :], ALU.subtract)
    b.inc(ins, "dv", 1)
    b.wait(nc.scalar, "dv")
    ins = nc.scalar.activation(rows[:, RSTD, :], rows[:, VAR, :], AF.Sqrt, bias=EPS_LN)
    b.inc(ins, "ac", 1)
    b.wait(nc.vector, "ac")
    nc.vector.reciprocal(rows[:, RSTD, :], rows[:, RSTD, :])
    ins = nc.vector.tensor_tensor(rows[:, MR, :], rows[:, MU, :], rows[:, RSTD, :], ALU.mult)
    b.inc(ins, "dv", 1)
    b.wait(nc.tensor, "dv")
    nc.tensor.matmul(ps_b[:, 0, :], ones1[:], rows[:, RSTD, :], start=True, stop=True)
    ins = nc.tensor.matmul(ps_b[:, 1, :], ones1[:], rows[:, MR, :], start=True, stop=True)
    b.inc(ins, "mm", 1)
    b.wait(nc.vector, "mm")
    for c in range(NCHUNK):
        nc.vector.tensor_tensor(tmpc[:], xsl[:, c, :], ps_b[:, 0, :], ALU.mult)
        ins = nc.vector.tensor_tensor(xn[:, c, :], tmpc[:], ps_b[:, 1, :], ALU.subtract)
    b.inc(ins, "dv", 1)


def emit_wcast(b, pairs, tag=""):
    nc = b.nc
    m0 = b.mark()
    wf = b.sb(f"wcf{tag}", [128, 2, 4096], F32)
    wb = b.sb(f"wcb{tag}", [128, 2, 4096], BF16)
    for src, dst, Kd, Md in pairs:
        for kc in range(Kd // 128):
            par = kc % 2
            if b.cnt.get("wcv", 0) >= 2:
                b.wait_at(nc.scalar, "wcv", b.cnt["wcv"] - 1)
            ins = nc.scalar.dma_start(wf[:, par, 0:Md], src[kc * 128 : (kc + 1) * 128, :])
            b.inc(ins, "wcl", 16)
            b.wait(nc.vector, "wcl")
            if b.cnt.get("wcs", 0) >= 2:
                b.wait_at(nc.vector, "wcs", b.cnt["wcs"] - 16)
            ins = nc.vector.tensor_copy(wb[:, par, 0:Md], wf[:, par, 0:Md])
            b.inc(ins, "wcv", 1)
            b.drain_inc(nc.vector, "wcd")
            b.wait(nc.gpsimd, "wcd")
            ins = nc.gpsimd.dma_start(
                dst[:, :, kc * 128 : (kc + 1) * 128].rearrange("o p m -> p o m"),
                wb[:, par, 0:Md].rearrange("p (o m) -> p o m", m=128))
            b.inc(ins, "wcs", 16)
    b.restore(m0)


def emit_cast_flat(b, pairs, tag=""):
    nc = b.nc
    m0 = b.mark()
    wf = b.sb(f"cff{tag}", [128, 2, 4096], F32)
    wb = b.sb(f"cfb{tag}", [128, 2, 4096], BF16)
    for src, dst, Kd, Md in pairs:
        for kc in range(Kd // 128):
            par = kc % 2
            if b.cnt.get("wcv", 0) >= 2:
                b.wait_at(nc.scalar, "wcv", b.cnt["wcv"] - 1)
            ins = nc.scalar.dma_start(wf[:, par, 0:Md], src[kc * 128 : (kc + 1) * 128, :])
            b.inc(ins, "wcl", 16)
            b.wait(nc.vector, "wcl")
            if b.cnt.get("wcs", 0) >= 2:
                b.wait_at(nc.vector, "wcs", b.cnt["wcs"] - 16)
            ins = nc.vector.tensor_copy(wb[:, par, 0:Md], wf[:, par, 0:Md])
            b.inc(ins, "wcv", 1)
            b.drain_inc(nc.vector, "wcd")
            b.wait(nc.gpsimd, "wcd")
            ins = nc.gpsimd.dma_start(dst[kc * 128 : (kc + 1) * 128, :], wb[:, par, 0:Md])
            b.inc(ins, "wcs", 16)
    b.restore(m0)


def emit_casts(b, bufs, flat_pairs, blk_pairs, tag=""):
    nc = b.nc
    wfA, wbA, wfB, wbB = bufs
    MT = 1024
    jobs = []
    for src, dst, Kd, Md in flat_pairs:
        for kc in range(Kd // 128):
            for mc in range(Md // MT):
                jobs.append((src, dst, kc, mc, False))
    for src, dst, Kd, Md in blk_pairs:
        for kc in range(Kd // 128):
            for mc in range(Md // MT):
                jobs.append((src, dst, kc, mc, True))
    for gi, (src, dst, kc, mc, blk) in enumerate(jobs):
        lane = 0
        i = b.cnt.get("wcv", 0) if lane == 0 else b.cnt.get("w2l", 0) // 16
        par = i % 2
        sl = src[kc * 128 : (kc + 1) * 128, mc * MT : (mc + 1) * MT]
        if blk:
            dsl = dst[mc * (MT // 128) : (mc + 1) * (MT // 128), :,
                      kc * 128 : (kc + 1) * 128].rearrange("o p m -> p o m")
            wv = None
        else:
            dsl = dst[kc * 128 : (kc + 1) * 128, mc * MT : (mc + 1) * MT]
        if lane == 0:
            if i >= 2:
                b.wait_at(nc.scalar, "wcv", b.cnt["wcv"] - 1)
            ins = nc.scalar.dma_start(wfA[:, par, :], sl)
            b.inc(ins, "wcl", 16)
            b.wait(nc.vector, "wcl")
            if i >= 2:
                b.wait_at(nc.vector, "wcs", b.cnt["wcs"] - 16)
            ins = nc.vector.tensor_copy(wbA[:, par, :], wfA[:, par, :])
            b.inc(ins, "wcv", 1)
            b.drain_inc(nc.vector, "wcd")
            b.wait(nc.gpsimd, "wcd")
            ins = nc.gpsimd.dma_start(dsl, wbA[:, par, :].rearrange("p (o m) -> p o m", m=128)
                                      if blk else wbA[:, par, :])
            b.inc(ins, "wcs", 16)
        else:
            if i >= 2:
                b.wait_at(nc.gpsimd, "w2s", b.cnt["w2s"] - 16)
            ins = nc.gpsimd.dma_start(wfB[:, par, :], sl)
            b.inc(ins, "w2l", 16)
            b.wait(nc.gpsimd, "w2l")
            ins = nc.gpsimd.tensor_copy(wbB[:, par, :], wfB[:, par, :])
            b.drain_inc(nc.gpsimd, "w2d")
            ins = nc.gpsimd.dma_start(dsl, wbB[:, par, :].rearrange("p (o m) -> p o m", m=128)
                                      if blk else wbB[:, par, :])
            b.inc(ins, "w2s", 16)


def emit_attn2(b, xs, qT, kT, vtok, woR, bo8, gam8, ibt, tag=""):
    nc = b.nc
    m0 = b.mark()
    NJ = T // 128
    kTs = b.sb(f"a2k{tag}", [128, 8, T], BF16)
    vres = b.sb(f"a2v{tag}", [128, NJ, 1024], BF16)
    wsb = b.sb(f"a2w{tag}", [128, 8, 1024], BF16)
    qTs = b.sb(f"a2q{tag}", [128, 8, SL], BF16)
    ssb = b.sb(f"a2s{tag}", [128, 2, 1024], BF16)
    att = b.sb(f"a2a{tag}", [128, 8, SL], BF16)
    xsl = b.sb(f"a2x{tag}", [128, 8, SL], F32)
    tmp = b.sb(f"a2t{tag}", [128, SL], F32)
    bo = b.sb(f"a2bo{tag}", [128, 8], F32)
    gam = b.sb(f"a2g{tag}", [128, 8], F32)
    ibb = b.sb(f"a2ib{tag}", [128, 1], F32)
    ps_s = b.ps(f"a2pss{tag}", [128, 4, SL], F32)
    ps_o = b.ps(f"a2pso{tag}", [128, 2, SL], F32)
    ps_y = b.ps(f"a2psy{tag}", [128, 2, SL], F32)

    if getattr(b, "wcs_flat", 0) > 0:
        b.wait_at(nc.sync, "wcs", b.wcs_flat)
    if getattr(b, "w2s_flat", 0) > 0:
        b.wait_at(nc.sync, "w2s", b.w2s_flat)
    ins = nc.sync.dma_start(kTs[:], kT.ap().rearrange("c p t -> p c t"))
    b.inc(ins, "wld", 16)
    for j0 in range(0, NJ, 8):
        ins = nc.sync.dma_start(
            vres[:, j0 : j0 + 8, :],
            vtok[j0 * 128 : (j0 + 8) * 128, :].rearrange("(j p) d -> p j d", p=128))
        b.inc(ins, "wld", 16)
    ins = nc.sync.dma_start(wsb[:], woR.ap().rearrange("(c p) m -> p c m", p=128))
    b.inc(ins, "wld", 16)
    for src, dst in ((bo8, bo), (gam8, gam), (ibt, ibb)):
        ins = nc.sync.dma_start(dst[:], src.ap())
        b.inc(ins, "wld", 16)
    b.barrier()

    U = 8 * NJ
    for s in range(NSL):
        t0 = s * SL
        ins = nc.sync.dma_start(qTs[:], qT[:, :, t0 : t0 + SL].rearrange("c p t -> p c t"))
        b.inc(ins, "xld", 16)
        ins = nc.sync.dma_start(xsl[:], xs[:, :, t0 : t0 + SL].rearrange("c p t -> p c t"))
        b.inc(ins, "xld", 16)
        amm0 = b.cnt.get("amm", 0)
        asg0 = b.cnt.get("asg", 0)
        aov0 = b.cnt.get("aov", 0)
        aevc0 = b.cnt.get("aevc", 0)
        b.wait(nc.tensor, "xld")

        def logits(u):
            p, j = u // NJ, u % NJ
            sp = u % 2
            if u >= 2:
                b.wait_at(nc.tensor, "asg", asg0 + u - 1)
            nc.tensor.matmul(ps_s[:, 2 * sp, :], kTs[0:64, p, j * 128 : (j + 1) * 128],
                             qTs[0:64, p, :], start=True, stop=True, skip_group_check=True)
            ins = nc.tensor.matmul(ps_s[:, 2 * sp + 1, :], kTs[64:128, p, j * 128 : (j + 1) * 128],
                                   qTs[64:128, p, :], start=True, stop=True, skip_group_check=True)
            b.inc(ins, "amm", 1)

        def attv(u):
            p, j = u // NJ, u % NJ
            sp = u % 2
            po = p % 2
            b.wait_at(nc.tensor, "asg", asg0 + u + 1)
            if j == 0 and p >= 2:
                b.wait_at(nc.tensor, "aevc", aevc0 + p - 1)
            nc.tensor.matmul(ps_o[0:64, po, :], vres[:, j, p * 128 : p * 128 + 64],
                             ssb[:, sp, 0:SL], start=(j == 0), stop=(j == NJ - 1),
                             skip_group_check=True)
            ins = nc.tensor.matmul(ps_o[64:128, po, :], vres[:, j, p * 128 + 64 : p * 128 + 128],
                                   ssb[:, sp, SL : 2 * SL], start=(j == 0), stop=(j == NJ - 1),
                                   skip_group_check=True)
            b.inc(ins, "aov", 1)
            if j == NJ - 1:
                b.wait(nc.vector, "aov")
                ins = nc.vector.tensor_copy(att[:, p, :], ps_o[:, po, :])
                b.inc(ins, "aevc", 1)

        def sig(u):
            sp = u % 2
            b.wait_at(nc.scalar, "amm", amm0 + u + 1)
            if u >= 2:
                b.wait_at(nc.scalar, "aov", aov0 + u - 1)
            ins = nc.scalar.activation(ssb[:, sp, :],
                                       ps_s[:, 2 * sp : 2 * sp + 2, :].rearrange("p a t -> p (a t)"),
                                       AF.Sigmoid, bias=ibb[:, 0:1])
            b.inc(ins, "asg", 1)

        logits(0)
        sig(0)
        for u in range(1, U):
            logits(u)
            sig(u)
            attv(u - 1)
        attv(U - 1)

        b.wait(nc.tensor, "aevc")
        for ot in range(NCHUNK):
            par = ot % 2
            if b.cnt.get("pev", 0) >= 2:
                b.wait_at(nc.tensor, "pev", b.cnt["pev"] - 2)
            for c in range(NCHUNK):
                ins = nc.tensor.matmul(ps_y[:, par, :], wsb[:, c, ot * 128 : (ot + 1) * 128],
                                       att[:, c, :], start=(c == 0), stop=(c == 7))
            b.inc(ins, "mm", 1)
            b.wait(nc.vector, "mm")
            nc.vector.tensor_scalar(tmp[:], ps_y[:, par, :], bo[:, ot : ot + 1],
                                    gam[:, ot : ot + 1], ALU.add, ALU.mult)
            ins = nc.vector.tensor_tensor(xsl[:, ot, :], tmp[:], xsl[:, ot, :], ALU.add)
            b.inc(ins, "pev", 2)
        b.drain_inc(nc.vector, "vdr")
        b.wait(nc.gpsimd, "vdr")
        ins = nc.gpsimd.dma_start(xs[:, :, t0 : t0 + SL].rearrange("c p t -> p c t"), xsl[:])
        b.inc(ins, "xst", 16)
        b.wait(nc.gpsimd, "xst")
        b.barrier()
    b.restore(m0)


def emit_l1ffn(b, xs, w1R, w2R, h1R, h2R, w3R, scl, tag=""):
    nc = b.nc
    m0 = b.mark()
    NH = DHID // 128
    xsl = b.sb(f"p4x{tag}", [128, 2, 8, SL], F32)
    xb = b.sb(f"p4xb{tag}", [128, 8, SL], BF16)
    sqb = b.sb(f"p4sq{tag}", [128, 8, SL], BF16)
    xn = b.sb(f"p4xn{tag}", [128, 2, 8, SL], BF16)
    hb = b.sb(f"p4h{tag}", [128, NH, SL], BF16)
    hb2 = b.sb(f"p4h2{tag}", [128, NH, SL], BF16)
    sil = b.sb(f"p4sil{tag}", [128, 2, SL], F32)
    tmp = b.sb(f"p4t{tag}", [128, 2, SL], F32)
    rows = b.sb(f"p4rows{tag}", [1, 8, SL], F32)
    onesb = b.sb(f"p4ones{tag}", [128, 1], BF16)
    ones1 = b.sb(f"p4one1{tag}", [1, 128], F32)
    sc = b.sb(f"p4sc{tag}", [128, 5, NH], F32)
    w12 = b.sb(f"p4w12{tag}", [128, 2, 2, 8, 128], BF16)
    wh = b.sb(f"p4wh{tag}", [128, 2, 32, 128], BF16)
    ps_s = b.ps(f"p4pss{tag}", [1, SL], F32)
    ps_q = b.ps(f"p4psq{tag}", [1, SL], F32)
    ps_b = b.ps(f"p4psb{tag}", [128, 2, SL], F32)
    ps_g = b.ps(f"p4psg{tag}", [128, 2, SL], F32)
    ps_h = b.ps(f"p4psh{tag}", [128, 2, SL], F32)

    if b.cnt.get("wcs", 0) > 0:
        b.wait(nc.sync, "wcs")
    if b.cnt.get("w2s", 0) > 0:
        b.wait(nc.sync, "w2s")
    ins = nc.sync.dma_start(sc[:], scl.ap())
    b.inc(ins, "wld", 16)
    b.wait(nc.vector, "wld")
    nc.vector.memset(onesb[:], 1.0)
    nc.vector.memset(ones1[:], 1.0)
    b.barrier()

    xst0 = b.cnt.get("xst", 0)
    hevA_prev = None
    for s in range(NSL):
        t0 = s * SL
        sp = s % 2
        xsl_s = xsl[:, sp]
        xn_s = xn[:, sp]
        if s >= 2:
            b.wait_at(nc.sync, "xst", xst0 + 16 * (s - 1))
        ins = nc.sync.dma_start(xsl_s[:], xs[:, :, t0 : t0 + SL].rearrange("c p t -> p c t"))
        b.inc(ins, "xld", 16)
        b.wait(nc.vector, "xld")
        _ln_to_xn(b, xsl_s, xn_s, xb, sqb, rows, onesb, ones1, ps_s, ps_q, ps_b, tmp[:, 0, :])
        b.wait(nc.tensor, "dv")
        wat = {}
        for ot in range(NH):
            par = ot % 2
            if ot == 0:
                if hevA_prev is not None:
                    b.wait_at(nc.sync, "hev", hevA_prev)
                ins = nc.sync.dma_start(w12[:, 0, 0, :, :], w1R[0, :, :])
                b.inc(ins, "wld", 16)
                ins = nc.sync.dma_start(w12[:, 0, 1, :, :], w2R[0, :, :])
                b.inc(ins, "wld", 16)
                wat[0] = b.cnt["wld"]
            if ot + 1 < NH:
                if ot >= 1:
                    b.wait_at(nc.sync, "hev", b.cnt["hev"])
                ins = nc.sync.dma_start(w12[:, (ot + 1) % 2, 0, :, :], w1R[ot + 1, :, :])
                b.inc(ins, "wld", 16)
                ins = nc.sync.dma_start(w12[:, (ot + 1) % 2, 1, :, :], w2R[ot + 1, :, :])
                b.inc(ins, "wld", 16)
                wat[ot + 1] = b.cnt["wld"]
            b.wait_at(nc.tensor, "wld", wat[ot])
            if b.cnt.get("hev", 0) >= 2:
                b.wait_at(nc.tensor, "hev", b.cnt["hev"] - 1)
            for c in range(NCHUNK):
                nc.tensor.matmul(ps_g[:, par, :], w12[:, par, 0, c, :], xn_s[:, c, :],
                                 start=(c == 0), stop=(c == 7), skip_group_check=True)
            for c in range(NCHUNK):
                ins = nc.tensor.matmul(ps_h[:, par, :], w12[:, par, 1, c, :], xn_s[:, c, :],
                                       start=(c == 0), stop=(c == 7), skip_group_check=True)
            b.inc(ins, "mm", 1)
            b.wait(nc.scalar, "mm")
            ins = nc.scalar.activation(sil[:, par, :], ps_g[:, par, :], AF.Silu,
                                       scale=sc[:, 0, ot : ot + 1])
            b.inc(ins, "ac", 1)
            b.wait(nc.vector, "ac")
            ins = nc.vector.scalar_tensor_tensor(hb[:, ot, :], ps_h[:, par, :],
                                                 sc[:, 1, ot : ot + 1], sil[:, par, :],
                                                 ALU.mult, ALU.mult)
            b.inc(ins, "hev", 1)
        hevA_prev = b.cnt["hev"]
        for stage, (wR, hsrc, hdst, sci) in enumerate(
                ((h1R, hb, hb2, 2), (h2R, hb2, hb, 3))):
            b.wait(nc.tensor, "hev")
            wat = {}
            for ot in range(NH):
                par = ot % 2
                if ot == 0:
                    if stage == 0:
                        if b.cnt.get("dev", 0) > 0:
                            b.wait(nc.sync, "dev")
                    else:
                        b.wait(nc.sync, "hev")
                    ins = nc.sync.dma_start(wh[:, 0, :, :], wR[0, :, :])
                    b.inc(ins, "wld", 16)
                    wat[0] = b.cnt["wld"]
                if ot + 1 < NH:
                    if ot >= 1:
                        b.wait_at(nc.sync, "hev", b.cnt["hev"])
                    ins = nc.sync.dma_start(wh[:, (ot + 1) % 2, :, :], wR[ot + 1, :, :])
                    b.inc(ins, "wld", 16)
                    wat[ot + 1] = b.cnt["wld"]
                b.wait_at(nc.tensor, "wld", wat[ot])
                if b.cnt.get("hev", 0) >= 2:
                    b.wait_at(nc.tensor, "hev", b.cnt["hev"] - 1)
                for kc in range(NH):
                    ins = nc.tensor.matmul(ps_g[:, par, :], wh[:, par, kc, :], hsrc[:, kc, :],
                                           start=(kc == 0), stop=(kc == NH - 1),
                                           skip_group_check=True)
                b.inc(ins, "mm", 1)
                b.wait(nc.scalar, "mm")
                ins = nc.scalar.activation(hdst[:, ot, :], ps_g[:, par, :], AF.Gelu,
                                           scale=sc[:, sci, ot : ot + 1])
                b.inc(ins, "hev", 1)
        b.wait(nc.tensor, "hev")
        wat = {}
        for ot in range(NCHUNK):
            par = ot % 2
            if ot == 0:
                b.wait(nc.sync, "dev")
                b.wait(nc.sync, "hev")
                ins = nc.sync.dma_start(wh[:, 0, :, :], w3R[0, :, :])
                b.inc(ins, "wld", 16)
                wat[0] = b.cnt["wld"]
            if ot + 1 < NCHUNK:
                if ot >= 1:
                    b.wait_at(nc.sync, "dev", b.cnt["dev"])
                ins = nc.sync.dma_start(wh[:, (ot + 1) % 2, :, :], w3R[ot + 1, :, :])
                b.inc(ins, "wld", 16)
                wat[ot + 1] = b.cnt["wld"]
            b.wait_at(nc.tensor, "wld", wat[ot])
            if b.cnt.get("dev", 0) >= 2:
                b.wait_at(nc.tensor, "dev", b.cnt["dev"] - 1)
            for kc in range(NH):
                ins = nc.tensor.matmul(ps_g[:, par, :], wh[:, par, kc, :], hb[:, kc, :],
                                       start=(kc == 0), stop=(kc == NH - 1),
                                       skip_group_check=True)
            b.inc(ins, "mm", 1)
            b.wait(nc.vector, "mm")
            nc.vector.tensor_scalar(tmp[:, 1, :], ps_g[:, par, :], sc[:, 4, ot : ot + 1],
                                    0.5, ALU.mult, ALU.mult)
            ins = nc.vector.tensor_tensor(xsl_s[:, ot, :], tmp[:, 1, :], xsl_s[:, ot, :], ALU.add)
            b.inc(ins, "dev", 1)
        b.drain_inc(nc.vector, "vdr")
        b.wait(nc.gpsimd, "vdr")
        ins = nc.gpsimd.dma_start(xs[:, :, t0 : t0 + SL].rearrange("c p t -> p c t"), xsl_s[:])
        b.inc(ins, "xst", 16)
    b.wait(nc.gpsimd, "xst")
    b.barrier()
    b.restore(m0)


def build_wprep():
    nc = bass.Bass("TRN2", target_bir_lowering=False, debug=False)
    w1T = nc.dram_tensor("w1T", [D, DHID], F32, kind="ExternalInput")
    w2T = nc.dram_tensor("w2T", [D, DHID], F32, kind="ExternalInput")
    h1T = nc.dram_tensor("h1T", [DHID, DHID], F32, kind="ExternalInput")
    h2T = nc.dram_tensor("h2T", [DHID, DHID], F32, kind="ExternalInput")
    w3T = nc.dram_tensor("w3T", [DHID, D], F32, kind="ExternalInput")
    w1R = nc.dram_tensor("w1R", [32, 128, D], BF16, kind="ExternalOutput")
    w2R = nc.dram_tensor("w2R", [32, 128, D], BF16, kind="ExternalOutput")
    h1R = nc.dram_tensor("h1R", [32, 128, DHID], BF16, kind="ExternalOutput")
    h2R = nc.dram_tensor("h2R", [32, 128, DHID], BF16, kind="ExternalOutput")
    w3R = nc.dram_tensor("w3R", [8, 128, DHID], BF16, kind="ExternalOutput")
    b = Bld(nc)
    emit_wcast(b, [(w1T, w1R, D, DHID), (w2T, w2R, D, DHID), (h1T, h1R, DHID, DHID),
                   (h2T, h2R, DHID, DHID), (w3T, w3R, DHID, D)])
    return nc


def build_p4():
    nc = bass.Bass("TRN2", target_bir_lowering=False, debug=False)
    xs = nc.dram_tensor("xs", [NCHUNK, 128, T], F32, kind="ExternalInput")
    xso = nc.dram_tensor("xso", [NCHUNK, 128, T], F32, kind="ExternalOutput")
    w1R = nc.dram_tensor("w1R", [32, 128, D], BF16, kind="ExternalInput")
    w2R = nc.dram_tensor("w2R", [32, 128, D], BF16, kind="ExternalInput")
    h1R = nc.dram_tensor("h1R", [32, 128, DHID], BF16, kind="ExternalInput")
    h2R = nc.dram_tensor("h2R", [32, 128, DHID], BF16, kind="ExternalInput")
    w3R = nc.dram_tensor("w3R", [8, 128, DHID], BF16, kind="ExternalInput")
    scl = nc.dram_tensor("scl", [128, 5, 32], F32, kind="ExternalInput")
    b = Bld(nc)
    b.setup_consts([EPS_LN, EPS_HLN])
    emit_dram_copy(b, xso, xs)
    emit_l1ffn(b, xso, w1R, w2R, h1R, h2R, w3R, scl)
    return nc


def _wprep_inputs(inputs):
    return {
        "w1T": np.ascontiguousarray(inputs["L1_ffn_w1_w"].T),
        "w2T": np.ascontiguousarray(inputs["L1_ffn_w2_w"].T),
        "h1T": np.ascontiguousarray(inputs["L1_ffn_h1_w"].T),
        "h2T": np.ascontiguousarray(inputs["L1_ffn_h2_w"].T),
        "w3T": np.ascontiguousarray(inputs["L1_ffn_w3_w"].T),
    }


def _p4_scl(inputs):
    scl = np.zeros((128, 5, 32), np.float32)
    scl[:, 0, :] = inputs["L1_ffn_w1_s"].reshape(32, 128).T
    scl[:, 1, :] = inputs["L1_ffn_w2_s"].reshape(32, 128).T
    scl[:, 2, :] = inputs["L1_ffn_h1_s"].reshape(32, 128).T
    scl[:, 3, :] = inputs["L1_ffn_h2_s"].reshape(32, 128).T
    scl[:, 4, 0:8] = inputs["L1_ffn_w3_s"].reshape(8, 128).T
    return np.ascontiguousarray(scl)



def emit_bu(b, xs, bT, BuT, tag=""):
    nc = b.nc
    m0 = b.mark()
    wsb = b.sb(f"p5w{tag}", [128, 8, 1024], BF16)
    wf = b.sb(f"p5wf{tag}", [128, 8, 1024], F32)
    xsl = b.sb(f"p5x{tag}", [128, 8, SL], F32)
    xb = b.sb(f"p5xb{tag}", [128, 8, SL], BF16)
    sqb = b.sb(f"p5sq{tag}", [128, 8, SL], BF16)
    xn = b.sb(f"p5xn{tag}", [128, 8, SL], BF16)
    bub = b.sb(f"p5bu{tag}", [128, 8, SL], BF16)
    tmp = b.sb(f"p5t{tag}", [128, SL], F32)
    rows = b.sb(f"p5rows{tag}", [1, 8, SL], F32)
    onesb = b.sb(f"p5ones{tag}", [128, 1], BF16)
    ones1 = b.sb(f"p5one1{tag}", [1, 128], F32)
    ps_s = b.ps(f"p5pss{tag}", [1, SL], F32)
    ps_q = b.ps(f"p5psq{tag}", [1, SL], F32)
    ps_b = b.ps(f"p5psb{tag}", [128, 2, SL], F32)
    ps_y = b.ps(f"p5psy{tag}", [128, 2, SL], F32)

    ins = nc.sync.dma_start(wf[:], bT.ap().rearrange("(c p) m -> p c m", p=128))
    b.inc(ins, "wld", 16)
    b.wait(nc.vector, "wld")
    nc.vector.tensor_copy(wsb[:], wf[:])
    nc.vector.memset(onesb[:], 1.0)
    nc.vector.memset(ones1[:], 1.0)
    b.barrier()
    for s in range(NSL):
        t0 = s * SL
        ins = nc.sync.dma_start(xsl[:], xs[:, :, t0 : t0 + SL].rearrange("c p t -> p c t"))
        b.inc(ins, "xld", 16)
        b.wait(nc.vector, "xld")
        _ln_to_xn(b, xsl, xn, xb, sqb, rows, onesb, ones1, ps_s, ps_q, ps_b, tmp)
        b.wait(nc.tensor, "dv")
        for ot in range(NCHUNK):
            par = ot % 2
            if b.cnt.get("bev", 0) >= 2:
                b.wait_at(nc.tensor, "bev", b.cnt["bev"] - 1)
            for c in range(NCHUNK):
                ins = nc.tensor.matmul(ps_y[:, par, :], wsb[:, c, ot * 128 : (ot + 1) * 128],
                                       xn[:, c, :], start=(c == 0), stop=(c == 7))
            b.inc(ins, "mm", 1)
            b.wait(nc.vector, "mm")
            ins = nc.vector.tensor_copy(bub[:, ot, :], ps_y[:, par, :])
            b.inc(ins, "bev", 1)
        b.drain_inc(nc.vector, "vdr")
        b.wait(nc.gpsimd, "vdr")
        ins = nc.gpsimd.dma_start(BuT[:, :, t0 : t0 + SL].rearrange("c p t -> p c t"), bub[:])
        b.inc(ins, "bst", 16)
        b.wait(nc.gpsimd, "bst")
        b.barrier()
    b.restore(m0)


def emit_rnn(b, BuT, aT, idn, hT, tag=""):
    nc = b.nc
    m0 = b.mark()
    asb = b.sb(f"p6a{tag}", [128, 8, 1024], BF16)
    af = b.sb(f"p6af{tag}", [128, 8, 1024], F32)
    ids = b.sb(f"p6id{tag}", [128, 128], BF16)
    bub = b.sb(f"p6bu{tag}", [128, 8, T + 16], BF16)
    ht = b.sb(f"p6ht{tag}", [128, 2, 8], BF16)
    hst = b.sb(f"p6h{tag}", [128, 8, T], BF16)
    ps_a = b.ps(f"p6psa{tag}", [128, 8], F32)
    ps_b2 = b.ps(f"p6psb{tag}", [128, 8], F32)

    ins = nc.sync.dma_start(af[:], aT.ap().rearrange("(c p) m -> p c m", p=128))
    b.inc(ins, "wld", 16)
    ins = nc.sync.dma_start(ids[:], idn.ap())
    b.inc(ins, "wld", 16)
    ins = nc.sync.dma_start(bub[:, :, 0:T], BuT.ap().rearrange("c p t -> p c t"))
    b.inc(ins, "wld", 16)
    b.wait(nc.vector, "wld")
    nc.vector.tensor_copy(asb[:], af[:])
    nc.vector.memset(ht[:, 0, :], 0.0)
    nc.vector.memset(bub[:, :, T : T + 16], 0.0)
    b.barrier()

    G = b.cnt.get("gl", 0)
    M = b.cnt.get("mm", 0)
    C = b.cnt.get("hcp", 0)
    UN = 8
    nc.tensor.matmul(ps_a[:], ids[:], bub[:, :, 0:1], start=True, stop=False,
                     skip_group_check=True)
    with nc.Fori(0, T // UN, engines=[ET.PE, ET.Activation, ET.DVE]) as j:
        for sub in range(UN):
            cur, nxt = sub % 2, 1 - sub % 2
            tv = j * UN + sub
            pcur = ps_a if cur == 0 else ps_b2
            pnxt = ps_b2 if cur == 0 else ps_a
            nc.tensor.wait_ge(b.sem("gl"), G + tv)
            for m in range(8):
                for k in range(8):
                    ins = nc.tensor.matmul(
                        pcur[:, m : m + 1],
                        asb[:, k, m * 128 : (m + 1) * 128],
                        ht[:, cur, k : k + 1],
                        start=False, stop=(k == 7), skip_group_check=True)
                    if m == 7 and k == 7:
                        ins.then_inc(b.sem("mm"), 1)
            nc.tensor.matmul(pnxt[:], ids[:], bub[:, :, bass.ds(tv + 1, 1)],
                             start=True, stop=False, skip_group_check=True)
            nc.scalar.wait_ge(b.sem("mm"), M + tv + 1)
            ins = nc.scalar.activation(ht[:, nxt, :], pcur[:], AF.Gelu)
            ins.then_inc(b.sem("gl"), 1)
            nc.vector.wait_ge(b.sem("gl"), G + tv + 1)
            ins = nc.vector.tensor_copy(hst[:, :, bass.ds(tv, 1)], ht[:, nxt, :])
            ins.then_inc(b.sem("hcp"), 1)
    b.cnt["gl"] = G + T
    b.cnt["mm"] = M + T
    b.cnt["hcp"] = C + T
    b.wait(nc.vector, "hcp")
    b.drain_inc(nc.vector, "vdr")
    b.wait(nc.gpsimd, "vdr")
    ins = nc.gpsimd.dma_start(hT.ap().rearrange("c p t -> p c t"), hst[:])
    b.inc(ins, "hst", 16)
    b.wait(nc.gpsimd, "hst")
    b.barrier()
    b.restore(m0)


def emit_cout(b, xs, hT, cT, gb8, tag=""):
    nc = b.nc
    m0 = b.mark()
    wsb = b.sb(f"p7w{tag}", [128, 8, 1024], BF16)
    wf = b.sb(f"p7wf{tag}", [128, 8, 1024], F32)
    gb = b.sb(f"p7gb{tag}", [128, 2, 8], F32)
    xsl = b.sb(f"p7x{tag}", [128, 8, SL], F32)
    hsl = b.sb(f"p7h{tag}", [128, 8, SL], BF16)
    yf = b.sb(f"p7y{tag}", [128, 8, SL], F32)
    yb = b.sb(f"p7yb{tag}", [128, 8, SL], BF16)
    sqb = b.sb(f"p7sq{tag}", [128, 8, SL], BF16)
    tmp = b.sb(f"p7t{tag}", [128, 2, SL], F32)
    rows = b.sb(f"p7rows{tag}", [1, 8, SL], F32)
    onesb = b.sb(f"p7ones{tag}", [128, 1], BF16)
    ones1 = b.sb(f"p7one1{tag}", [1, 128], F32)
    ps_s = b.ps(f"p7pss{tag}", [1, SL], F32)
    ps_q = b.ps(f"p7psq{tag}", [1, SL], F32)
    ps_b = b.ps(f"p7psb{tag}", [128, 2, SL], F32)
    ps_y = b.ps(f"p7psy{tag}", [128, 2, SL], F32)
    MU, E2, VAR, RSTD, MR = 0, 1, 2, 3, 4

    ins = nc.sync.dma_start(wf[:], cT.ap().rearrange("(c p) m -> p c m", p=128))
    b.inc(ins, "wld", 16)
    ins = nc.sync.dma_start(gb[:], gb8.ap())
    b.inc(ins, "wld", 16)
    b.wait(nc.vector, "wld")
    nc.vector.tensor_copy(wsb[:], wf[:])
    nc.vector.memset(onesb[:], 1.0)
    nc.vector.memset(ones1[:], 1.0)
    b.barrier()
    for s in range(NSL):
        t0 = s * SL
        ins = nc.sync.dma_start(xsl[:], xs[:, :, t0 : t0 + SL].rearrange("c p t -> p c t"))
        b.inc(ins, "xld", 16)
        ins = nc.sync.dma_start(hsl[:], hT[:, :, t0 : t0 + SL].rearrange("c p t -> p c t"))
        b.inc(ins, "xld", 16)
        b.wait(nc.tensor, "xld")
        for ot in range(NCHUNK):
            par = ot % 2
            if b.cnt.get("cev", 0) >= 2:
                b.wait_at(nc.tensor, "cev", b.cnt["cev"] - 2)
            for c in range(NCHUNK):
                ins = nc.tensor.matmul(ps_y[:, par, :], wsb[:, c, ot * 128 : (ot + 1) * 128],
                                       hsl[:, c, :], start=(c == 0), stop=(c == 7))
            b.inc(ins, "mm", 1)
            b.wait(nc.vector, "mm")
            nc.vector.tensor_copy(yf[:, ot, :], ps_y[:, par, :])
            ins = nc.vector.tensor_copy(yb[:, ot, :], ps_y[:, par, :])
            b.inc(ins, "cev", 2)
        ins = nc.vector.tensor_tensor(sqb[:], yb[:], yb[:], ALU.mult)
        b.inc(ins, "ac", 1)
        b.wait(nc.tensor, "ac")
        for c in range(NCHUNK):
            nc.tensor.matmul(ps_s[:], onesb[:], yb[:, c, :], start=(c == 0), stop=(c == 7))
        for c in range(NCHUNK):
            ins = nc.tensor.matmul(ps_q[:], onesb[:], sqb[:, c, :], start=(c == 0), stop=(c == 7))
        b.inc(ins, "mm", 1)
        b.wait(nc.vector, "mm")
        nc.vector.tensor_scalar_mul(rows[:, MU, :], ps_s[:], 1.0 / D)
        nc.vector.tensor_scalar_mul(rows[:, E2, :], ps_q[:], 1.0 / D)
        nc.vector.tensor_tensor(rows[:, VAR, :], rows[:, MU, :], rows[:, MU, :], ALU.mult)
        ins = nc.vector.tensor_tensor(rows[:, VAR, :], rows[:, E2, :], rows[:, VAR, :], ALU.subtract)
        b.inc(ins, "dv", 1)
        b.wait(nc.scalar, "dv")
        ins = nc.scalar.activation(rows[:, RSTD, :], rows[:, VAR, :], AF.Sqrt, bias=EPS_LN)
        b.inc(ins, "ac", 1)
        b.wait(nc.vector, "ac")
        nc.vector.reciprocal(rows[:, RSTD, :], rows[:, RSTD, :])
        ins = nc.vector.tensor_tensor(rows[:, MR, :], rows[:, MU, :], rows[:, RSTD, :], ALU.mult)
        b.inc(ins, "dv", 1)
        b.wait(nc.tensor, "dv")
        nc.tensor.matmul(ps_b[:, 0, :], ones1[:], rows[:, RSTD, :], start=True, stop=True)
        ins = nc.tensor.matmul(ps_b[:, 1, :], ones1[:], rows[:, MR, :], start=True, stop=True)
        b.inc(ins, "mm", 1)
        b.wait(nc.vector, "mm")
        for c in range(NCHUNK):
            nc.vector.tensor_tensor(tmp[:, 0, :], yf[:, c, :], ps_b[:, 0, :], ALU.mult)
            nc.vector.tensor_tensor(tmp[:, 1, :], tmp[:, 0, :], ps_b[:, 1, :], ALU.subtract)
            nc.vector.tensor_scalar(tmp[:, 0, :], tmp[:, 1, :], gb[:, 0, c : c + 1],
                                    gb[:, 1, c : c + 1], ALU.mult, ALU.add)
            ins = nc.vector.tensor_tensor(xsl[:, c, :], tmp[:, 0, :], xsl[:, c, :], ALU.add)
        b.inc(ins, "dv", 1)
        b.drain_inc(nc.vector, "vdr")
        b.wait(nc.gpsimd, "vdr")
        ins = nc.gpsimd.dma_start(xs[:, :, t0 : t0 + SL].rearrange("c p t -> p c t"), xsl[:])
        b.inc(ins, "xst", 16)
        b.wait(nc.gpsimd, "xst")
        b.barrier()
    b.restore(m0)


def build_p5():
    nc = bass.Bass("TRN2", target_bir_lowering=False, debug=False)
    xs = nc.dram_tensor("xs", [NCHUNK, 128, T], F32, kind="ExternalInput")
    bT = nc.dram_tensor("bT", [D, D], F32, kind="ExternalInput")
    aT = nc.dram_tensor("aT", [D, D], F32, kind="ExternalInput")
    idn = nc.dram_tensor("idn", [128, 128], BF16, kind="ExternalInput")
    BuT = nc.dram_tensor("BuT", [NCHUNK, 128, T], BF16, kind="ExternalOutput")
    hT = nc.dram_tensor("hT", [NCHUNK, 128, T], BF16, kind="ExternalOutput")
    b = Bld(nc)
    b.setup_consts([EPS_LN, EPS_HLN])
    emit_bu(b, xs, bT, BuT)
    emit_rnn(b, BuT, aT, idn, hT)
    return nc


def build_p6():
    nc = bass.Bass("TRN2", target_bir_lowering=False, debug=False)
    xs = nc.dram_tensor("xs", [NCHUNK, 128, T], F32, kind="ExternalInput")
    xso = nc.dram_tensor("xso", [NCHUNK, 128, T], F32, kind="ExternalOutput")
    hT = nc.dram_tensor("hT", [NCHUNK, 128, T], BF16, kind="ExternalInput")
    cT = nc.dram_tensor("cT", [D, D], F32, kind="ExternalInput")
    gb8 = nc.dram_tensor("gb8", [128, 2, NCHUNK], F32, kind="ExternalInput")
    b = Bld(nc)
    b.setup_consts([EPS_LN, EPS_HLN])
    emit_dram_copy(b, xso, xs)
    emit_cout(b, xso, hT, cT, gb8)
    return nc



def _core_inputs(inputs, bcore):
    x = inputs["L0_attn_x"][bcore]
    xT = np.ascontiguousarray(x.T).reshape(NCHUNK, 128, T)
    d = {"xs": xT}
    d["wrT"] = np.ascontiguousarray(inputs["L0_ffn_W_r"].T)
    d["wkT"] = np.ascontiguousarray(inputs["L0_ffn_W_k"].T)
    d["wvT"] = np.ascontiguousarray(inputs["L0_ffn_W_v"].T)
    murk = np.stack([inputs["L0_ffn_mu_r"].reshape(NCHUNK, 128).T,
                     inputs["L0_ffn_mu_k"].reshape(NCHUNK, 128).T], axis=1)
    d["murk"] = np.ascontiguousarray(murk)
    return d


def emit_dram_copy(b, dst, src, nbytes_hint=None):
    nc = b.nc
    m0 = b.mark()
    cp = b.sb("dcp", [128, NCHUNK, T // 4], F32)
    for i in range(4):
        if i > 0:
            b.wait(nc.sync, "dcp")
        ins = nc.sync.dma_start(cp[:], src[:, :, i * (T // 4) : (i + 1) * (T // 4)].rearrange("c p t -> p c t"))
        b.inc(ins, "dcp", 16)
        b.wait(nc.sync, "dcp")
        ins = nc.sync.dma_start(dst[:, :, i * (T // 4) : (i + 1) * (T // 4)].rearrange("c p t -> p c t"), cp[:])
        b.inc(ins, "dcp", 16)
    b.wait(nc.sync, "dcp")
    b.barrier()
    b.restore(m0)


def build_p0():
    nc = bass.Bass("TRN2", target_bir_lowering=False, debug=False)
    xs = nc.dram_tensor("xs", [NCHUNK, 128, T], F32, kind="ExternalInput")
    xso = nc.dram_tensor("xso", [NCHUNK, 128, T], F32, kind="ExternalOutput")
    wrT = nc.dram_tensor("wrT", [D, D], F32, kind="ExternalInput")
    wkT = nc.dram_tensor("wkT", [D, D], F32, kind="ExternalInput")
    wvT = nc.dram_tensor("wvT", [D, D], F32, kind="ExternalInput")
    murk = nc.dram_tensor("murk", [128, 2, NCHUNK], F32, kind="ExternalInput")
    b = Bld(nc)
    b.setup_consts([EPS_LN, EPS_HLN])
    emit_dram_copy(b, xso, xs)
    emit_l0ffn(b, xso, wrT, wkT, wvT, murk)
    return nc


def build_fused():
    nc = bass.Bass("TRN2", target_bir_lowering=False, debug=False)
    xin = nc.dram_tensor("xin", [NCHUNK, 128, T], F32, kind="ExternalInput")
    xout = nc.dram_tensor("xout", [NCHUNK, 128, T], F32, kind="ExternalOutput")
    wrT = nc.dram_tensor("wrT", [D, D], F32, kind="ExternalInput")
    wkT = nc.dram_tensor("wkT", [D, D], F32, kind="ExternalInput")
    wvT = nc.dram_tensor("wvT", [D, D], F32, kind="ExternalInput")
    murk = nc.dram_tensor("murk", [128, 2, NCHUNK], F32, kind="ExternalInput")
    wqT = nc.dram_tensor("wqT", [D, D], F32, kind="ExternalInput")
    wkTa = nc.dram_tensor("wkTa", [D, D], F32, kind="ExternalInput")
    wvTa = nc.dram_tensor("wvTa", [D, D], F32, kind="ExternalInput")
    qkvbias = nc.dram_tensor("qkvbias", [128, 3, NCHUNK], F32, kind="ExternalInput")
    qkln = nc.dram_tensor("qkln", [128, 4], F32, kind="ExternalInput")
    bvrow = nc.dram_tensor("bvrow", [1, D], F32, kind="ExternalInput")
    hpatd = nc.dram_tensor("hpatd", [2, 128], F32, kind="ExternalInput")
    woT = nc.dram_tensor("woT", [D, D], F32, kind="ExternalInput")
    bo8 = nc.dram_tensor("bo8", [128, NCHUNK], F32, kind="ExternalInput")
    gam8 = nc.dram_tensor("gam8", [128, NCHUNK], F32, kind="ExternalInput")
    ibt = nc.dram_tensor("ibt", [128, 1], F32, kind="ExternalInput")
    w1T = nc.dram_tensor("w1T", [D, DHID], F32, kind="ExternalInput")
    w2T = nc.dram_tensor("w2T", [D, DHID], F32, kind="ExternalInput")
    h1T = nc.dram_tensor("h1T", [DHID, DHID], F32, kind="ExternalInput")
    h2T = nc.dram_tensor("h2T", [DHID, DHID], F32, kind="ExternalInput")
    w3T = nc.dram_tensor("w3T", [DHID, D], F32, kind="ExternalInput")
    scl = nc.dram_tensor("scl", [128, 5, 32], F32, kind="ExternalInput")
    bT = nc.dram_tensor("bT", [D, D], F32, kind="ExternalInput")
    aT = nc.dram_tensor("aT", [D, D], F32, kind="ExternalInput")
    cT = nc.dram_tensor("cT", [D, D], F32, kind="ExternalInput")
    idn = nc.dram_tensor("idn", [128, 128], BF16, kind="ExternalInput")
    gb8 = nc.dram_tensor("gb8", [128, 2, NCHUNK], F32, kind="ExternalInput")
    qT = nc.dram_tensor("qT", [NCHUNK, 128, T], BF16, kind="Internal")
    kT = nc.dram_tensor("kT", [NCHUNK, 128, T], BF16, kind="Internal")
    vtok = nc.dram_tensor("vtok", [T, D], BF16, kind="Internal")
    BuT = nc.dram_tensor("BuT", [NCHUNK, 128, T], BF16, kind="Internal")
    hT = nc.dram_tensor("hT", [NCHUNK, 128, T], BF16, kind="Internal")
    w1R = nc.dram_tensor("w1R", [32, 128, D], BF16, kind="Internal")
    w2R = nc.dram_tensor("w2R", [32, 128, D], BF16, kind="Internal")
    h1R = nc.dram_tensor("h1R", [32, 128, DHID], BF16, kind="Internal")
    h2R = nc.dram_tensor("h2R", [32, 128, DHID], BF16, kind="Internal")
    w3R = nc.dram_tensor("w3R", [8, 128, DHID], BF16, kind="Internal")
    woR = nc.dram_tensor("woR", [D, D], BF16, kind="Internal")
    wqR = nc.dram_tensor("wqR", [D, D], BF16, kind="Internal")
    wkR = nc.dram_tensor("wkR", [D, D], BF16, kind="Internal")
    wvR = nc.dram_tensor("wvR", [D, D], BF16, kind="Internal")
    wrR = nc.dram_tensor("wrR", [D, D], BF16, kind="Internal")
    wkfR = nc.dram_tensor("wkfR", [D, D], BF16, kind="Internal")
    wvfR = nc.dram_tensor("wvfR", [D, D], BF16, kind="Internal")

    b = Bld(nc)
    cbufs = (b.sb("cwfA", [128, 2, 1024], F32), b.sb("cwbA", [128, 2, 1024], BF16),
             b.sb("cwfB", [128, 2, 1024], F32), b.sb("cwbB", [128, 2, 1024], BF16))
    b.setup_consts([EPS_LN, EPS_HLN])
    emit_casts(b, cbufs, [(wrT, wrR, D, D), (wkT, wkfR, D, D), (wvT, wvfR, D, D),
                          (wqT, wqR, D, D), (wkTa, wkR, D, D), (wvTa, wvR, D, D),
                          (woT, woR, D, D)], [])
    b.wcs_flat = b.cnt.get("wcs", 0)
    b.w2s_flat = b.cnt.get("w2s", 0)
    emit_casts(b, cbufs, [], [(w1T, w1R, D, DHID), (w2T, w2R, D, DHID),
                              (h1T, h1R, DHID, DHID), (h2T, h2R, DHID, DHID),
                              (w3T, w3R, DHID, D)])
    emit_l0ffn(b, xout, wrR, wkfR, wvfR, murk, tag="a", xsrc=xin)
    emit_qkv2(b, xout, wqR, wkR, wvR, qkvbias, qkln, bvrow, hpatd, qT, kT, vtok)
    emit_attn2(b, xout, qT, kT, vtok, woR, bo8, gam8, ibt)
    emit_l0ffn(b, xout, wrR, wkfR, wvfR, murk, tag="b")
    emit_l1ffn(b, xout, w1R, w2R, h1R, h2R, w3R, scl, tag="a")
    emit_bu(b, xout, bT, BuT)
    emit_rnn(b, BuT, aT, idn, hT)
    emit_cout(b, xout, hT, cT, gb8)
    emit_l1ffn(b, xout, w1R, w2R, h1R, h2R, w3R, scl, tag="b")
    return nc


def _fused_inputs(inputs):
    def tile8(v):
        return np.ascontiguousarray(v.reshape(NCHUNK, 128).T)
    qkln = np.stack([np.tile(inputs["L0_attn_qn_scale"], 2), np.tile(inputs["L0_attn_qn_bias"], 2),
                     np.tile(inputs["L0_attn_kn_scale"], 2), np.tile(inputs["L0_attn_kn_bias"], 2)],
                    axis=1).astype(np.float32)
    common = {
        "wrT": np.ascontiguousarray(inputs["L0_ffn_W_r"].T),
        "wkT": np.ascontiguousarray(inputs["L0_ffn_W_k"].T),
        "wvT": np.ascontiguousarray(inputs["L0_ffn_W_v"].T),
        "murk": np.ascontiguousarray(np.stack(
            [inputs["L0_ffn_mu_r"].reshape(NCHUNK, 128).T,
             inputs["L0_ffn_mu_k"].reshape(NCHUNK, 128).T], axis=1)),
        "wqT": np.ascontiguousarray(inputs["L0_attn_Wq"]),
        "wkTa": np.ascontiguousarray(inputs["L0_attn_Wk"]),
        "wvTa": np.ascontiguousarray(inputs["L0_attn_Wv"]),
        "qkvbias": np.ascontiguousarray(np.stack(
            [tile8(inputs["L0_attn_bq"]), tile8(inputs["L0_attn_bk"]),
             tile8(inputs["L0_attn_bv"])], axis=1)),
        "qkln": np.ascontiguousarray(qkln),
        "bvrow": np.ascontiguousarray(inputs["L0_attn_bv"][None, :]),
        "hpatd": np.repeat(np.eye(2, dtype=np.float32), 64, axis=1),
        "woT": np.ascontiguousarray(inputs["L0_attn_Wo"]),
        "bo8": tile8(inputs["L0_attn_bo"]),
        "gam8": tile8(inputs["L0_attn_gamma"]),
        "ibt": np.ascontiguousarray(np.tile(inputs["L0_attn_init_bias"].reshape(1, 1),
                                            (128, 1)).astype(np.float32)),
        "w1T": np.ascontiguousarray(inputs["L1_ffn_w1_w"].T),
        "w2T": np.ascontiguousarray(inputs["L1_ffn_w2_w"].T),
        "h1T": np.ascontiguousarray(inputs["L1_ffn_h1_w"].T),
        "h2T": np.ascontiguousarray(inputs["L1_ffn_h2_w"].T),
        "w3T": np.ascontiguousarray(inputs["L1_ffn_w3_w"].T),
        "scl": _p4_scl(inputs),
        "bT": np.ascontiguousarray(inputs["L1_attn_B"].T),
        "aT": np.ascontiguousarray(inputs["L1_attn_A"].T),
        "cT": np.ascontiguousarray(inputs["L1_attn_C"].T),
        "idn": np.eye(128).astype(BF),
        "gb8": np.ascontiguousarray(np.stack(
            [inputs["L1_attn_gamma"].reshape(NCHUNK, 128).T,
             inputs["L1_attn_beta"].reshape(NCHUNK, 128).T], axis=1)),
    }
    return common


def kernel(**inputs):
    nc = build_fused()
    common = _fused_inputs(inputs)
    x = inputs["L0_attn_x"].astype(np.float32, copy=False)
    in_maps = [dict(common, xin=_x_to_dev(x[bc])) for bc in range(8)]
    res = run_bass_kernel_spmd(nc, in_maps, core_ids=list(range(8)))
    out = np.stack([_x_from_dev(res.results[bc]["xout"]) for bc in range(8)])
    return out.astype(np.float32)


def _x_to_dev(x2d):
    return np.ascontiguousarray(x2d.T).reshape(NCHUNK, 128, T)


def _x_from_dev(xdev):
    return xdev.reshape(D, T).T




# version 21
# speedup vs baseline: 1.5717x; 1.0088x over previous
import numpy as np
import ml_dtypes
import concourse.bass as bass
import concourse.mybir as mybir
from concourse.bass_utils import run_bass_kernel_spmd

F32 = mybir.dt.float32
BF16 = mybir.dt.bfloat16
AF = mybir.ActivationFunctionType
ALU = mybir.AluOpType
ET = mybir.EngineType
BF = ml_dtypes.bfloat16

T = 4096
D = 1024
NCHUNK = 8
SL = 512
NSL = T // SL
DHID = 4096
EPS_LN = 1e-5
EPS_HLN = 1e-6


class Bld:

    def __init__(self, nc):
        self.nc = nc
        self.sems = {}
        self.cnt = {}

    def sem(self, name):
        if name not in self.sems:
            self.sems[name] = self.nc.alloc_semaphore(name=name)
            self.cnt[name] = 0
        return self.sems[name]

    def inc(self, ins, name, k=1):
        ins.then_inc(self.sem(name), k)
        self.cnt[name] += k

    def wait(self, eng, name):
        eng.wait_ge(self.sem(name), self.cnt[name])

    def wait_at(self, eng, name, v):
        eng.wait_ge(self.sem(name), v)

    def barrier(self):
        self.nc.all_engine_barrier()

    def setup_consts(self, vals):
        t = self.sb("consts", [128, len(vals)], F32)
        for i, v in enumerate(vals):
            self.nc.vector.memset(t[:, i : i + 1], v)
            self.nc.const_aps.register(F32, v, t[:, i : i + 1])
        self.barrier()

    def drain_inc(self, eng, name):
        ins = eng.drain()
        self.inc(ins, name, 1)

    def sb(self, name, shape, dtype):
        return self.nc.alloc_sbuf_tensor(name, shape, dtype)

    def ps(self, name, shape, dtype=F32):
        return self.nc.alloc_psum_tensor(name, shape, dtype)

    def mark(self):
        nc = self.nc
        return (nc.sbuf_base, nc.sbuf_top, nc.psum_base, nc.psum_top)

    def restore(self, m):
        nc = self.nc
        nc.sbuf_base, nc.sbuf_top, nc.psum_base, nc.psum_top = m



def emit_l0ffn(b, xs, wrT, wkT, wvT, murk, tag="", xsrc=None):
    nc = b.nc
    m0 = b.mark()
    wsb = b.sb(f"p0w{tag}", [128, 3, 8, 1024], BF16)
    wf = b.sb(f"p0wf{tag}", [128, 8, 1024], F32)
    mu = b.sb(f"p0mu{tag}", [128, 4, 8], F32)
    xsl = b.sb(f"p0x{tag}", [128, 8, SL], F32)
    xb = b.sb(f"p0xb{tag}", [128, 8, SL], BF16)
    sqb = b.sb(f"p0sq{tag}", [128, 8, SL], BF16)
    xn = b.sb(f"p0xn{tag}", [128, 8, SL + 1], BF16)
    xr = b.sb(f"p0xr{tag}", [128, 8, SL], BF16)
    xk = b.sb(f"p0xk{tag}", [128, 8, SL], BF16)
    rg = b.sb(f"p0rg{tag}", [128, 8, SL], BF16)
    kk = b.sb(f"p0kk{tag}", [128, 8, SL], BF16)
    tmp = b.sb(f"p0tmp{tag}", [128, 8, SL], F32)
    rows = b.sb(f"p0rows{tag}", [1, 8, SL], F32)
    onesb = b.sb(f"p0ones{tag}", [128, 1], BF16)
    ones1 = b.sb(f"p0one1{tag}", [1, 128], F32)
    ps_s = b.ps(f"p0pss{tag}", [1, SL], F32)
    ps_q = b.ps(f"p0psq{tag}", [1, SL], F32)
    ps_b = b.ps(f"p0psb{tag}", [128, 2, SL], F32)
    ps_y = b.ps(f"p0psy{tag}", [128, 2, SL], F32)

    MU, E2, VAR, RSTD, MR = 0, 1, 2, 3, 4
    for wi, w in enumerate((wrT, wkT, wvT)):
        if wi > 0:
            b.wait(nc.sync, "wfree")
        ins = nc.sync.dma_start(wf[:], w.ap().rearrange("(c p) m -> p c m", p=128))
        b.inc(ins, "wld", 16)
        b.wait(nc.vector, "wld")
        ins = nc.vector.tensor_copy(wsb[:, wi, :, :], wf[:])
        b.inc(ins, "wfree", 1)
    ins = nc.sync.dma_start(mu[:, 0:2, :], murk.ap())
    b.inc(ins, "wld", 16)
    b.wait(nc.vector, "wld")
    nc.vector.memset(onesb[:], 1.0)
    nc.vector.memset(ones1[:], 1.0)
    nc.vector.tensor_scalar(mu[:, 2:4, :], mu[:, 0:2, :], -1.0, 1.0, ALU.mult, ALU.add)
    nc.vector.memset(xn[:, :, 0:1], 0.0)
    b.barrier()

    for s in range(NSL):
        _p0_slice(b, xs, s, wsb, mu, xsl, xb, sqb, xn, xr, xk, rg, kk, tmp, rows,
                  onesb, ones1, ps_s, ps_q, ps_b, ps_y, xsrc=xsrc)
    b.restore(m0)


def _p0_slice(b, xs, s, wsb, mu, xsl, xb, sqb, xn, xr, xk, rg, kk, tmp, rows,
              onesb, ones1, ps_s, ps_q, ps_b, ps_y, xsrc=None):
    nc = b.nc
    MU, E2, VAR, RSTD, MR = 0, 1, 2, 3, 4
    t0 = s * SL
    xrd = xs if xsrc is None else xsrc
    ins = nc.sync.dma_start(xsl[:], xrd[:, :, t0 : t0 + SL].rearrange("c p t -> p c t"))
    b.inc(ins, "xld", 16)
    if s > 0:
        nc.vector.tensor_copy(xn[:, :, 0:1], xn[:, :, SL : SL + 1])
    b.wait(nc.vector, "xld")
    nc.vector.tensor_copy(xb[:], xsl[:])
    ins = nc.vector.tensor_tensor(sqb[:], xb[:], xb[:], ALU.mult)
    b.inc(ins, "dv", 1)
    b.wait(nc.tensor, "dv")
    for c in range(NCHUNK):
        nc.tensor.matmul(ps_s[:], onesb[:], xb[:, c, :], start=(c == 0), stop=(c == 7))
    for c in range(NCHUNK):
        ins = nc.tensor.matmul(ps_q[:], onesb[:], sqb[:, c, :], start=(c == 0), stop=(c == 7))
    b.inc(ins, "mm", 1)
    b.wait(nc.vector, "mm")
    nc.vector.tensor_scalar_mul(rows[:, MU, :], ps_s[:], 1.0 / D)
    nc.vector.tensor_scalar_mul(rows[:, E2, :], ps_q[:], 1.0 / D)
    nc.vector.tensor_tensor(rows[:, VAR, :], rows[:, MU, :], rows[:, MU, :], ALU.mult)
    ins = nc.vector.tensor_tensor(rows[:, VAR, :], rows[:, E2, :], rows[:, VAR, :], ALU.subtract)
    b.inc(ins, "dv", 1)
    b.wait(nc.scalar, "dv")
    nc.scalar.activation(rows[:, E2, :], rows[:, VAR, :], AF.Ln, bias=EPS_LN)
    ins = nc.scalar.activation(rows[:, RSTD, :], rows[:, E2, :], AF.Exp, scale=-0.5)
    b.inc(ins, "ac", 1)
    b.wait(nc.vector, "ac")
    ins = nc.vector.tensor_tensor(rows[:, MR, :], rows[:, MU, :], rows[:, RSTD, :], ALU.mult)
    b.inc(ins, "dv", 1)
    b.wait(nc.tensor, "dv")
    nc.tensor.matmul(ps_b[:, 0, :], ones1[:], rows[:, RSTD, :], start=True, stop=True)
    ins = nc.tensor.matmul(ps_b[:, 1, :], ones1[:], rows[:, MR, :], start=True, stop=True)
    b.inc(ins, "mm", 1)
    b.wait(nc.vector, "mm")
    for c in range(NCHUNK):
        nc.vector.tensor_tensor(tmp[:, c, :], xsl[:, c, :], ps_b[:, 0, :], ALU.mult)
        nc.vector.tensor_tensor(xn[:, c, 1 : SL + 1], tmp[:, c, :], ps_b[:, 1, :], ALU.subtract)
    for c in range(NCHUNK):
        nc.vector.tensor_scalar_mul(xr[:, c, :], xn[:, c, 0:SL], mu[:, 2, c : c + 1])
        nc.vector.scalar_tensor_tensor(xr[:, c, :], xn[:, c, 1 : SL + 1], mu[:, 0, c : c + 1],
                                       xr[:, c, :], ALU.mult, ALU.add)
        nc.vector.tensor_scalar_mul(xk[:, c, :], xn[:, c, 0:SL], mu[:, 3, c : c + 1])
        ins = nc.vector.scalar_tensor_tensor(xk[:, c, :], xn[:, c, 1 : SL + 1], mu[:, 1, c : c + 1],
                                             xk[:, c, :], ALU.mult, ALU.add)
    b.inc(ins, "dv", 1)
    b.wait(nc.tensor, "dv")
    for ot in range(NCHUNK):
        par = ot % 2
        if ot >= 2:
            b.wait_at(nc.tensor, "ev", b.cnt["ev"] - 1)
        for c in range(NCHUNK):
            ins = nc.tensor.matmul(ps_y[:, par, :], wsb[:, 0, c, ot * 128 : (ot + 1) * 128],
                                   xr[:, c, :], start=(c == 0), stop=(c == 7))
        b.inc(ins, "mm", 1)
        b.wait(nc.scalar, "mm")
        ins = nc.scalar.activation(rg[:, ot, :], ps_y[:, par, :], AF.Sigmoid)
        b.inc(ins, "ev", 1)
    for ot in range(NCHUNK):
        par = ot % 2
        b.wait_at(nc.tensor, "ev", b.cnt["ev"] - 1)
        for c in range(NCHUNK):
            ins = nc.tensor.matmul(ps_y[:, par, :], wsb[:, 1, c, ot * 128 : (ot + 1) * 128],
                                   xk[:, c, :], start=(c == 0), stop=(c == 7))
        b.inc(ins, "mm", 1)
        b.wait(nc.vector, "mm")
        ins = nc.vector.tensor_scalar_max(tmp[:, ot, :], ps_y[:, par, :], 0.0)
        b.inc(ins, "ev", 1)
        ins = nc.vector.tensor_tensor(kk[:, ot, :], tmp[:, ot, :], tmp[:, ot, :], ALU.mult)
        b.inc(ins, "ac", 1)
    b.wait(nc.tensor, "ac")
    for ot in range(NCHUNK):
        par = ot % 2
        b.wait_at(nc.tensor, "ev", b.cnt["ev"] - 1)
        for c in range(NCHUNK):
            ins = nc.tensor.matmul(ps_y[:, par, :], wsb[:, 2, c, ot * 128 : (ot + 1) * 128],
                                   kk[:, c, :], start=(c == 0), stop=(c == 7))
        b.inc(ins, "mm", 1)
        b.wait(nc.vector, "mm")
        nc.vector.tensor_tensor(tmp[:, ot, :], ps_y[:, par, :], rg[:, ot, :], ALU.mult)
        ins = nc.vector.scalar_tensor_tensor(xsl[:, ot, :], tmp[:, ot, :], 0.5,
                                             xsl[:, ot, :], ALU.mult, ALU.add)
        b.inc(ins, "ev", 1)
    b.drain_inc(nc.vector, "vdr")
    b.wait(nc.gpsimd, "vdr")
    ins = nc.gpsimd.dma_start(xs[:, :, t0 : t0 + SL].rearrange("c p t -> p c t"), xsl[:])
    b.inc(ins, "xst", 16)
    b.wait(nc.gpsimd, "xst")
    b.barrier()



def emit_qkv(b, xs, wqT, wkT, wvT, qkvbias, qkln, bvrow, hpatd, qT, kT, vtok, tag=""):
    nc = b.nc
    m0 = b.mark()
    wsb = b.sb(f"p1w{tag}", [128, 3, 8, 1024], BF16)
    wf = b.sb(f"p1wf{tag}", [128, 8, 1024], F32)
    bias = b.sb(f"p1b{tag}", [128, 3, 8], F32)
    qkl = b.sb(f"p1l{tag}", [128, 4], F32)
    bvb = b.sb(f"p1bvb{tag}", [128, 1024], F32)
    bvr = b.sb(f"p1bvr{tag}", [1, 1024], F32)
    xsl = b.sb(f"p1x{tag}", [128, 8, SL], F32)
    xb = b.sb(f"p1xb{tag}", [128, 8, SL], BF16)
    sqb = b.sb(f"p1sq{tag}", [128, 8, SL], BF16)
    xn = b.sb(f"p1xn{tag}", [128, 8, SL], BF16)
    qb = b.sb(f"p1qb{tag}", [128, 8, SL], BF16)
    sqq = b.sb(f"p1sqq{tag}", [128, 2, SL], BF16)
    qn2 = b.sb(f"p1qn{tag}", [128, 2, 8, SL], BF16)
    vsl = b.sb(f"p1v{tag}", [128, 4, 1024], BF16)
    tmp = b.sb(f"p1tmp{tag}", [128, 4, SL], F32)
    rows = b.sb(f"p1rows{tag}", [1, 8, SL], F32)
    hrows = b.sb(f"p1hr{tag}", [2, 2, 5, SL], F32)
    onesb = b.sb(f"p1ones{tag}", [128, 1], BF16)
    ones1 = b.sb(f"p1one1{tag}", [1, 128], F32)
    blk2 = b.sb(f"p1blk{tag}", [128, 2], BF16)
    hpat = b.sb(f"p1hp{tag}", [2, 128], F32)
    ps_s = b.ps(f"p1pss{tag}", [1, SL], F32)
    ps_q = b.ps(f"p1psq{tag}", [1, SL], F32)
    ps_b = b.ps(f"p1psb{tag}", [128, 2, SL], F32)
    ps_y = b.ps(f"p1psy{tag}", [128, 2, SL], F32)
    ps_h = b.ps(f"p1psh{tag}", [2, 2, SL], F32)

    MU, E2, VAR, RSTD, MR = 0, 1, 2, 3, 4
    for wi, w in enumerate((wqT, wkT, wvT)):
        if wi > 0:
            b.wait(nc.sync, "wfree")
        ins = nc.sync.dma_start(wf[:], w.ap().rearrange("(c p) m -> p c m", p=128))
        b.inc(ins, "wld", 16)
        b.wait(nc.vector, "wld")
        ins = nc.vector.tensor_copy(wsb[:, wi, :, :], wf[:])
        b.inc(ins, "wfree", 1)
    for src, dst in ((qkvbias, bias), (qkln, qkl), (bvrow, bvr)):
        ins = nc.sync.dma_start(dst[:], src.ap())
        b.inc(ins, "wld", 16)
    b.wait(nc.vector, "wld")
    nc.vector.memset(onesb[:], 1.0)
    nc.vector.memset(ones1[:], 1.0)
    nc.vector.memset(blk2[:, :], 0.0)
    nc.vector.memset(blk2[0:64, 0:1], 1.0)
    nc.vector.memset(blk2[64:128, 1:2], 1.0)
    ins = nc.sync.dma_start(hpat[:], hpatd.ap())
    b.inc(ins, "wld", 16)
    b.wait(nc.vector, "wld")
    ins = nc.vector.memset(blk2[64:128, 0:1], 0.0)
    b.inc(ins, "dv", 1)
    b.wait(nc.tensor, "dv")
    nc.tensor.matmul(ps_b[:, 0, :], ones1[:], bvr[:, 0:SL], start=True, stop=True)
    ins = nc.tensor.matmul(ps_b[:, 1, :], ones1[:], bvr[:, SL:D], start=True, stop=True)
    b.inc(ins, "mm", 1)
    b.wait(nc.vector, "mm")
    nc.vector.tensor_copy(bvb[:], ps_b.ap().rearrange("p a t -> p (a t)"))
    b.barrier()

    for s in range(NSL):
        t0 = s * SL
        ins = nc.sync.dma_start(xsl[:], xs[:, :, t0 : t0 + SL].rearrange("c p t -> p c t"))
        b.inc(ins, "xld", 16)
        b.wait(nc.vector, "xld")
        nc.vector.tensor_copy(xb[:], xsl[:])
        ins = nc.vector.tensor_tensor(sqb[:], xb[:], xb[:], ALU.mult)
        b.inc(ins, "dv", 1)
        b.wait(nc.tensor, "dv")
        for c in range(NCHUNK):
            nc.tensor.matmul(ps_s[:], onesb[:], xb[:, c, :], start=(c == 0), stop=(c == 7))
        for c in range(NCHUNK):
            ins = nc.tensor.matmul(ps_q[:], onesb[:], sqb[:, c, :], start=(c == 0), stop=(c == 7))
        b.inc(ins, "mm", 1)
        b.wait(nc.vector, "mm")
        nc.vector.tensor_scalar_mul(rows[:, MU, :], ps_s[:], 1.0 / D)
        nc.vector.tensor_scalar_mul(rows[:, E2, :], ps_q[:], 1.0 / D)
        nc.vector.tensor_tensor(rows[:, VAR, :], rows[:, MU, :], rows[:, MU, :], ALU.mult)
        ins = nc.vector.tensor_tensor(rows[:, VAR, :], rows[:, E2, :], rows[:, VAR, :], ALU.subtract)
        b.inc(ins, "dv", 1)
        b.wait(nc.scalar, "dv")
        nc.scalar.activation(rows[:, E2, :], rows[:, VAR, :], AF.Ln, bias=EPS_LN)
        ins = nc.scalar.activation(rows[:, RSTD, :], rows[:, E2, :], AF.Exp, scale=-0.5)
        b.inc(ins, "ac", 1)
        b.wait(nc.vector, "ac")
        ins = nc.vector.tensor_tensor(rows[:, MR, :], rows[:, MU, :], rows[:, RSTD, :], ALU.mult)
        b.inc(ins, "dv", 1)
        b.wait(nc.tensor, "dv")
        nc.tensor.matmul(ps_b[:, 0, :], ones1[:], rows[:, RSTD, :], start=True, stop=True)
        ins = nc.tensor.matmul(ps_b[:, 1, :], ones1[:], rows[:, MR, :], start=True, stop=True)
        b.inc(ins, "mm", 1)
        b.wait(nc.vector, "mm")
        for c in range(NCHUNK):
            nc.vector.tensor_tensor(tmp[:, 0, :], xsl[:, c, :], ps_b[:, 0, :], ALU.mult)
            ins = nc.vector.tensor_tensor(xn[:, c, :], tmp[:, 0, :], ps_b[:, 1, :], ALU.subtract)
        b.inc(ins, "dv", 1)
        b.wait(nc.tensor, "dv")
        for qk in range(2):
            SC, BI = (0, 1) if qk == 0 else (2, 3)
            dst = qT if qk == 0 else kT
            qn = qn2[:, qk, :, :]
            for ot in range(NCHUNK):
                par = ot % 2
                if b.cnt.get("ev", 0) > 0:
                    b.wait_at(nc.tensor, "ev", b.cnt["ev"] - 1)
                for c in range(NCHUNK):
                    ins = nc.tensor.matmul(ps_y[:, par, :], wsb[:, qk, c, ot * 128 : (ot + 1) * 128],
                                           xn[:, c, :], start=(c == 0), stop=(c == 7))
                b.inc(ins, "mm", 1)
                b.wait(nc.vector, "mm")
                ins = nc.vector.tensor_scalar_add(qb[:, ot, :], ps_y[:, par, :],
                                                  bias[:, qk, ot : ot + 1])
                b.inc(ins, "ev", 1)
                ins = nc.vector.tensor_tensor(sqq[:, par, :], qb[:, ot, :], qb[:, ot, :], ALU.mult)
                b.inc(ins, "ac", 1)
                b.wait(nc.tensor, "ac")
                b.wait(nc.tensor, "dv")
                nc.tensor.matmul(ps_h[:, 0, :], blk2[:], qb[:, ot, :], start=True, stop=True)
                ins = nc.tensor.matmul(ps_h[:, 1, :], blk2[:], sqq[:, par, :], start=True, stop=True)
                b.inc(ins, "mm", 1)
                b.wait(nc.vector, "mm")
                hr = hrows[:, par, :, :]
                nc.vector.tensor_scalar_mul(hr[:, MU, :], ps_h[:, 0, :], 1.0 / 64)
                nc.vector.tensor_scalar_mul(hr[:, E2, :], ps_h[:, 1, :], 1.0 / 64)
                nc.vector.tensor_tensor(hr[:, VAR, :], hr[:, MU, :], hr[:, MU, :], ALU.mult)
                ins = nc.vector.tensor_tensor(hr[:, VAR, :], hr[:, E2, :], hr[:, VAR, :], ALU.subtract)
                b.inc(ins, "dv", 1)
                b.wait(nc.scalar, "dv")
                ins = nc.scalar.activation(hr[:, RSTD, :], hr[:, VAR, :], AF.Sqrt, bias=EPS_HLN)
                b.inc(ins, "ac", 1)
                b.wait(nc.vector, "ac")
                nc.vector.reciprocal(hr[:, RSTD, :], hr[:, RSTD, :])
                ins = nc.vector.tensor_tensor(hr[:, MR, :], hr[:, MU, :], hr[:, RSTD, :], ALU.mult)
                b.inc(ins, "dv", 1)
                b.wait(nc.tensor, "dv")
                if b.cnt.get("qnev", 0) > 0:
                    b.wait_at(nc.tensor, "qnev", b.cnt["qnev"])
                nc.tensor.matmul(ps_b[:, 0, :], hpat[:], hr[:, RSTD, :], start=True, stop=True)
                ins = nc.tensor.matmul(ps_b[:, 1, :], hpat[:], hr[:, MR, :], start=True, stop=True)
                b.inc(ins, "mm", 1)
                b.wait(nc.vector, "mm")
                nc.vector.tensor_tensor(tmp[:, 2 * par, :], qb[:, ot, :], ps_b[:, 0, :], ALU.mult)
                nc.vector.tensor_tensor(tmp[:, 2 * par + 1, :], tmp[:, 2 * par, :],
                                        ps_b[:, 1, :], ALU.subtract)
                ins = nc.vector.tensor_scalar(qn[:, ot, :], tmp[:, 2 * par + 1, :],
                                              qkl[:, SC : SC + 1], qkl[:, BI : BI + 1],
                                              ALU.mult, ALU.add)
                b.inc(ins, "qnev", 1)
            b.drain_inc(nc.vector, "vdr")
            b.wait(nc.gpsimd, "vdr")
            ins = nc.gpsimd.dma_start(dst[:, :, t0 : t0 + SL].rearrange("c p t -> p c t"), qn[:])
            b.inc(ins, "qst", 16)
        b.wait(nc.tensor, "qnev")
        for tt in range(4):
            par = tt % 2
            pv = ps_y if par == 0 else ps_b
            if tt >= 2:
                b.wait_at(nc.tensor, "vev", b.cnt["vev"] - 1)
            for half in range(2):
                for c in range(NCHUNK):
                    ins = nc.tensor.matmul(pv[:, half, :],
                                           xn[:, c, tt * 128 : (tt + 1) * 128],
                                           wsb[:, 2, c, half * SL : (half + 1) * SL],
                                           start=(c == 0), stop=(c == 7))
            b.inc(ins, "mm", 1)
            b.wait(nc.vector, "mm")
            ins = nc.vector.tensor_tensor(vsl[:, tt, :], pv.ap().rearrange("p a t -> p (a t)"),
                                          bvb[:], ALU.add)
            b.inc(ins, "vev", 1)
        b.drain_inc(nc.vector, "vdr")
        b.wait(nc.gpsimd, "vdr")
        ins = nc.gpsimd.dma_start(vtok[t0 : t0 + SL, :].rearrange("(q p) d -> p q d", p=128), vsl[:])
        b.inc(ins, "qst", 16)
        b.wait(nc.gpsimd, "qst")
        b.barrier()
    b.restore(m0)


def build_p1():
    nc = bass.Bass("TRN2", target_bir_lowering=False, debug=False)
    xs = nc.dram_tensor("xs", [NCHUNK, 128, T], F32, kind="ExternalInput")
    wqT = nc.dram_tensor("wqT", [D, D], F32, kind="ExternalInput")
    wkTa = nc.dram_tensor("wkTa", [D, D], F32, kind="ExternalInput")
    wvTa = nc.dram_tensor("wvTa", [D, D], F32, kind="ExternalInput")
    qkvbias = nc.dram_tensor("qkvbias", [128, 3, NCHUNK], F32, kind="ExternalInput")
    qkln = nc.dram_tensor("qkln", [128, 4], F32, kind="ExternalInput")
    bvrow = nc.dram_tensor("bvrow", [1, D], F32, kind="ExternalInput")
    hpatd = nc.dram_tensor("hpatd", [2, 128], F32, kind="ExternalInput")
    qT = nc.dram_tensor("qT", [NCHUNK, 128, T], BF16, kind="ExternalOutput")
    kT = nc.dram_tensor("kT", [NCHUNK, 128, T], BF16, kind="ExternalOutput")
    vtok = nc.dram_tensor("vtok", [T, D], BF16, kind="ExternalOutput")
    b = Bld(nc)
    b.setup_consts([EPS_LN, EPS_HLN, -0.5])
    emit_qkv(b, xs, wqT, wkTa, wvTa, qkvbias, qkln, bvrow, hpatd, qT, kT, vtok)
    return nc


def _p1_inputs(inputs, x_stage):
    def tile8(v):
        return np.ascontiguousarray(v.reshape(NCHUNK, 128).T)
    bias = np.stack([tile8(inputs["L0_attn_bq"]), tile8(inputs["L0_attn_bk"]),
                     tile8(inputs["L0_attn_bv"])], axis=1)
    qkln = np.stack([np.tile(inputs["L0_attn_qn_scale"], 2), np.tile(inputs["L0_attn_qn_bias"], 2),
                     np.tile(inputs["L0_attn_kn_scale"], 2), np.tile(inputs["L0_attn_kn_bias"], 2)],
                    axis=1)
    common = {
        "wqT": np.ascontiguousarray(inputs["L0_attn_Wq"]),
        "wkTa": np.ascontiguousarray(inputs["L0_attn_Wk"]),
        "wvTa": np.ascontiguousarray(inputs["L0_attn_Wv"]),
        "qkvbias": np.ascontiguousarray(bias),
        "qkln": np.ascontiguousarray(qkln),
        "bvrow": np.ascontiguousarray(inputs["L0_attn_bv"][None, :]),
        "hpatd": np.repeat(np.eye(2, dtype=np.float32), 64, axis=1),
    }
    maps = []
    for bc in range(8):
        d = dict(common)
        d["xs"] = np.ascontiguousarray(x_stage[bc].T).reshape(NCHUNK, 128, T)
        maps.append(d)
    return maps



def emit_qkv2(b, xs, wqR, wkR, wvR, qkvbias, qkln, bvrow, hpatd, qT, kT, vtok, tag=""):
    nc = b.nc
    m0 = b.mark()
    MU, E2i, VAR, RSTD, MR = 0, 1, 2, 3, 4
    wsb = b.sb(f"q2w{tag}", [128, 3, 8, 1024], BF16)
    bias = b.sb(f"q2b{tag}", [128, 3, 8], F32)
    qkl = b.sb(f"q2l{tag}", [128, 4], F32)
    bvb = b.sb(f"q2bvb{tag}", [128, 1024], F32)
    bvr = b.sb(f"q2bvr{tag}", [1, 1024], F32)
    xsl = b.sb(f"q2x{tag}", [128, 8, SL], F32)
    xb = b.sb(f"q2xb{tag}", [128, 8, SL], BF16)
    sqb = b.sb(f"q2sq{tag}", [128, 8, SL], BF16)
    xn = b.sb(f"q2xn{tag}", [128, 8, SL], BF16)
    qb = b.sb(f"q2qb{tag}", [128, 2, 8, SL], BF16)
    sqq = b.sb(f"q2sqq{tag}", [128, 2, SL], BF16)
    qn2 = b.sb(f"q2qn{tag}", [128, 2, 8, SL], BF16)
    vsl = b.sb(f"q2v{tag}", [128, 4, 1024], BF16)
    tmp = b.sb(f"q2tmp{tag}", [128, 4, SL], F32)
    rows = b.sb(f"q2rows{tag}", [1, 8, SL], F32)
    hrows = b.sb(f"q2hr{tag}", [2, 2, 5, SL], F32)
    onesb = b.sb(f"q2ones{tag}", [128, 1], BF16)
    ones1 = b.sb(f"q2one1{tag}", [1, 128], F32)
    blk2 = b.sb(f"q2blk{tag}", [128, 2], BF16)
    hpat = b.sb(f"q2hp{tag}", [2, 128], F32)
    ps_y = b.ps(f"q2psy{tag}", [128, 2, SL], F32)
    ps_h = b.ps(f"q2psh{tag}", [128, 2, SL], F32)
    ps_b = b.ps(f"q2psb{tag}", [128, 4, SL], F32)

    if getattr(b, "wcs_flat", 0) > 0:
        b.wait_at(nc.sync, "wcs", b.wcs_flat)
    if getattr(b, "w2s_flat", 0) > 0:
        b.wait_at(nc.sync, "w2s", b.w2s_flat)
    for wi, w in enumerate((wqR, wkR, wvR)):
        ins = nc.sync.dma_start(wsb[:, wi, :, :], w.ap().rearrange("(c p) m -> p c m", p=128))
        b.inc(ins, "wld", 16)
    for src, dst in ((qkvbias, bias), (qkln, qkl), (bvrow, bvr), (hpatd, hpat)):
        ins = nc.sync.dma_start(dst[:], src.ap())
        b.inc(ins, "wld", 16)
    b.wait(nc.vector, "wld")
    nc.vector.memset(onesb[:], 1.0)
    nc.vector.memset(ones1[:], 1.0)
    nc.vector.memset(blk2[:, :], 0.0)
    nc.vector.memset(blk2[0:64, 0:1], 1.0)
    ins = nc.vector.memset(blk2[64:128, 1:2], 1.0)
    b.inc(ins, "dv", 1)
    b.wait(nc.tensor, "dv")
    nc.tensor.matmul(ps_b[:, 0, :], ones1[:], bvr[:, 0:SL], start=True, stop=True)
    ins = nc.tensor.matmul(ps_b[:, 1, :], ones1[:], bvr[:, SL:D], start=True, stop=True)
    b.inc(ins, "mm", 1)
    b.wait(nc.vector, "mm")
    nc.vector.tensor_copy(bvb[:, 0:SL], ps_b[:, 0, :])
    ins = nc.vector.tensor_copy(bvb[:, SL:D], ps_b[:, 1, :])
    b.inc(ins, "dv", 1)
    b.barrier()

    for s in range(NSL):
        t0 = s * SL
        ins = nc.sync.dma_start(xsl[:], xs[:, :, t0 : t0 + SL].rearrange("c p t -> p c t"))
        b.inc(ins, "xld", 16)
        b.wait(nc.vector, "xld")
        _ln_to_xn(b, xsl, xn, xb, sqb, rows, onesb, ones1,
                  ps_h[0:1, 0, :], ps_h[64:65, 0, :], ps_b[:, 0:2, :], tmp[:, 0, :])
        b.wait(nc.tensor, "dv")
        qmm0 = b.cnt.get("qmm", 0)
        qbe0 = b.cnt.get("qbe", 0)
        qsq0 = b.cnt.get("qsq", 0)
        smm0 = b.cnt.get("smm", 0)
        dvv0 = b.cnt.get("dvv", 0)
        acs0 = b.cnt.get("acs", 0)
        hrm0 = b.cnt.get("hrm", 0)
        bmm0 = b.cnt.get("bmm", 0)
        apl0 = b.cnt.get("apl", 0)

        def P1(u):
            qk, ot = u // 8, u % 8
            if u >= 2:
                b.wait_at(nc.tensor, "qbe", qbe0 + u - 1)
            for c in range(NCHUNK):
                ins = nc.tensor.matmul(ps_y[:, u % 2, :], wsb[:, qk, c, ot * 128 : (ot + 1) * 128],
                                       xn[:, c, :], start=(c == 0), stop=(c == 7),
                                       skip_group_check=True)
            b.inc(ins, "qmm", 1)

        def P2(u):
            qk, ot = u // 8, u % 8
            b.wait_at(nc.tensor, "qsq", qsq0 + u + 1)
            if u >= 2:
                b.wait_at(nc.tensor, "dvv", dvv0 + u - 1)
            nc.tensor.matmul(ps_h[0:2, u % 2, :], blk2[:], qb[:, qk, ot, :],
                             start=True, stop=True, skip_group_check=True)
            ins = nc.tensor.matmul(ps_h[64:66, u % 2, :], blk2[:], sqq[:, u % 2, :],
                                   start=True, stop=True, skip_group_check=True)
            b.inc(ins, "smm", 1)

        def P3(u):
            hr = hrows[:, u % 2, :, :]
            b.wait_at(nc.tensor, "hrm", hrm0 + u + 1)
            if u >= 2:
                b.wait_at(nc.tensor, "apl", apl0 + u - 1)
            nc.tensor.matmul(ps_b[:, 2 * (u % 2), :], hpat[:], hr[:, RSTD, :],
                             start=True, stop=True, skip_group_check=True)
            ins = nc.tensor.matmul(ps_b[:, 2 * (u % 2) + 1, :], hpat[:], hr[:, MR, :],
                                   start=True, stop=True, skip_group_check=True)
            b.inc(ins, "bmm", 1)

        def E1(u):
            qk, ot = u // 8, u % 8
            b.wait_at(nc.vector, "qmm", qmm0 + u + 1)
            ins = nc.vector.tensor_scalar_add(qb[:, qk, ot, :], ps_y[:, u % 2, :],
                                              bias[:, qk, ot : ot + 1])
            b.inc(ins, "qbe", 1)

        def E2(u):
            qk, ot = u // 8, u % 8
            if u >= 2:
                b.wait_at(nc.vector, "smm", smm0 + u - 1)
            ins = nc.vector.tensor_tensor(sqq[:, u % 2, :], qb[:, qk, ot, :],
                                          qb[:, qk, ot, :], ALU.mult)
            b.inc(ins, "qsq", 1)

        def E3(u):
            hr = hrows[:, u % 2, :, :]
            b.wait_at(nc.vector, "smm", smm0 + u + 1)
            if u >= 2:
                b.wait_at(nc.vector, "bmm", bmm0 + u - 1)
            nc.vector.tensor_scalar_mul(hr[:, MU, :], ps_h[0:2, u % 2, :], 1.0 / 64)
            nc.vector.tensor_scalar_mul(hr[:, E2i, :], ps_h[64:66, u % 2, :], 1.0 / 64)
            nc.vector.tensor_tensor(hr[:, VAR, :], hr[:, MU, :], hr[:, MU, :], ALU.mult)
            ins = nc.vector.tensor_tensor(hr[:, VAR, :], hr[:, E2i, :], hr[:, VAR, :], ALU.subtract)
            b.inc(ins, "dvv", 1)

        def A1(u):
            hr = hrows[:, u % 2, :, :]
            b.wait_at(nc.scalar, "dvv", dvv0 + u + 1)
            nc.scalar.activation(hr[:, E2i, :], hr[:, VAR, :], AF.Ln, bias=EPS_HLN)
            ins = nc.scalar.activation(hr[:, RSTD, :], hr[:, E2i, :], AF.Exp, scale=-0.5)
            b.inc(ins, "acs", 1)

        def E4(u):
            hr = hrows[:, u % 2, :, :]
            b.wait_at(nc.vector, "acs", acs0 + u + 1)
            ins = nc.vector.tensor_tensor(hr[:, MR, :], hr[:, MU, :], hr[:, RSTD, :], ALU.mult)
            b.inc(ins, "hrm", 1)

        def E5(u):
            qk, ot = u // 8, u % 8
            SC, BI = (0, 1) if qk == 0 else (2, 3)
            b.wait_at(nc.vector, "bmm", bmm0 + u + 1)
            nc.vector.tensor_tensor(tmp[:, 2, :], qb[:, qk, ot, :], ps_b[:, 2 * (u % 2), :],
                                    ALU.mult)
            nc.vector.tensor_tensor(tmp[:, 3, :], tmp[:, 2, :], ps_b[:, 2 * (u % 2) + 1, :],
                                    ALU.subtract)
            ins = nc.vector.tensor_scalar(qn2[:, qk, ot, :], tmp[:, 3, :],
                                          qkl[:, SC : SC + 1], qkl[:, BI : BI + 1],
                                          ALU.mult, ALU.add)
            b.inc(ins, "apl", 1)

        for u in range(18):
            if u < 16:
                P1(u)
                E1(u)
                E2(u)
            if 1 <= u <= 16:
                P2(u - 1)
                E3(u - 1)
                A1(u - 1)
                E4(u - 1)
            if u >= 2:
                P3(u - 2)
                E5(u - 2)

        b.wait(nc.tensor, "apl")
        for tt in range(4):
            pv = ps_y if tt % 2 == 0 else ps_b
            if tt >= 2:
                b.wait_at(nc.tensor, "vev", b.cnt["vev"] - 1)
            for half in range(2):
                for c in range(NCHUNK):
                    ins = nc.tensor.matmul(pv[:, half, :], xn[:, c, tt * 128 : (tt + 1) * 128],
                                           wsb[:, 2, c, half * SL : (half + 1) * SL],
                                           start=(c == 0), stop=(c == 7), skip_group_check=True)
            b.inc(ins, "mm", 1)
            b.wait(nc.vector, "mm")
            nc.vector.tensor_tensor(vsl[:, tt, 0:SL], pv[:, 0, :], bvb[:, 0:SL], ALU.add)
            ins = nc.vector.tensor_tensor(vsl[:, tt, SL:D], pv[:, 1, :], bvb[:, SL:D], ALU.add)
            b.inc(ins, "vev", 1)
        b.drain_inc(nc.vector, "vdr")
        b.wait(nc.gpsimd, "vdr")
        ins = nc.gpsimd.dma_start(qT[:, :, t0 : t0 + SL].rearrange("c p t -> p c t"), qn2[:, 0, :, :])
        b.inc(ins, "qst", 16)
        ins = nc.gpsimd.dma_start(kT[:, :, t0 : t0 + SL].rearrange("c p t -> p c t"), qn2[:, 1, :, :])
        b.inc(ins, "qst", 16)
        ins = nc.gpsimd.dma_start(vtok[t0 : t0 + SL, :].rearrange("(q p) d -> p q d", p=128), vsl[:])
        b.inc(ins, "qst", 16)
        b.wait(nc.gpsimd, "qst")
        b.barrier()
    b.restore(m0)


def emit_attn(b, xs, qT, kT, vtok, woT, bo8, gam8, ibt, tag=""):
    nc = b.nc
    m0 = b.mark()
    wsb = b.sb(f"p2w{tag}", [128, 8, 1024], BF16)
    wf = b.sb(f"p2wf{tag}", [128, 8, 1024], F32)
    kTs = b.sb(f"p2k{tag}", [128, 8, T], BF16)
    qTs = b.sb(f"p2q{tag}", [128, 8, SL], BF16)
    vts = b.sb(f"p2v{tag}", [128, 2, 1024], BF16)
    ssb = b.sb(f"p2s{tag}", [128, 2, SL], BF16)
    att = b.sb(f"p2a{tag}", [128, 8, SL], BF16)
    xsl = b.sb(f"p2x{tag}", [128, 8, SL], F32)
    tmp = b.sb(f"p2t{tag}", [128, SL], F32)
    bo = b.sb(f"p2bo{tag}", [128, 8], F32)
    gam = b.sb(f"p2g{tag}", [128, 8], F32)
    ibb = b.sb(f"p2ib{tag}", [128, 1], F32)
    ps_s = b.ps(f"p2pss{tag}", [128, 2, SL], F32)
    ps_o = b.ps(f"p2pso{tag}", [128, 4, SL], F32)
    ps_y = b.ps(f"p2psy{tag}", [128, 2, SL], F32)

    ins = nc.sync.dma_start(wf[:], woT.ap().rearrange("(c p) m -> p c m", p=128))
    b.inc(ins, "wld", 16)
    b.wait(nc.vector, "wld")
    ins = nc.vector.tensor_copy(wsb[:], wf[:])
    b.inc(ins, "wfree", 1)
    ins = nc.sync.dma_start(kTs[:], kT.ap().rearrange("c p t -> p c t"))
    b.inc(ins, "wld", 16)
    for src, dst in ((bo8, bo), (gam8, gam), (ibt, ibb)):
        ins = nc.sync.dma_start(dst[:], src.ap())
        b.inc(ins, "wld", 16)
    b.barrier()

    for s in range(NSL):
        t0 = s * SL
        ins = nc.sync.dma_start(qTs[:], qT[:, :, t0 : t0 + SL].rearrange("c p t -> p c t"))
        b.inc(ins, "xld", 16)
        ins = nc.sync.dma_start(xsl[:], xs[:, :, t0 : t0 + SL].rearrange("c p t -> p c t"))
        b.inc(ins, "xld", 16)
        b.wait(nc.tensor, "xld")
        for hp in range(2):
            if b.cnt.get("aev", 0) > 0:
                b.wait(nc.tensor, "aev")
            vld_at = {}
            for j in range(T // 128):
                par = j % 2
                if j == 0:
                    b.wait(nc.sync, "ov")
                    ins = nc.sync.dma_start(vts[:, 0, :], vtok[0:128, :])
                    b.inc(ins, "vld", 16)
                    vld_at[0] = b.cnt["vld"]
                if j + 1 < T // 128:
                    if j >= 1:
                        b.wait(nc.sync, "ov")
                    ins = nc.sync.dma_start(vts[:, (j + 1) % 2, :], vtok[(j + 1) * 128 : (j + 2) * 128, :])
                    b.inc(ins, "vld", 16)
                    vld_at[j + 1] = b.cnt["vld"]
                b.wait_at(nc.tensor, "vld", vld_at[j])
                sg0 = b.cnt.get("sg", 0)
                ov0 = b.cnt.get("ov", 0)
                for h8 in range(8):
                    h = hp * 8 + h8
                    hc, hr = h // 2, (h % 2) * 64
                    spar = h8 % 2
                    if sg0 + h8 - 1 > 0:
                        b.wait_at(nc.tensor, "sg", sg0 + h8 - 1)
                    ins = nc.tensor.matmul(ps_s[:, spar, :],
                                           kTs[hr : hr + 64, hc, j * 128 : (j + 1) * 128],
                                           qTs[hr : hr + 64, hc, :], start=True, stop=True,
                                           skip_group_check=True)
                    b.inc(ins, "sm", 1)
                    b.wait(nc.scalar, "sm")
                    if ov0 + h8 - 1 > 0:
                        b.wait_at(nc.scalar, "ov", ov0 + h8 - 1)
                    ins = nc.scalar.activation(ssb[:, spar, :], ps_s[:, spar, :], AF.Sigmoid,
                                               bias=ibb[:, 0:1])
                    b.inc(ins, "sg", 1)
                    b.wait(nc.tensor, "sg")
                    ins = nc.tensor.matmul(ps_o[(h8 % 2) * 64 : (h8 % 2) * 64 + 64, h8 // 2, :],
                                           vts[:, par, h * 64 : (h + 1) * 64],
                                           ssb[:, spar, :], start=(j == 0), stop=(j == T // 128 - 1),
                                           skip_group_check=True)
                    b.inc(ins, "ov", 1)
            b.wait(nc.vector, "ov")
            for h8 in range(8):
                h = hp * 8 + h8
                hc, hr = h // 2, (h % 2) * 64
                ins = nc.vector.tensor_copy(att[hr : hr + 64, hc, :],
                                            ps_o[(h8 % 2) * 64 : (h8 % 2) * 64 + 64, h8 // 2, :])
                b.inc(ins, "aev", 1)
        b.wait(nc.tensor, "aev")
        for ot in range(NCHUNK):
            par = ot % 2
            if b.cnt.get("pev", 0) >= 2:
                b.wait_at(nc.tensor, "pev", b.cnt["pev"] - 2)
            for c in range(NCHUNK):
                ins = nc.tensor.matmul(ps_y[:, par, :], wsb[:, c, ot * 128 : (ot + 1) * 128],
                                       att[:, c, :], start=(c == 0), stop=(c == 7))
            b.inc(ins, "mm", 1)
            b.wait(nc.vector, "mm")
            nc.vector.tensor_scalar(tmp[:], ps_y[:, par, :], bo[:, ot : ot + 1],
                                    gam[:, ot : ot + 1], ALU.add, ALU.mult)
            ins = nc.vector.tensor_tensor(xsl[:, ot, :], tmp[:], xsl[:, ot, :], ALU.add)
            b.inc(ins, "pev", 2)
        b.drain_inc(nc.vector, "vdr")
        b.wait(nc.gpsimd, "vdr")
        ins = nc.gpsimd.dma_start(xs[:, :, t0 : t0 + SL].rearrange("c p t -> p c t"), xsl[:])
        b.inc(ins, "xst", 16)
        b.wait(nc.gpsimd, "xst")
        b.barrier()
    b.restore(m0)


def build_p2():
    nc = bass.Bass("TRN2", target_bir_lowering=False, debug=False)
    xs = nc.dram_tensor("xs", [NCHUNK, 128, T], F32, kind="ExternalInput")
    xso = nc.dram_tensor("xso", [NCHUNK, 128, T], F32, kind="ExternalOutput")
    qT = nc.dram_tensor("qT", [NCHUNK, 128, T], BF16, kind="ExternalInput")
    kT = nc.dram_tensor("kT", [NCHUNK, 128, T], BF16, kind="ExternalInput")
    vtok = nc.dram_tensor("vtok", [T, D], BF16, kind="ExternalInput")
    woT = nc.dram_tensor("woT", [D, D], F32, kind="ExternalInput")
    bo8 = nc.dram_tensor("bo8", [128, NCHUNK], F32, kind="ExternalInput")
    gam8 = nc.dram_tensor("gam8", [128, NCHUNK], F32, kind="ExternalInput")
    ibt = nc.dram_tensor("ibt", [128, 1], F32, kind="ExternalInput")
    b = Bld(nc)
    b.setup_consts([EPS_LN, EPS_HLN, -0.5])
    emit_dram_copy(b, xso, xs)
    emit_attn(b, xso, qT, kT, vtok, woT, bo8, gam8, ibt)
    return nc


def _p2_inputs(inputs, x_stage, qn, kn, v):
    def tile8(vv):
        return np.ascontiguousarray(vv.reshape(NCHUNK, 128).T)
    common = {
        "woT": np.ascontiguousarray(inputs["L0_attn_Wo"]),
        "bo8": tile8(inputs["L0_attn_bo"]),
        "gam8": tile8(inputs["L0_attn_gamma"]),
        "ibt": np.ascontiguousarray(np.tile(inputs["L0_attn_init_bias"].reshape(1, 1), (128, 1)).astype(np.float32)),
    }
    maps = []
    for bc in range(8):
        d = dict(common)
        d["xs"] = np.ascontiguousarray(x_stage[bc].T).reshape(NCHUNK, 128, T)
        d["qT"] = np.ascontiguousarray(qn[bc].T.astype(BF)).reshape(NCHUNK, 128, T)
        d["kT"] = np.ascontiguousarray(kn[bc].T.astype(BF)).reshape(NCHUNK, 128, T)
        d["vtok"] = np.ascontiguousarray(v[bc].astype(BF))
        maps.append(d)
    return maps



def _ln_to_xn(b, xsl, xn, xb, sqb, rows, onesb, ones1, ps_s, ps_q, ps_b, tmpc):
    nc = b.nc
    MU, E2, VAR, RSTD, MR = 0, 1, 2, 3, 4
    nc.vector.tensor_copy(xb[:], xsl[:])
    ins = nc.vector.tensor_tensor(sqb[:], xb[:], xb[:], ALU.mult)
    b.inc(ins, "dv", 1)
    b.wait(nc.tensor, "dv")
    for c in range(NCHUNK):
        nc.tensor.matmul(ps_s[:], onesb[:], xb[:, c, :], start=(c == 0), stop=(c == 7))
    for c in range(NCHUNK):
        ins = nc.tensor.matmul(ps_q[:], onesb[:], sqb[:, c, :], start=(c == 0), stop=(c == 7))
    b.inc(ins, "mm", 1)
    b.wait(nc.vector, "mm")
    nc.vector.tensor_scalar_mul(rows[:, MU, :], ps_s[:], 1.0 / D)
    nc.vector.tensor_scalar_mul(rows[:, E2, :], ps_q[:], 1.0 / D)
    nc.vector.tensor_tensor(rows[:, VAR, :], rows[:, MU, :], rows[:, MU, :], ALU.mult)
    ins = nc.vector.tensor_tensor(rows[:, VAR, :], rows[:, E2, :], rows[:, VAR, :], ALU.subtract)
    b.inc(ins, "dv", 1)
    b.wait(nc.scalar, "dv")
    nc.scalar.activation(rows[:, E2, :], rows[:, VAR, :], AF.Ln, bias=EPS_LN)
    ins = nc.scalar.activation(rows[:, RSTD, :], rows[:, E2, :], AF.Exp, scale=-0.5)
    b.inc(ins, "ac", 1)
    b.wait(nc.vector, "ac")
    ins = nc.vector.tensor_tensor(rows[:, MR, :], rows[:, MU, :], rows[:, RSTD, :], ALU.mult)
    b.inc(ins, "dv", 1)
    b.wait(nc.tensor, "dv")
    nc.tensor.matmul(ps_b[:, 0, :], ones1[:], rows[:, RSTD, :], start=True, stop=True)
    ins = nc.tensor.matmul(ps_b[:, 1, :], ones1[:], rows[:, MR, :], start=True, stop=True)
    b.inc(ins, "mm", 1)
    b.wait(nc.vector, "mm")
    for c in range(NCHUNK):
        nc.vector.tensor_tensor(tmpc[:], xsl[:, c, :], ps_b[:, 0, :], ALU.mult)
        ins = nc.vector.tensor_tensor(xn[:, c, :], tmpc[:], ps_b[:, 1, :], ALU.subtract)
    b.inc(ins, "dv", 1)


def emit_wcast(b, pairs, tag=""):
    nc = b.nc
    m0 = b.mark()
    wf = b.sb(f"wcf{tag}", [128, 2, 4096], F32)
    wb = b.sb(f"wcb{tag}", [128, 2, 4096], BF16)
    for src, dst, Kd, Md in pairs:
        for kc in range(Kd // 128):
            par = kc % 2
            if b.cnt.get("wcv", 0) >= 2:
                b.wait_at(nc.scalar, "wcv", b.cnt["wcv"] - 1)
            ins = nc.scalar.dma_start(wf[:, par, 0:Md], src[kc * 128 : (kc + 1) * 128, :])
            b.inc(ins, "wcl", 16)
            b.wait(nc.vector, "wcl")
            if b.cnt.get("wcs", 0) >= 2:
                b.wait_at(nc.vector, "wcs", b.cnt["wcs"] - 16)
            ins = nc.vector.tensor_copy(wb[:, par, 0:Md], wf[:, par, 0:Md])
            b.inc(ins, "wcv", 1)
            b.drain_inc(nc.vector, "wcd")
            b.wait(nc.gpsimd, "wcd")
            ins = nc.gpsimd.dma_start(
                dst[:, :, kc * 128 : (kc + 1) * 128].rearrange("o p m -> p o m"),
                wb[:, par, 0:Md].rearrange("p (o m) -> p o m", m=128))
            b.inc(ins, "wcs", 16)
    b.restore(m0)


def emit_cast_flat(b, pairs, tag=""):
    nc = b.nc
    m0 = b.mark()
    wf = b.sb(f"cff{tag}", [128, 2, 4096], F32)
    wb = b.sb(f"cfb{tag}", [128, 2, 4096], BF16)
    for src, dst, Kd, Md in pairs:
        for kc in range(Kd // 128):
            par = kc % 2
            if b.cnt.get("wcv", 0) >= 2:
                b.wait_at(nc.scalar, "wcv", b.cnt["wcv"] - 1)
            ins = nc.scalar.dma_start(wf[:, par, 0:Md], src[kc * 128 : (kc + 1) * 128, :])
            b.inc(ins, "wcl", 16)
            b.wait(nc.vector, "wcl")
            if b.cnt.get("wcs", 0) >= 2:
                b.wait_at(nc.vector, "wcs", b.cnt["wcs"] - 16)
            ins = nc.vector.tensor_copy(wb[:, par, 0:Md], wf[:, par, 0:Md])
            b.inc(ins, "wcv", 1)
            b.drain_inc(nc.vector, "wcd")
            b.wait(nc.gpsimd, "wcd")
            ins = nc.gpsimd.dma_start(dst[kc * 128 : (kc + 1) * 128, :], wb[:, par, 0:Md])
            b.inc(ins, "wcs", 16)
    b.restore(m0)


def emit_casts(b, bufs, flat_pairs, blk_pairs, tag=""):
    nc = b.nc
    wfA, wbA, wfB, wbB = bufs
    MT = 1024
    jobs = []
    for src, dst, Kd, Md in flat_pairs:
        for kc in range(Kd // 128):
            for mc in range(Md // MT):
                jobs.append((src, dst, kc, mc, False))
    for src, dst, Kd, Md in blk_pairs:
        for kc in range(Kd // 128):
            for mc in range(Md // MT):
                jobs.append((src, dst, kc, mc, True))
    for gi, (src, dst, kc, mc, blk) in enumerate(jobs):
        lane = 0
        i = b.cnt.get("wcv", 0) if lane == 0 else b.cnt.get("w2l", 0) // 16
        par = i % 2
        sl = src[kc * 128 : (kc + 1) * 128, mc * MT : (mc + 1) * MT]
        if blk:
            dsl = dst[mc * (MT // 128) : (mc + 1) * (MT // 128), :,
                      kc * 128 : (kc + 1) * 128].rearrange("o p m -> p o m")
            wv = None
        else:
            dsl = dst[kc * 128 : (kc + 1) * 128, mc * MT : (mc + 1) * MT]
        if lane == 0:
            if i >= 2:
                b.wait_at(nc.scalar, "wcv", b.cnt["wcv"] - 1)
            ins = nc.scalar.dma_start(wfA[:, par, :], sl)
            b.inc(ins, "wcl", 16)
            b.wait(nc.vector, "wcl")
            if i >= 2:
                b.wait_at(nc.vector, "wcs", b.cnt["wcs"] - 16)
            ins = nc.vector.tensor_copy(wbA[:, par, :], wfA[:, par, :])
            b.inc(ins, "wcv", 1)
            b.drain_inc(nc.vector, "wcd")
            b.wait(nc.gpsimd, "wcd")
            ins = nc.gpsimd.dma_start(dsl, wbA[:, par, :].rearrange("p (o m) -> p o m", m=128)
                                      if blk else wbA[:, par, :])
            b.inc(ins, "wcs", 16)
        else:
            if i >= 2:
                b.wait_at(nc.gpsimd, "w2s", b.cnt["w2s"] - 16)
            ins = nc.gpsimd.dma_start(wfB[:, par, :], sl)
            b.inc(ins, "w2l", 16)
            b.wait(nc.gpsimd, "w2l")
            ins = nc.gpsimd.tensor_copy(wbB[:, par, :], wfB[:, par, :])
            b.drain_inc(nc.gpsimd, "w2d")
            ins = nc.gpsimd.dma_start(dsl, wbB[:, par, :].rearrange("p (o m) -> p o m", m=128)
                                      if blk else wbB[:, par, :])
            b.inc(ins, "w2s", 16)


def emit_attn2(b, xs, qT, kT, vtok, woR, bo8, gam8, ibt, tag=""):
    nc = b.nc
    m0 = b.mark()
    NJ = T // 128
    kTs = b.sb(f"a2k{tag}", [128, 8, T], BF16)
    vres = b.sb(f"a2v{tag}", [128, NJ, 1024], BF16)
    wsb = b.sb(f"a2w{tag}", [128, 8, 1024], BF16)
    qTs = b.sb(f"a2q{tag}", [128, 8, SL], BF16)
    ssb = b.sb(f"a2s{tag}", [128, 2, 1024], BF16)
    att = b.sb(f"a2a{tag}", [128, 8, SL], BF16)
    xsl = b.sb(f"a2x{tag}", [128, 8, SL], F32)
    tmp = b.sb(f"a2t{tag}", [128, SL], F32)
    bo = b.sb(f"a2bo{tag}", [128, 8], F32)
    gam = b.sb(f"a2g{tag}", [128, 8], F32)
    ibb = b.sb(f"a2ib{tag}", [128, 1], F32)
    ps_s = b.ps(f"a2pss{tag}", [128, 4, SL], F32)
    ps_o = b.ps(f"a2pso{tag}", [128, 2, SL], F32)
    ps_y = b.ps(f"a2psy{tag}", [128, 2, SL], F32)

    if getattr(b, "wcs_flat", 0) > 0:
        b.wait_at(nc.sync, "wcs", b.wcs_flat)
    if getattr(b, "w2s_flat", 0) > 0:
        b.wait_at(nc.sync, "w2s", b.w2s_flat)
    ins = nc.sync.dma_start(kTs[:], kT.ap().rearrange("c p t -> p c t"))
    b.inc(ins, "wld", 16)
    for j0 in range(0, NJ, 8):
        ins = nc.sync.dma_start(
            vres[:, j0 : j0 + 8, :],
            vtok[j0 * 128 : (j0 + 8) * 128, :].rearrange("(j p) d -> p j d", p=128))
        b.inc(ins, "wld", 16)
    ins = nc.sync.dma_start(wsb[:], woR.ap().rearrange("(c p) m -> p c m", p=128))
    b.inc(ins, "wld", 16)
    for src, dst in ((bo8, bo), (gam8, gam), (ibt, ibb)):
        ins = nc.sync.dma_start(dst[:], src.ap())
        b.inc(ins, "wld", 16)
    b.barrier()

    U = 8 * NJ
    for s in range(NSL):
        t0 = s * SL
        ins = nc.sync.dma_start(qTs[:], qT[:, :, t0 : t0 + SL].rearrange("c p t -> p c t"))
        b.inc(ins, "xld", 16)
        ins = nc.sync.dma_start(xsl[:], xs[:, :, t0 : t0 + SL].rearrange("c p t -> p c t"))
        b.inc(ins, "xld", 16)
        amm0 = b.cnt.get("amm", 0)
        asg0 = b.cnt.get("asg", 0)
        aov0 = b.cnt.get("aov", 0)
        aevc0 = b.cnt.get("aevc", 0)
        b.wait(nc.tensor, "xld")

        def logits(u):
            p, j = u // NJ, u % NJ
            sp = u % 2
            if u >= 2:
                b.wait_at(nc.tensor, "asg", asg0 + u - 1)
            nc.tensor.matmul(ps_s[:, 2 * sp, :], kTs[0:64, p, j * 128 : (j + 1) * 128],
                             qTs[0:64, p, :], start=True, stop=True, skip_group_check=True)
            ins = nc.tensor.matmul(ps_s[:, 2 * sp + 1, :], kTs[64:128, p, j * 128 : (j + 1) * 128],
                                   qTs[64:128, p, :], start=True, stop=True, skip_group_check=True)
            b.inc(ins, "amm", 1)

        def attv(u):
            p, j = u // NJ, u % NJ
            sp = u % 2
            po = p % 2
            b.wait_at(nc.tensor, "asg", asg0 + u + 1)
            if j == 0 and p >= 2:
                b.wait_at(nc.tensor, "aevc", aevc0 + p - 1)
            nc.tensor.matmul(ps_o[0:64, po, :], vres[:, j, p * 128 : p * 128 + 64],
                             ssb[:, sp, 0:SL], start=(j == 0), stop=(j == NJ - 1),
                             skip_group_check=True)
            ins = nc.tensor.matmul(ps_o[64:128, po, :], vres[:, j, p * 128 + 64 : p * 128 + 128],
                                   ssb[:, sp, SL : 2 * SL], start=(j == 0), stop=(j == NJ - 1),
                                   skip_group_check=True)
            b.inc(ins, "aov", 1)
            if j == NJ - 1:
                b.wait(nc.vector, "aov")
                ins = nc.vector.tensor_copy(att[:, p, :], ps_o[:, po, :])
                b.inc(ins, "aevc", 1)

        def sig(u):
            sp = u % 2
            b.wait_at(nc.scalar, "amm", amm0 + u + 1)
            if u >= 2:
                b.wait_at(nc.scalar, "aov", aov0 + u - 1)
            ins = nc.scalar.activation(ssb[:, sp, :],
                                       ps_s[:, 2 * sp : 2 * sp + 2, :].rearrange("p a t -> p (a t)"),
                                       AF.Sigmoid, bias=ibb[:, 0:1])
            b.inc(ins, "asg", 1)

        logits(0)
        sig(0)
        for u in range(1, U):
            logits(u)
            sig(u)
            attv(u - 1)
        attv(U - 1)

        b.wait(nc.tensor, "aevc")
        for ot in range(NCHUNK):
            par = ot % 2
            if b.cnt.get("pev", 0) >= 2:
                b.wait_at(nc.tensor, "pev", b.cnt["pev"] - 2)
            for c in range(NCHUNK):
                ins = nc.tensor.matmul(ps_y[:, par, :], wsb[:, c, ot * 128 : (ot + 1) * 128],
                                       att[:, c, :], start=(c == 0), stop=(c == 7))
            b.inc(ins, "mm", 1)
            b.wait(nc.vector, "mm")
            nc.vector.tensor_scalar(tmp[:], ps_y[:, par, :], bo[:, ot : ot + 1],
                                    gam[:, ot : ot + 1], ALU.add, ALU.mult)
            ins = nc.vector.tensor_tensor(xsl[:, ot, :], tmp[:], xsl[:, ot, :], ALU.add)
            b.inc(ins, "pev", 2)
        b.drain_inc(nc.vector, "vdr")
        b.wait(nc.gpsimd, "vdr")
        ins = nc.gpsimd.dma_start(xs[:, :, t0 : t0 + SL].rearrange("c p t -> p c t"), xsl[:])
        b.inc(ins, "xst", 16)
        b.wait(nc.gpsimd, "xst")
        b.barrier()
    b.restore(m0)


def emit_l1ffn(b, xs, w1R, w2R, h1R, h2R, w3R, scl, tag=""):
    nc = b.nc
    m0 = b.mark()
    NH = DHID // 128
    xsl = b.sb(f"p4x{tag}", [128, 2, 8, SL], F32)
    xb = b.sb(f"p4xb{tag}", [128, 8, SL], BF16)
    sqb = b.sb(f"p4sq{tag}", [128, 8, SL], BF16)
    xn = b.sb(f"p4xn{tag}", [128, 2, 8, SL], BF16)
    hb = b.sb(f"p4h{tag}", [128, NH, SL], BF16)
    hb2 = b.sb(f"p4h2{tag}", [128, NH, SL], BF16)
    sil = b.sb(f"p4sil{tag}", [128, 2, SL], F32)
    tmp = b.sb(f"p4t{tag}", [128, 2, SL], F32)
    rows = b.sb(f"p4rows{tag}", [1, 8, SL], F32)
    onesb = b.sb(f"p4ones{tag}", [128, 1], BF16)
    ones1 = b.sb(f"p4one1{tag}", [1, 128], F32)
    sc = b.sb(f"p4sc{tag}", [128, 5, NH], F32)
    w12 = b.sb(f"p4w12{tag}", [128, 2, 2, 8, 128], BF16)
    wh = b.sb(f"p4wh{tag}", [128, 2, 32, 128], BF16)
    ps_s = b.ps(f"p4pss{tag}", [1, SL], F32)
    ps_q = b.ps(f"p4psq{tag}", [1, SL], F32)
    ps_b = b.ps(f"p4psb{tag}", [128, 2, SL], F32)
    ps_g = b.ps(f"p4psg{tag}", [128, 2, SL], F32)
    ps_h = b.ps(f"p4psh{tag}", [128, 2, SL], F32)

    if b.cnt.get("wcs", 0) > 0:
        b.wait(nc.sync, "wcs")
    if b.cnt.get("w2s", 0) > 0:
        b.wait(nc.sync, "w2s")
    ins = nc.sync.dma_start(sc[:], scl.ap())
    b.inc(ins, "wld", 16)
    b.wait(nc.vector, "wld")
    nc.vector.memset(onesb[:], 1.0)
    nc.vector.memset(ones1[:], 1.0)
    b.barrier()

    xst0 = b.cnt.get("xst", 0)
    hevA_prev = None
    for s in range(NSL):
        t0 = s * SL
        sp = s % 2
        xsl_s = xsl[:, sp]
        xn_s = xn[:, sp]
        if s >= 2:
            b.wait_at(nc.sync, "xst", xst0 + 16 * (s - 1))
        ins = nc.sync.dma_start(xsl_s[:], xs[:, :, t0 : t0 + SL].rearrange("c p t -> p c t"))
        b.inc(ins, "xld", 16)
        b.wait(nc.vector, "xld")
        _ln_to_xn(b, xsl_s, xn_s, xb, sqb, rows, onesb, ones1, ps_s, ps_q, ps_b, tmp[:, 0, :])
        b.wait(nc.tensor, "dv")
        if b.cnt.get("dev", 0) > 0:
            b.wait(nc.tensor, "dev")
        wat = {}
        for ot in range(NH):
            par = ot % 2
            if ot == 0:
                if hevA_prev is not None:
                    b.wait_at(nc.sync, "hev", hevA_prev)
                ins = nc.sync.dma_start(w12[:, 0, 0, :, :], w1R[0, :, :])
                b.inc(ins, "wld", 16)
                ins = nc.sync.dma_start(w12[:, 0, 1, :, :], w2R[0, :, :])
                b.inc(ins, "wld", 16)
                wat[0] = b.cnt["wld"]
            if ot + 1 < NH:
                if ot >= 1:
                    b.wait_at(nc.sync, "hev", b.cnt["hev"])
                ins = nc.sync.dma_start(w12[:, (ot + 1) % 2, 0, :, :], w1R[ot + 1, :, :])
                b.inc(ins, "wld", 16)
                ins = nc.sync.dma_start(w12[:, (ot + 1) % 2, 1, :, :], w2R[ot + 1, :, :])
                b.inc(ins, "wld", 16)
                wat[ot + 1] = b.cnt["wld"]
            b.wait_at(nc.tensor, "wld", wat[ot])
            if b.cnt.get("hev", 0) >= 2:
                b.wait_at(nc.tensor, "hev", b.cnt["hev"] - 1)
            for c in range(NCHUNK):
                nc.tensor.matmul(ps_g[:, par, :], w12[:, par, 0, c, :], xn_s[:, c, :],
                                 start=(c == 0), stop=(c == 7), skip_group_check=True)
            for c in range(NCHUNK):
                ins = nc.tensor.matmul(ps_h[:, par, :], w12[:, par, 1, c, :], xn_s[:, c, :],
                                       start=(c == 0), stop=(c == 7), skip_group_check=True)
            b.inc(ins, "mm", 1)
            b.wait(nc.scalar, "mm")
            ins = nc.scalar.activation(sil[:, par, :], ps_g[:, par, :], AF.Silu,
                                       scale=sc[:, 0, ot : ot + 1])
            b.inc(ins, "ac", 1)
            b.wait(nc.vector, "ac")
            ins = nc.vector.scalar_tensor_tensor(hb[:, ot, :], ps_h[:, par, :],
                                                 sc[:, 1, ot : ot + 1], sil[:, par, :],
                                                 ALU.mult, ALU.mult)
            b.inc(ins, "hev", 1)
        hevA_prev = b.cnt["hev"]
        for stage, (wR, hsrc, hdst, sci) in enumerate(
                ((h1R, hb, hb2, 2), (h2R, hb2, hb, 3))):
            b.wait(nc.tensor, "hev")
            wat = {}
            for ot in range(NH):
                par = ot % 2
                if ot == 0:
                    if stage == 0:
                        if b.cnt.get("dev", 0) > 0:
                            b.wait(nc.sync, "dev")
                    else:
                        b.wait(nc.sync, "hev")
                    ins = nc.sync.dma_start(wh[:, 0, :, :], wR[0, :, :])
                    b.inc(ins, "wld", 16)
                    wat[0] = b.cnt["wld"]
                if ot + 1 < NH:
                    if ot >= 1:
                        b.wait_at(nc.sync, "hev", b.cnt["hev"])
                    ins = nc.sync.dma_start(wh[:, (ot + 1) % 2, :, :], wR[ot + 1, :, :])
                    b.inc(ins, "wld", 16)
                    wat[ot + 1] = b.cnt["wld"]
                b.wait_at(nc.tensor, "wld", wat[ot])
                if b.cnt.get("hev", 0) >= 2:
                    b.wait_at(nc.tensor, "hev", b.cnt["hev"] - 1)
                for kc in range(NH):
                    ins = nc.tensor.matmul(ps_g[:, par, :], wh[:, par, kc, :], hsrc[:, kc, :],
                                           start=(kc == 0), stop=(kc == NH - 1),
                                           skip_group_check=True)
                b.inc(ins, "mm", 1)
                b.wait(nc.scalar, "mm")
                ins = nc.scalar.activation(hdst[:, ot, :], ps_g[:, par, :], AF.Gelu,
                                           scale=sc[:, sci, ot : ot + 1])
                b.inc(ins, "hev", 1)
        b.wait(nc.tensor, "hev")
        wat = {}
        for ot in range(NCHUNK):
            par = ot % 2
            if ot == 0:
                b.wait(nc.sync, "dev")
                b.wait(nc.sync, "hev")
                ins = nc.sync.dma_start(wh[:, 0, :, :], w3R[0, :, :])
                b.inc(ins, "wld", 16)
                wat[0] = b.cnt["wld"]
            if ot + 1 < NCHUNK:
                if ot >= 1:
                    b.wait_at(nc.sync, "dev", b.cnt["dev"])
                ins = nc.sync.dma_start(wh[:, (ot + 1) % 2, :, :], w3R[ot + 1, :, :])
                b.inc(ins, "wld", 16)
                wat[ot + 1] = b.cnt["wld"]
            b.wait_at(nc.tensor, "wld", wat[ot])
            if b.cnt.get("dev", 0) >= 2:
                b.wait_at(nc.tensor, "dev", b.cnt["dev"] - 1)
            for kc in range(NH):
                ins = nc.tensor.matmul(ps_g[:, par, :], wh[:, par, kc, :], hb[:, kc, :],
                                       start=(kc == 0), stop=(kc == NH - 1),
                                       skip_group_check=True)
            b.inc(ins, "mm", 1)
            b.wait(nc.vector, "mm")
            nc.vector.tensor_scalar(tmp[:, 1, :], ps_g[:, par, :], sc[:, 4, ot : ot + 1],
                                    0.5, ALU.mult, ALU.mult)
            ins = nc.vector.tensor_tensor(xsl_s[:, ot, :], tmp[:, 1, :], xsl_s[:, ot, :], ALU.add)
            b.inc(ins, "dev", 1)
        b.drain_inc(nc.vector, "vdr")
        b.wait(nc.gpsimd, "vdr")
        ins = nc.gpsimd.dma_start(xs[:, :, t0 : t0 + SL].rearrange("c p t -> p c t"), xsl_s[:])
        b.inc(ins, "xst", 16)
    b.wait(nc.gpsimd, "xst")
    b.barrier()
    b.restore(m0)


def build_wprep():
    nc = bass.Bass("TRN2", target_bir_lowering=False, debug=False)
    w1T = nc.dram_tensor("w1T", [D, DHID], F32, kind="ExternalInput")
    w2T = nc.dram_tensor("w2T", [D, DHID], F32, kind="ExternalInput")
    h1T = nc.dram_tensor("h1T", [DHID, DHID], F32, kind="ExternalInput")
    h2T = nc.dram_tensor("h2T", [DHID, DHID], F32, kind="ExternalInput")
    w3T = nc.dram_tensor("w3T", [DHID, D], F32, kind="ExternalInput")
    w1R = nc.dram_tensor("w1R", [32, 128, D], BF16, kind="ExternalOutput")
    w2R = nc.dram_tensor("w2R", [32, 128, D], BF16, kind="ExternalOutput")
    h1R = nc.dram_tensor("h1R", [32, 128, DHID], BF16, kind="ExternalOutput")
    h2R = nc.dram_tensor("h2R", [32, 128, DHID], BF16, kind="ExternalOutput")
    w3R = nc.dram_tensor("w3R", [8, 128, DHID], BF16, kind="ExternalOutput")
    b = Bld(nc)
    emit_wcast(b, [(w1T, w1R, D, DHID), (w2T, w2R, D, DHID), (h1T, h1R, DHID, DHID),
                   (h2T, h2R, DHID, DHID), (w3T, w3R, DHID, D)])
    return nc


def build_p4():
    nc = bass.Bass("TRN2", target_bir_lowering=False, debug=False)
    xs = nc.dram_tensor("xs", [NCHUNK, 128, T], F32, kind="ExternalInput")
    xso = nc.dram_tensor("xso", [NCHUNK, 128, T], F32, kind="ExternalOutput")
    w1R = nc.dram_tensor("w1R", [32, 128, D], BF16, kind="ExternalInput")
    w2R = nc.dram_tensor("w2R", [32, 128, D], BF16, kind="ExternalInput")
    h1R = nc.dram_tensor("h1R", [32, 128, DHID], BF16, kind="ExternalInput")
    h2R = nc.dram_tensor("h2R", [32, 128, DHID], BF16, kind="ExternalInput")
    w3R = nc.dram_tensor("w3R", [8, 128, DHID], BF16, kind="ExternalInput")
    scl = nc.dram_tensor("scl", [128, 5, 32], F32, kind="ExternalInput")
    b = Bld(nc)
    b.setup_consts([EPS_LN, EPS_HLN, -0.5])
    emit_dram_copy(b, xso, xs)
    emit_l1ffn(b, xso, w1R, w2R, h1R, h2R, w3R, scl)
    return nc


def _wprep_inputs(inputs):
    return {
        "w1T": np.ascontiguousarray(inputs["L1_ffn_w1_w"].T),
        "w2T": np.ascontiguousarray(inputs["L1_ffn_w2_w"].T),
        "h1T": np.ascontiguousarray(inputs["L1_ffn_h1_w"].T),
        "h2T": np.ascontiguousarray(inputs["L1_ffn_h2_w"].T),
        "w3T": np.ascontiguousarray(inputs["L1_ffn_w3_w"].T),
    }


def _p4_scl(inputs):
    scl = np.zeros((128, 5, 32), np.float32)
    scl[:, 0, :] = inputs["L1_ffn_w1_s"].reshape(32, 128).T
    scl[:, 1, :] = inputs["L1_ffn_w2_s"].reshape(32, 128).T
    scl[:, 2, :] = inputs["L1_ffn_h1_s"].reshape(32, 128).T
    scl[:, 3, :] = inputs["L1_ffn_h2_s"].reshape(32, 128).T
    scl[:, 4, 0:8] = inputs["L1_ffn_w3_s"].reshape(8, 128).T
    return np.ascontiguousarray(scl)



def emit_bu(b, xs, bT, BuT, tag=""):
    nc = b.nc
    m0 = b.mark()
    wsb = b.sb(f"p5w{tag}", [128, 8, 1024], BF16)
    wf = b.sb(f"p5wf{tag}", [128, 8, 1024], F32)
    xsl = b.sb(f"p5x{tag}", [128, 8, SL], F32)
    xb = b.sb(f"p5xb{tag}", [128, 8, SL], BF16)
    sqb = b.sb(f"p5sq{tag}", [128, 8, SL], BF16)
    xn = b.sb(f"p5xn{tag}", [128, 8, SL], BF16)
    bub = b.sb(f"p5bu{tag}", [128, 8, SL], BF16)
    tmp = b.sb(f"p5t{tag}", [128, SL], F32)
    rows = b.sb(f"p5rows{tag}", [1, 8, SL], F32)
    onesb = b.sb(f"p5ones{tag}", [128, 1], BF16)
    ones1 = b.sb(f"p5one1{tag}", [1, 128], F32)
    ps_s = b.ps(f"p5pss{tag}", [1, SL], F32)
    ps_q = b.ps(f"p5psq{tag}", [1, SL], F32)
    ps_b = b.ps(f"p5psb{tag}", [128, 2, SL], F32)
    ps_y = b.ps(f"p5psy{tag}", [128, 2, SL], F32)

    ins = nc.sync.dma_start(wf[:], bT.ap().rearrange("(c p) m -> p c m", p=128))
    b.inc(ins, "wld", 16)
    b.wait(nc.vector, "wld")
    nc.vector.tensor_copy(wsb[:], wf[:])
    nc.vector.memset(onesb[:], 1.0)
    nc.vector.memset(ones1[:], 1.0)
    b.barrier()
    for s in range(NSL):
        t0 = s * SL
        ins = nc.sync.dma_start(xsl[:], xs[:, :, t0 : t0 + SL].rearrange("c p t -> p c t"))
        b.inc(ins, "xld", 16)
        b.wait(nc.vector, "xld")
        _ln_to_xn(b, xsl, xn, xb, sqb, rows, onesb, ones1, ps_s, ps_q, ps_b, tmp)
        b.wait(nc.tensor, "dv")
        for ot in range(NCHUNK):
            par = ot % 2
            if b.cnt.get("bev", 0) >= 2:
                b.wait_at(nc.tensor, "bev", b.cnt["bev"] - 1)
            for c in range(NCHUNK):
                ins = nc.tensor.matmul(ps_y[:, par, :], wsb[:, c, ot * 128 : (ot + 1) * 128],
                                       xn[:, c, :], start=(c == 0), stop=(c == 7))
            b.inc(ins, "mm", 1)
            b.wait(nc.vector, "mm")
            ins = nc.vector.tensor_copy(bub[:, ot, :], ps_y[:, par, :])
            b.inc(ins, "bev", 1)
        b.drain_inc(nc.vector, "vdr")
        b.wait(nc.gpsimd, "vdr")
        ins = nc.gpsimd.dma_start(BuT[:, :, t0 : t0 + SL].rearrange("c p t -> p c t"), bub[:])
        b.inc(ins, "bst", 16)
        b.wait(nc.gpsimd, "bst")
        b.barrier()
    b.restore(m0)


def emit_rnn(b, BuT, aT, idn, hT, tag=""):
    nc = b.nc
    m0 = b.mark()
    asb = b.sb(f"p6a{tag}", [128, 8, 1024], BF16)
    af = b.sb(f"p6af{tag}", [128, 8, 1024], F32)
    ids = b.sb(f"p6id{tag}", [128, 128], BF16)
    bub = b.sb(f"p6bu{tag}", [128, 8, T + 16], BF16)
    ht = b.sb(f"p6ht{tag}", [128, 2, 8], BF16)
    hst = b.sb(f"p6h{tag}", [128, 8, T], BF16)
    ps_a = b.ps(f"p6psa{tag}", [128, 8], F32)
    ps_b2 = b.ps(f"p6psb{tag}", [128, 8], F32)

    ins = nc.sync.dma_start(af[:], aT.ap().rearrange("(c p) m -> p c m", p=128))
    b.inc(ins, "wld", 16)
    ins = nc.sync.dma_start(ids[:], idn.ap())
    b.inc(ins, "wld", 16)
    ins = nc.sync.dma_start(bub[:, :, 0:T], BuT.ap().rearrange("c p t -> p c t"))
    b.inc(ins, "wld", 16)
    b.wait(nc.vector, "wld")
    nc.vector.tensor_copy(asb[:], af[:])
    nc.vector.memset(ht[:, 0, :], 0.0)
    nc.vector.memset(bub[:, :, T : T + 16], 0.0)
    b.barrier()

    G = b.cnt.get("gl", 0)
    M = b.cnt.get("mm", 0)
    C = b.cnt.get("hcp", 0)
    UN = 8
    nc.tensor.matmul(ps_a[:], ids[:], bub[:, :, 0:1], start=True, stop=False,
                     skip_group_check=True)
    with nc.Fori(0, T // UN, engines=[ET.PE, ET.Activation, ET.DVE]) as j:
        for sub in range(UN):
            cur, nxt = sub % 2, 1 - sub % 2
            tv = j * UN + sub
            pcur = ps_a if cur == 0 else ps_b2
            pnxt = ps_b2 if cur == 0 else ps_a
            nc.tensor.wait_ge(b.sem("gl"), G + tv)
            for m in range(8):
                for k in range(8):
                    ins = nc.tensor.matmul(
                        pcur[:, m : m + 1],
                        asb[:, k, m * 128 : (m + 1) * 128],
                        ht[:, cur, k : k + 1],
                        start=False, stop=(k == 7), skip_group_check=True)
                    if m == 7 and k == 7:
                        ins.then_inc(b.sem("mm"), 1)
            nc.tensor.matmul(pnxt[:], ids[:], bub[:, :, bass.ds(tv + 1, 1)],
                             start=True, stop=False, skip_group_check=True)
            nc.scalar.wait_ge(b.sem("mm"), M + tv + 1)
            ins = nc.scalar.activation(ht[:, nxt, :], pcur[:], AF.Gelu)
            ins.then_inc(b.sem("gl"), 1)
            nc.vector.wait_ge(b.sem("gl"), G + tv + 1)
            ins = nc.vector.tensor_copy(hst[:, :, bass.ds(tv, 1)], ht[:, nxt, :])
            ins.then_inc(b.sem("hcp"), 1)
    b.cnt["gl"] = G + T
    b.cnt["mm"] = M + T
    b.cnt["hcp"] = C + T
    b.wait(nc.vector, "hcp")
    b.drain_inc(nc.vector, "vdr")
    b.wait(nc.gpsimd, "vdr")
    ins = nc.gpsimd.dma_start(hT.ap().rearrange("c p t -> p c t"), hst[:])
    b.inc(ins, "hst", 16)
    b.wait(nc.gpsimd, "hst")
    b.barrier()
    b.restore(m0)


def emit_cout(b, xs, hT, cT, gb8, tag=""):
    nc = b.nc
    m0 = b.mark()
    wsb = b.sb(f"p7w{tag}", [128, 8, 1024], BF16)
    wf = b.sb(f"p7wf{tag}", [128, 8, 1024], F32)
    gb = b.sb(f"p7gb{tag}", [128, 2, 8], F32)
    xsl = b.sb(f"p7x{tag}", [128, 8, SL], F32)
    hsl = b.sb(f"p7h{tag}", [128, 8, SL], BF16)
    yf = b.sb(f"p7y{tag}", [128, 8, SL], F32)
    yb = b.sb(f"p7yb{tag}", [128, 8, SL], BF16)
    sqb = b.sb(f"p7sq{tag}", [128, 8, SL], BF16)
    tmp = b.sb(f"p7t{tag}", [128, 2, SL], F32)
    rows = b.sb(f"p7rows{tag}", [1, 8, SL], F32)
    onesb = b.sb(f"p7ones{tag}", [128, 1], BF16)
    ones1 = b.sb(f"p7one1{tag}", [1, 128], F32)
    ps_s = b.ps(f"p7pss{tag}", [1, SL], F32)
    ps_q = b.ps(f"p7psq{tag}", [1, SL], F32)
    ps_b = b.ps(f"p7psb{tag}", [128, 2, SL], F32)
    ps_y = b.ps(f"p7psy{tag}", [128, 2, SL], F32)
    MU, E2, VAR, RSTD, MR = 0, 1, 2, 3, 4

    ins = nc.sync.dma_start(wf[:], cT.ap().rearrange("(c p) m -> p c m", p=128))
    b.inc(ins, "wld", 16)
    ins = nc.sync.dma_start(gb[:], gb8.ap())
    b.inc(ins, "wld", 16)
    b.wait(nc.vector, "wld")
    nc.vector.tensor_copy(wsb[:], wf[:])
    nc.vector.memset(onesb[:], 1.0)
    nc.vector.memset(ones1[:], 1.0)
    b.barrier()
    for s in range(NSL):
        t0 = s * SL
        ins = nc.sync.dma_start(xsl[:], xs[:, :, t0 : t0 + SL].rearrange("c p t -> p c t"))
        b.inc(ins, "xld", 16)
        ins = nc.sync.dma_start(hsl[:], hT[:, :, t0 : t0 + SL].rearrange("c p t -> p c t"))
        b.inc(ins, "xld", 16)
        b.wait(nc.tensor, "xld")
        for ot in range(NCHUNK):
            par = ot % 2
            if b.cnt.get("cev", 0) >= 2:
                b.wait_at(nc.tensor, "cev", b.cnt["cev"] - 2)
            for c in range(NCHUNK):
                ins = nc.tensor.matmul(ps_y[:, par, :], wsb[:, c, ot * 128 : (ot + 1) * 128],
                                       hsl[:, c, :], start=(c == 0), stop=(c == 7))
            b.inc(ins, "mm", 1)
            b.wait(nc.vector, "mm")
            nc.vector.tensor_copy(yf[:, ot, :], ps_y[:, par, :])
            ins = nc.vector.tensor_copy(yb[:, ot, :], ps_y[:, par, :])
            b.inc(ins, "cev", 2)
        ins = nc.vector.tensor_tensor(sqb[:], yb[:], yb[:], ALU.mult)
        b.inc(ins, "ac", 1)
        b.wait(nc.tensor, "ac")
        for c in range(NCHUNK):
            nc.tensor.matmul(ps_s[:], onesb[:], yb[:, c, :], start=(c == 0), stop=(c == 7))
        for c in range(NCHUNK):
            ins = nc.tensor.matmul(ps_q[:], onesb[:], sqb[:, c, :], start=(c == 0), stop=(c == 7))
        b.inc(ins, "mm", 1)
        b.wait(nc.vector, "mm")
        nc.vector.tensor_scalar_mul(rows[:, MU, :], ps_s[:], 1.0 / D)
        nc.vector.tensor_scalar_mul(rows[:, E2, :], ps_q[:], 1.0 / D)
        nc.vector.tensor_tensor(rows[:, VAR, :], rows[:, MU, :], rows[:, MU, :], ALU.mult)
        ins = nc.vector.tensor_tensor(rows[:, VAR, :], rows[:, E2, :], rows[:, VAR, :], ALU.subtract)
        b.inc(ins, "dv", 1)
        b.wait(nc.scalar, "dv")
        nc.scalar.activation(rows[:, E2, :], rows[:, VAR, :], AF.Ln, bias=EPS_LN)
        ins = nc.scalar.activation(rows[:, RSTD, :], rows[:, E2, :], AF.Exp, scale=-0.5)
        b.inc(ins, "ac", 1)
        b.wait(nc.vector, "ac")
        ins = nc.vector.tensor_tensor(rows[:, MR, :], rows[:, MU, :], rows[:, RSTD, :], ALU.mult)
        b.inc(ins, "dv", 1)
        b.wait(nc.tensor, "dv")
        nc.tensor.matmul(ps_b[:, 0, :], ones1[:], rows[:, RSTD, :], start=True, stop=True)
        ins = nc.tensor.matmul(ps_b[:, 1, :], ones1[:], rows[:, MR, :], start=True, stop=True)
        b.inc(ins, "mm", 1)
        b.wait(nc.vector, "mm")
        for c in range(NCHUNK):
            nc.vector.tensor_tensor(tmp[:, 0, :], yf[:, c, :], ps_b[:, 0, :], ALU.mult)
            nc.vector.tensor_tensor(tmp[:, 1, :], tmp[:, 0, :], ps_b[:, 1, :], ALU.subtract)
            nc.vector.tensor_scalar(tmp[:, 0, :], tmp[:, 1, :], gb[:, 0, c : c + 1],
                                    gb[:, 1, c : c + 1], ALU.mult, ALU.add)
            ins = nc.vector.tensor_tensor(xsl[:, c, :], tmp[:, 0, :], xsl[:, c, :], ALU.add)
        b.inc(ins, "dv", 1)
        b.drain_inc(nc.vector, "vdr")
        b.wait(nc.gpsimd, "vdr")
        ins = nc.gpsimd.dma_start(xs[:, :, t0 : t0 + SL].rearrange("c p t -> p c t"), xsl[:])
        b.inc(ins, "xst", 16)
        b.wait(nc.gpsimd, "xst")
        b.barrier()
    b.restore(m0)


def build_p5():
    nc = bass.Bass("TRN2", target_bir_lowering=False, debug=False)
    xs = nc.dram_tensor("xs", [NCHUNK, 128, T], F32, kind="ExternalInput")
    bT = nc.dram_tensor("bT", [D, D], F32, kind="ExternalInput")
    aT = nc.dram_tensor("aT", [D, D], F32, kind="ExternalInput")
    idn = nc.dram_tensor("idn", [128, 128], BF16, kind="ExternalInput")
    BuT = nc.dram_tensor("BuT", [NCHUNK, 128, T], BF16, kind="ExternalOutput")
    hT = nc.dram_tensor("hT", [NCHUNK, 128, T], BF16, kind="ExternalOutput")
    b = Bld(nc)
    b.setup_consts([EPS_LN, EPS_HLN, -0.5])
    emit_bu(b, xs, bT, BuT)
    emit_rnn(b, BuT, aT, idn, hT)
    return nc


def build_p6():
    nc = bass.Bass("TRN2", target_bir_lowering=False, debug=False)
    xs = nc.dram_tensor("xs", [NCHUNK, 128, T], F32, kind="ExternalInput")
    xso = nc.dram_tensor("xso", [NCHUNK, 128, T], F32, kind="ExternalOutput")
    hT = nc.dram_tensor("hT", [NCHUNK, 128, T], BF16, kind="ExternalInput")
    cT = nc.dram_tensor("cT", [D, D], F32, kind="ExternalInput")
    gb8 = nc.dram_tensor("gb8", [128, 2, NCHUNK], F32, kind="ExternalInput")
    b = Bld(nc)
    b.setup_consts([EPS_LN, EPS_HLN, -0.5])
    emit_dram_copy(b, xso, xs)
    emit_cout(b, xso, hT, cT, gb8)
    return nc



def _core_inputs(inputs, bcore):
    x = inputs["L0_attn_x"][bcore]
    xT = np.ascontiguousarray(x.T).reshape(NCHUNK, 128, T)
    d = {"xs": xT}
    d["wrT"] = np.ascontiguousarray(inputs["L0_ffn_W_r"].T)
    d["wkT"] = np.ascontiguousarray(inputs["L0_ffn_W_k"].T)
    d["wvT"] = np.ascontiguousarray(inputs["L0_ffn_W_v"].T)
    murk = np.stack([inputs["L0_ffn_mu_r"].reshape(NCHUNK, 128).T,
                     inputs["L0_ffn_mu_k"].reshape(NCHUNK, 128).T], axis=1)
    d["murk"] = np.ascontiguousarray(murk)
    return d


def emit_dram_copy(b, dst, src, nbytes_hint=None):
    nc = b.nc
    m0 = b.mark()
    cp = b.sb("dcp", [128, NCHUNK, T // 4], F32)
    for i in range(4):
        if i > 0:
            b.wait(nc.sync, "dcp")
        ins = nc.sync.dma_start(cp[:], src[:, :, i * (T // 4) : (i + 1) * (T // 4)].rearrange("c p t -> p c t"))
        b.inc(ins, "dcp", 16)
        b.wait(nc.sync, "dcp")
        ins = nc.sync.dma_start(dst[:, :, i * (T // 4) : (i + 1) * (T // 4)].rearrange("c p t -> p c t"), cp[:])
        b.inc(ins, "dcp", 16)
    b.wait(nc.sync, "dcp")
    b.barrier()
    b.restore(m0)


def build_p0():
    nc = bass.Bass("TRN2", target_bir_lowering=False, debug=False)
    xs = nc.dram_tensor("xs", [NCHUNK, 128, T], F32, kind="ExternalInput")
    xso = nc.dram_tensor("xso", [NCHUNK, 128, T], F32, kind="ExternalOutput")
    wrT = nc.dram_tensor("wrT", [D, D], F32, kind="ExternalInput")
    wkT = nc.dram_tensor("wkT", [D, D], F32, kind="ExternalInput")
    wvT = nc.dram_tensor("wvT", [D, D], F32, kind="ExternalInput")
    murk = nc.dram_tensor("murk", [128, 2, NCHUNK], F32, kind="ExternalInput")
    b = Bld(nc)
    b.setup_consts([EPS_LN, EPS_HLN, -0.5])
    emit_dram_copy(b, xso, xs)
    emit_l0ffn(b, xso, wrT, wkT, wvT, murk)
    return nc


def build_fused():
    nc = bass.Bass("TRN2", target_bir_lowering=False, debug=False)
    xin = nc.dram_tensor("xin", [NCHUNK, 128, T], F32, kind="ExternalInput")
    xout = nc.dram_tensor("xout", [NCHUNK, 128, T], F32, kind="ExternalOutput")
    wrT = nc.dram_tensor("wrT", [D, D], F32, kind="ExternalInput")
    wkT = nc.dram_tensor("wkT", [D, D], F32, kind="ExternalInput")
    wvT = nc.dram_tensor("wvT", [D, D], F32, kind="ExternalInput")
    murk = nc.dram_tensor("murk", [128, 2, NCHUNK], F32, kind="ExternalInput")
    wqT = nc.dram_tensor("wqT", [D, D], F32, kind="ExternalInput")
    wkTa = nc.dram_tensor("wkTa", [D, D], F32, kind="ExternalInput")
    wvTa = nc.dram_tensor("wvTa", [D, D], F32, kind="ExternalInput")
    qkvbias = nc.dram_tensor("qkvbias", [128, 3, NCHUNK], F32, kind="ExternalInput")
    qkln = nc.dram_tensor("qkln", [128, 4], F32, kind="ExternalInput")
    bvrow = nc.dram_tensor("bvrow", [1, D], F32, kind="ExternalInput")
    hpatd = nc.dram_tensor("hpatd", [2, 128], F32, kind="ExternalInput")
    woT = nc.dram_tensor("woT", [D, D], F32, kind="ExternalInput")
    bo8 = nc.dram_tensor("bo8", [128, NCHUNK], F32, kind="ExternalInput")
    gam8 = nc.dram_tensor("gam8", [128, NCHUNK], F32, kind="ExternalInput")
    ibt = nc.dram_tensor("ibt", [128, 1], F32, kind="ExternalInput")
    w1T = nc.dram_tensor("w1T", [D, DHID], F32, kind="ExternalInput")
    w2T = nc.dram_tensor("w2T", [D, DHID], F32, kind="ExternalInput")
    h1T = nc.dram_tensor("h1T", [DHID, DHID], F32, kind="ExternalInput")
    h2T = nc.dram_tensor("h2T", [DHID, DHID], F32, kind="ExternalInput")
    w3T = nc.dram_tensor("w3T", [DHID, D], F32, kind="ExternalInput")
    scl = nc.dram_tensor("scl", [128, 5, 32], F32, kind="ExternalInput")
    bT = nc.dram_tensor("bT", [D, D], F32, kind="ExternalInput")
    aT = nc.dram_tensor("aT", [D, D], F32, kind="ExternalInput")
    cT = nc.dram_tensor("cT", [D, D], F32, kind="ExternalInput")
    idn = nc.dram_tensor("idn", [128, 128], BF16, kind="ExternalInput")
    gb8 = nc.dram_tensor("gb8", [128, 2, NCHUNK], F32, kind="ExternalInput")
    qT = nc.dram_tensor("qT", [NCHUNK, 128, T], BF16, kind="Internal")
    kT = nc.dram_tensor("kT", [NCHUNK, 128, T], BF16, kind="Internal")
    vtok = nc.dram_tensor("vtok", [T, D], BF16, kind="Internal")
    BuT = nc.dram_tensor("BuT", [NCHUNK, 128, T], BF16, kind="Internal")
    hT = nc.dram_tensor("hT", [NCHUNK, 128, T], BF16, kind="Internal")
    w1R = nc.dram_tensor("w1R", [32, 128, D], BF16, kind="Internal")
    w2R = nc.dram_tensor("w2R", [32, 128, D], BF16, kind="Internal")
    h1R = nc.dram_tensor("h1R", [32, 128, DHID], BF16, kind="Internal")
    h2R = nc.dram_tensor("h2R", [32, 128, DHID], BF16, kind="Internal")
    w3R = nc.dram_tensor("w3R", [8, 128, DHID], BF16, kind="Internal")
    woR = nc.dram_tensor("woR", [D, D], BF16, kind="Internal")
    wqR = nc.dram_tensor("wqR", [D, D], BF16, kind="Internal")
    wkR = nc.dram_tensor("wkR", [D, D], BF16, kind="Internal")
    wvR = nc.dram_tensor("wvR", [D, D], BF16, kind="Internal")
    wrR = nc.dram_tensor("wrR", [D, D], BF16, kind="Internal")
    wkfR = nc.dram_tensor("wkfR", [D, D], BF16, kind="Internal")
    wvfR = nc.dram_tensor("wvfR", [D, D], BF16, kind="Internal")

    b = Bld(nc)
    b.setup_consts([EPS_LN, EPS_HLN, -0.5])
    emit_cast_flat(b, [(wqT, wqR, D, D), (wkTa, wkR, D, D), (wvTa, wvR, D, D),
                       (woT, woR, D, D)])
    b.wcs_flat = b.cnt.get("wcs", 0)
    emit_wcast(b, [(w1T, w1R, D, DHID), (w2T, w2R, D, DHID), (h1T, h1R, DHID, DHID),
                   (h2T, h2R, DHID, DHID), (w3T, w3R, DHID, D)])
    emit_l0ffn(b, xout, wrT, wkT, wvT, murk, tag="a", xsrc=xin)
    emit_qkv2(b, xout, wqR, wkR, wvR, qkvbias, qkln, bvrow, hpatd, qT, kT, vtok)
    emit_attn2(b, xout, qT, kT, vtok, woR, bo8, gam8, ibt)
    emit_l0ffn(b, xout, wrT, wkT, wvT, murk, tag="b")
    emit_l1ffn(b, xout, w1R, w2R, h1R, h2R, w3R, scl, tag="a")
    emit_bu(b, xout, bT, BuT)
    emit_rnn(b, BuT, aT, idn, hT)
    emit_cout(b, xout, hT, cT, gb8)
    emit_l1ffn(b, xout, w1R, w2R, h1R, h2R, w3R, scl, tag="b")
    return nc


def _fused_inputs(inputs):
    def tile8(v):
        return np.ascontiguousarray(v.reshape(NCHUNK, 128).T)
    qkln = np.stack([np.tile(inputs["L0_attn_qn_scale"], 2), np.tile(inputs["L0_attn_qn_bias"], 2),
                     np.tile(inputs["L0_attn_kn_scale"], 2), np.tile(inputs["L0_attn_kn_bias"], 2)],
                    axis=1).astype(np.float32)
    common = {
        "wrT": np.ascontiguousarray(inputs["L0_ffn_W_r"].T),
        "wkT": np.ascontiguousarray(inputs["L0_ffn_W_k"].T),
        "wvT": np.ascontiguousarray(inputs["L0_ffn_W_v"].T),
        "murk": np.ascontiguousarray(np.stack(
            [inputs["L0_ffn_mu_r"].reshape(NCHUNK, 128).T,
             inputs["L0_ffn_mu_k"].reshape(NCHUNK, 128).T], axis=1)),
        "wqT": np.ascontiguousarray(inputs["L0_attn_Wq"]),
        "wkTa": np.ascontiguousarray(inputs["L0_attn_Wk"]),
        "wvTa": np.ascontiguousarray(inputs["L0_attn_Wv"]),
        "qkvbias": np.ascontiguousarray(np.stack(
            [tile8(inputs["L0_attn_bq"]), tile8(inputs["L0_attn_bk"]),
             tile8(inputs["L0_attn_bv"])], axis=1)),
        "qkln": np.ascontiguousarray(qkln),
        "bvrow": np.ascontiguousarray(inputs["L0_attn_bv"][None, :]),
        "hpatd": np.repeat(np.eye(2, dtype=np.float32), 64, axis=1),
        "woT": np.ascontiguousarray(inputs["L0_attn_Wo"]),
        "bo8": tile8(inputs["L0_attn_bo"]),
        "gam8": tile8(inputs["L0_attn_gamma"]),
        "ibt": np.ascontiguousarray(np.tile(inputs["L0_attn_init_bias"].reshape(1, 1),
                                            (128, 1)).astype(np.float32)),
        "w1T": np.ascontiguousarray(inputs["L1_ffn_w1_w"].T),
        "w2T": np.ascontiguousarray(inputs["L1_ffn_w2_w"].T),
        "h1T": np.ascontiguousarray(inputs["L1_ffn_h1_w"].T),
        "h2T": np.ascontiguousarray(inputs["L1_ffn_h2_w"].T),
        "w3T": np.ascontiguousarray(inputs["L1_ffn_w3_w"].T),
        "scl": _p4_scl(inputs),
        "bT": np.ascontiguousarray(inputs["L1_attn_B"].T),
        "aT": np.ascontiguousarray(inputs["L1_attn_A"].T),
        "cT": np.ascontiguousarray(inputs["L1_attn_C"].T),
        "idn": np.eye(128).astype(BF),
        "gb8": np.ascontiguousarray(np.stack(
            [inputs["L1_attn_gamma"].reshape(NCHUNK, 128).T,
             inputs["L1_attn_beta"].reshape(NCHUNK, 128).T], axis=1)),
    }
    return common


def kernel(**inputs):
    nc = build_fused()
    common = _fused_inputs(inputs)
    x = inputs["L0_attn_x"].astype(np.float32, copy=False)
    in_maps = [dict(common, xin=_x_to_dev(x[bc])) for bc in range(8)]
    res = run_bass_kernel_spmd(nc, in_maps, core_ids=list(range(8)))
    out = np.stack([_x_from_dev(res.results[bc]["xout"]) for bc in range(8)])
    return out.astype(np.float32)


def _x_to_dev(x2d):
    return np.ascontiguousarray(x2d.T).reshape(NCHUNK, 128, T)


def _x_from_dev(xdev):
    return xdev.reshape(D, T).T




# version 25 (final)
# speedup vs baseline: 1.6026x; 1.0196x over previous
import numpy as np
import ml_dtypes
import concourse.bass as bass
import concourse.mybir as mybir
from concourse.bass_utils import run_bass_kernel_spmd

F32 = mybir.dt.float32
BF16 = mybir.dt.bfloat16
AF = mybir.ActivationFunctionType
ALU = mybir.AluOpType
ET = mybir.EngineType
BF = ml_dtypes.bfloat16

T = 4096
D = 1024
NCHUNK = 8
SL = 512
NSL = T // SL
DHID = 4096
EPS_LN = 1e-5
EPS_HLN = 1e-6


class Bld:

    def __init__(self, nc):
        self.nc = nc
        self.sems = {}
        self.cnt = {}

    def sem(self, name):
        if name not in self.sems:
            self.sems[name] = self.nc.alloc_semaphore(name=name)
            self.cnt[name] = 0
        return self.sems[name]

    def inc(self, ins, name, k=1):
        ins.then_inc(self.sem(name), k)
        self.cnt[name] += k

    def wait(self, eng, name):
        eng.wait_ge(self.sem(name), self.cnt[name])

    def wait_at(self, eng, name, v):
        eng.wait_ge(self.sem(name), v)

    def barrier(self):
        self.nc.all_engine_barrier()

    def setup_consts(self, vals):
        t = self.sb("consts", [128, len(vals)], F32)
        for i, v in enumerate(vals):
            self.nc.vector.memset(t[:, i : i + 1], v)
            self.nc.const_aps.register(F32, v, t[:, i : i + 1])
        self.barrier()

    def drain_inc(self, eng, name):
        ins = eng.drain()
        self.inc(ins, name, 1)

    def sb(self, name, shape, dtype):
        return self.nc.alloc_sbuf_tensor(name, shape, dtype)

    def ps(self, name, shape, dtype=F32):
        return self.nc.alloc_psum_tensor(name, shape, dtype)

    def mark(self):
        nc = self.nc
        return (nc.sbuf_base, nc.sbuf_top, nc.psum_base, nc.psum_top)

    def restore(self, m):
        nc = self.nc
        nc.sbuf_base, nc.sbuf_top, nc.psum_base, nc.psum_top = m



def emit_l0ffn(b, xs, wrT, wkT, wvT, murk, tag="", xsrc=None):
    nc = b.nc
    m0 = b.mark()
    wsb = b.sb(f"p0w{tag}", [128, 3, 8, 1024], BF16)
    wf = b.sb(f"p0wf{tag}", [128, 8, 1024], F32)
    mu = b.sb(f"p0mu{tag}", [128, 4, 8], F32)
    xsl = b.sb(f"p0x{tag}", [128, 8, SL], F32)
    xb = b.sb(f"p0xb{tag}", [128, 8, SL], BF16)
    sqb = b.sb(f"p0sq{tag}", [128, 8, SL], BF16)
    xn = b.sb(f"p0xn{tag}", [128, 8, SL + 1], BF16)
    xr = b.sb(f"p0xr{tag}", [128, 8, SL], BF16)
    xk = b.sb(f"p0xk{tag}", [128, 8, SL], BF16)
    rg = b.sb(f"p0rg{tag}", [128, 8, SL], BF16)
    kk = b.sb(f"p0kk{tag}", [128, 8, SL], BF16)
    tmp = b.sb(f"p0tmp{tag}", [128, 8, SL], F32)
    rows = b.sb(f"p0rows{tag}", [1, 8, SL], F32)
    onesb = b.sb(f"p0ones{tag}", [128, 1], BF16)
    ones1 = b.sb(f"p0one1{tag}", [1, 128], F32)
    ps_s = b.ps(f"p0pss{tag}", [1, SL], F32)
    ps_q = b.ps(f"p0psq{tag}", [1, SL], F32)
    ps_b = b.ps(f"p0psb{tag}", [128, 2, SL], F32)
    ps_y = b.ps(f"p0psy{tag}", [128, 2, SL], F32)

    MU, E2, VAR, RSTD, MR = 0, 1, 2, 3, 4
    for wi, w in enumerate((wrT, wkT, wvT)):
        if wi > 0:
            b.wait(nc.sync, "wfree")
        ins = nc.sync.dma_start(wf[:], w.ap().rearrange("(c p) m -> p c m", p=128))
        b.inc(ins, "wld", 16)
        b.wait(nc.vector, "wld")
        ins = nc.vector.tensor_copy(wsb[:, wi, :, :], wf[:])
        b.inc(ins, "wfree", 1)
    ins = nc.sync.dma_start(mu[:, 0:2, :], murk.ap())
    b.inc(ins, "wld", 16)
    b.wait(nc.vector, "wld")
    nc.vector.memset(onesb[:], 1.0)
    nc.vector.memset(ones1[:], 1.0)
    nc.vector.tensor_scalar(mu[:, 2:4, :], mu[:, 0:2, :], -1.0, 1.0, ALU.mult, ALU.add)
    nc.vector.memset(xn[:, :, 0:1], 0.0)
    b.barrier()

    for s in range(NSL):
        _p0_slice(b, xs, s, wsb, mu, xsl, xb, sqb, xn, xr, xk, rg, kk, tmp, rows,
                  onesb, ones1, ps_s, ps_q, ps_b, ps_y, xsrc=xsrc)
    b.restore(m0)


def _p0_slice(b, xs, s, wsb, mu, xsl, xb, sqb, xn, xr, xk, rg, kk, tmp, rows,
              onesb, ones1, ps_s, ps_q, ps_b, ps_y, xsrc=None):
    nc = b.nc
    MU, E2, VAR, RSTD, MR = 0, 1, 2, 3, 4
    t0 = s * SL
    xrd = xs if xsrc is None else xsrc
    ins = nc.sync.dma_start(xsl[:], xrd[:, :, t0 : t0 + SL].rearrange("c p t -> p c t"))
    b.inc(ins, "xld", 16)
    if s > 0:
        nc.vector.tensor_copy(xn[:, :, 0:1], xn[:, :, SL : SL + 1])
    b.wait(nc.vector, "xld")
    nc.vector.tensor_copy(xb[:], xsl[:])
    ins = nc.vector.tensor_tensor(sqb[:], xb[:], xb[:], ALU.mult)
    b.inc(ins, "dv", 1)
    b.wait(nc.tensor, "dv")
    for c in range(NCHUNK):
        nc.tensor.matmul(ps_s[:], onesb[:], xb[:, c, :], start=(c == 0), stop=(c == 7))
    for c in range(NCHUNK):
        ins = nc.tensor.matmul(ps_q[:], onesb[:], sqb[:, c, :], start=(c == 0), stop=(c == 7))
    b.inc(ins, "mm", 1)
    b.wait(nc.vector, "mm")
    nc.vector.tensor_scalar_mul(rows[:, MU, :], ps_s[:], 1.0 / D)
    nc.vector.tensor_scalar_mul(rows[:, E2, :], ps_q[:], 1.0 / D)
    nc.vector.tensor_tensor(rows[:, VAR, :], rows[:, MU, :], rows[:, MU, :], ALU.mult)
    ins = nc.vector.tensor_tensor(rows[:, VAR, :], rows[:, E2, :], rows[:, VAR, :], ALU.subtract)
    b.inc(ins, "dv", 1)
    b.wait(nc.scalar, "dv")
    nc.scalar.activation(rows[:, E2, :], rows[:, VAR, :], AF.Ln, bias=EPS_LN)
    ins = nc.scalar.activation(rows[:, RSTD, :], rows[:, E2, :], AF.Exp, scale=-0.5)
    b.inc(ins, "ac", 1)
    b.wait(nc.vector, "ac")
    ins = nc.vector.tensor_tensor(rows[:, MR, :], rows[:, MU, :], rows[:, RSTD, :], ALU.mult)
    b.inc(ins, "dv", 1)
    b.wait(nc.tensor, "dv")
    nc.tensor.matmul(ps_b[:, 0, :], ones1[:], rows[:, RSTD, :], start=True, stop=True)
    ins = nc.tensor.matmul(ps_b[:, 1, :], ones1[:], rows[:, MR, :], start=True, stop=True)
    b.inc(ins, "mm", 1)
    b.wait(nc.vector, "mm")
    for c in range(NCHUNK):
        nc.vector.tensor_tensor(tmp[:, c, :], xsl[:, c, :], ps_b[:, 0, :], ALU.mult)
        nc.vector.tensor_tensor(xn[:, c, 1 : SL + 1], tmp[:, c, :], ps_b[:, 1, :], ALU.subtract)
    for c in range(NCHUNK):
        nc.vector.tensor_scalar_mul(xr[:, c, :], xn[:, c, 0:SL], mu[:, 2, c : c + 1])
        nc.vector.scalar_tensor_tensor(xr[:, c, :], xn[:, c, 1 : SL + 1], mu[:, 0, c : c + 1],
                                       xr[:, c, :], ALU.mult, ALU.add)
        nc.vector.tensor_scalar_mul(xk[:, c, :], xn[:, c, 0:SL], mu[:, 3, c : c + 1])
        ins = nc.vector.scalar_tensor_tensor(xk[:, c, :], xn[:, c, 1 : SL + 1], mu[:, 1, c : c + 1],
                                             xk[:, c, :], ALU.mult, ALU.add)
    b.inc(ins, "dv", 1)
    b.wait(nc.tensor, "dv")
    for ot in range(NCHUNK):
        par = ot % 2
        if ot >= 2:
            b.wait_at(nc.tensor, "ev", b.cnt["ev"] - 1)
        for c in range(NCHUNK):
            ins = nc.tensor.matmul(ps_y[:, par, :], wsb[:, 0, c, ot * 128 : (ot + 1) * 128],
                                   xr[:, c, :], start=(c == 0), stop=(c == 7))
        b.inc(ins, "mm", 1)
        b.wait(nc.scalar, "mm")
        ins = nc.scalar.activation(rg[:, ot, :], ps_y[:, par, :], AF.Sigmoid)
        b.inc(ins, "ev", 1)
    for ot in range(NCHUNK):
        par = ot % 2
        b.wait_at(nc.tensor, "ev", b.cnt["ev"] - 1)
        for c in range(NCHUNK):
            ins = nc.tensor.matmul(ps_y[:, par, :], wsb[:, 1, c, ot * 128 : (ot + 1) * 128],
                                   xk[:, c, :], start=(c == 0), stop=(c == 7))
        b.inc(ins, "mm", 1)
        b.wait(nc.vector, "mm")
        ins = nc.vector.tensor_scalar_max(tmp[:, ot, :], ps_y[:, par, :], 0.0)
        b.inc(ins, "ev", 1)
        ins = nc.vector.tensor_tensor(kk[:, ot, :], tmp[:, ot, :], tmp[:, ot, :], ALU.mult)
        b.inc(ins, "ac", 1)
    b.wait(nc.tensor, "ac")
    for ot in range(NCHUNK):
        par = ot % 2
        b.wait_at(nc.tensor, "ev", b.cnt["ev"] - 1)
        for c in range(NCHUNK):
            ins = nc.tensor.matmul(ps_y[:, par, :], wsb[:, 2, c, ot * 128 : (ot + 1) * 128],
                                   kk[:, c, :], start=(c == 0), stop=(c == 7))
        b.inc(ins, "mm", 1)
        b.wait(nc.vector, "mm")
        nc.vector.tensor_tensor(tmp[:, ot, :], ps_y[:, par, :], rg[:, ot, :], ALU.mult)
        ins = nc.vector.scalar_tensor_tensor(xsl[:, ot, :], tmp[:, ot, :], 0.5,
                                             xsl[:, ot, :], ALU.mult, ALU.add)
        b.inc(ins, "ev", 1)
    b.drain_inc(nc.vector, "vdr")
    b.wait(nc.gpsimd, "vdr")
    ins = nc.gpsimd.dma_start(xs[:, :, t0 : t0 + SL].rearrange("c p t -> p c t"), xsl[:])
    b.inc(ins, "xst", 16)
    b.wait(nc.gpsimd, "xst")
    b.barrier()



def emit_qkv(b, xs, wqT, wkT, wvT, qkvbias, qkln, bvrow, hpatd, qT, kT, vtok, tag=""):
    nc = b.nc
    m0 = b.mark()
    wsb = b.sb(f"p1w{tag}", [128, 3, 8, 1024], BF16)
    wf = b.sb(f"p1wf{tag}", [128, 8, 1024], F32)
    bias = b.sb(f"p1b{tag}", [128, 3, 8], F32)
    qkl = b.sb(f"p1l{tag}", [128, 4], F32)
    bvb = b.sb(f"p1bvb{tag}", [128, 1024], F32)
    bvr = b.sb(f"p1bvr{tag}", [1, 1024], F32)
    xsl = b.sb(f"p1x{tag}", [128, 8, SL], F32)
    xb = b.sb(f"p1xb{tag}", [128, 8, SL], BF16)
    sqb = b.sb(f"p1sq{tag}", [128, 8, SL], BF16)
    xn = b.sb(f"p1xn{tag}", [128, 8, SL], BF16)
    qb = b.sb(f"p1qb{tag}", [128, 8, SL], BF16)
    sqq = b.sb(f"p1sqq{tag}", [128, 2, SL], BF16)
    qn2 = b.sb(f"p1qn{tag}", [128, 2, 8, SL], BF16)
    vsl = b.sb(f"p1v{tag}", [128, 4, 1024], BF16)
    tmp = b.sb(f"p1tmp{tag}", [128, 4, SL], F32)
    rows = b.sb(f"p1rows{tag}", [1, 8, SL], F32)
    hrows = b.sb(f"p1hr{tag}", [2, 2, 5, SL], F32)
    onesb = b.sb(f"p1ones{tag}", [128, 1], BF16)
    ones1 = b.sb(f"p1one1{tag}", [1, 128], F32)
    blk2 = b.sb(f"p1blk{tag}", [128, 2], BF16)
    hpat = b.sb(f"p1hp{tag}", [2, 128], F32)
    ps_s = b.ps(f"p1pss{tag}", [1, SL], F32)
    ps_q = b.ps(f"p1psq{tag}", [1, SL], F32)
    ps_b = b.ps(f"p1psb{tag}", [128, 2, SL], F32)
    ps_y = b.ps(f"p1psy{tag}", [128, 2, SL], F32)
    ps_h = b.ps(f"p1psh{tag}", [2, 2, SL], F32)

    MU, E2, VAR, RSTD, MR = 0, 1, 2, 3, 4
    for wi, w in enumerate((wqT, wkT, wvT)):
        if wi > 0:
            b.wait(nc.sync, "wfree")
        ins = nc.sync.dma_start(wf[:], w.ap().rearrange("(c p) m -> p c m", p=128))
        b.inc(ins, "wld", 16)
        b.wait(nc.vector, "wld")
        ins = nc.vector.tensor_copy(wsb[:, wi, :, :], wf[:])
        b.inc(ins, "wfree", 1)
    for src, dst in ((qkvbias, bias), (qkln, qkl), (bvrow, bvr)):
        ins = nc.sync.dma_start(dst[:], src.ap())
        b.inc(ins, "wld", 16)
    b.wait(nc.vector, "wld")
    nc.vector.memset(onesb[:], 1.0)
    nc.vector.memset(ones1[:], 1.0)
    nc.vector.memset(blk2[:, :], 0.0)
    nc.vector.memset(blk2[0:64, 0:1], 1.0)
    nc.vector.memset(blk2[64:128, 1:2], 1.0)
    ins = nc.sync.dma_start(hpat[:], hpatd.ap())
    b.inc(ins, "wld", 16)
    b.wait(nc.vector, "wld")
    ins = nc.vector.memset(blk2[64:128, 0:1], 0.0)
    b.inc(ins, "dv", 1)
    b.wait(nc.tensor, "dv")
    nc.tensor.matmul(ps_b[:, 0, :], ones1[:], bvr[:, 0:SL], start=True, stop=True)
    ins = nc.tensor.matmul(ps_b[:, 1, :], ones1[:], bvr[:, SL:D], start=True, stop=True)
    b.inc(ins, "mm", 1)
    b.wait(nc.vector, "mm")
    nc.vector.tensor_copy(bvb[:], ps_b.ap().rearrange("p a t -> p (a t)"))
    b.barrier()

    for s in range(NSL):
        t0 = s * SL
        ins = nc.sync.dma_start(xsl[:], xs[:, :, t0 : t0 + SL].rearrange("c p t -> p c t"))
        b.inc(ins, "xld", 16)
        b.wait(nc.vector, "xld")
        nc.vector.tensor_copy(xb[:], xsl[:])
        ins = nc.vector.tensor_tensor(sqb[:], xb[:], xb[:], ALU.mult)
        b.inc(ins, "dv", 1)
        b.wait(nc.tensor, "dv")
        for c in range(NCHUNK):
            nc.tensor.matmul(ps_s[:], onesb[:], xb[:, c, :], start=(c == 0), stop=(c == 7))
        for c in range(NCHUNK):
            ins = nc.tensor.matmul(ps_q[:], onesb[:], sqb[:, c, :], start=(c == 0), stop=(c == 7))
        b.inc(ins, "mm", 1)
        b.wait(nc.vector, "mm")
        nc.vector.tensor_scalar_mul(rows[:, MU, :], ps_s[:], 1.0 / D)
        nc.vector.tensor_scalar_mul(rows[:, E2, :], ps_q[:], 1.0 / D)
        nc.vector.tensor_tensor(rows[:, VAR, :], rows[:, MU, :], rows[:, MU, :], ALU.mult)
        ins = nc.vector.tensor_tensor(rows[:, VAR, :], rows[:, E2, :], rows[:, VAR, :], ALU.subtract)
        b.inc(ins, "dv", 1)
        b.wait(nc.scalar, "dv")
        nc.scalar.activation(rows[:, E2, :], rows[:, VAR, :], AF.Ln, bias=EPS_LN)
        ins = nc.scalar.activation(rows[:, RSTD, :], rows[:, E2, :], AF.Exp, scale=-0.5)
        b.inc(ins, "ac", 1)
        b.wait(nc.vector, "ac")
        ins = nc.vector.tensor_tensor(rows[:, MR, :], rows[:, MU, :], rows[:, RSTD, :], ALU.mult)
        b.inc(ins, "dv", 1)
        b.wait(nc.tensor, "dv")
        nc.tensor.matmul(ps_b[:, 0, :], ones1[:], rows[:, RSTD, :], start=True, stop=True)
        ins = nc.tensor.matmul(ps_b[:, 1, :], ones1[:], rows[:, MR, :], start=True, stop=True)
        b.inc(ins, "mm", 1)
        b.wait(nc.vector, "mm")
        for c in range(NCHUNK):
            nc.vector.tensor_tensor(tmp[:, 0, :], xsl[:, c, :], ps_b[:, 0, :], ALU.mult)
            ins = nc.vector.tensor_tensor(xn[:, c, :], tmp[:, 0, :], ps_b[:, 1, :], ALU.subtract)
        b.inc(ins, "dv", 1)
        b.wait(nc.tensor, "dv")
        for qk in range(2):
            SC, BI = (0, 1) if qk == 0 else (2, 3)
            dst = qT if qk == 0 else kT
            qn = qn2[:, qk, :, :]
            for ot in range(NCHUNK):
                par = ot % 2
                if b.cnt.get("ev", 0) > 0:
                    b.wait_at(nc.tensor, "ev", b.cnt["ev"] - 1)
                for c in range(NCHUNK):
                    ins = nc.tensor.matmul(ps_y[:, par, :], wsb[:, qk, c, ot * 128 : (ot + 1) * 128],
                                           xn[:, c, :], start=(c == 0), stop=(c == 7))
                b.inc(ins, "mm", 1)
                b.wait(nc.vector, "mm")
                ins = nc.vector.tensor_scalar_add(qb[:, ot, :], ps_y[:, par, :],
                                                  bias[:, qk, ot : ot + 1])
                b.inc(ins, "ev", 1)
                ins = nc.vector.tensor_tensor(sqq[:, par, :], qb[:, ot, :], qb[:, ot, :], ALU.mult)
                b.inc(ins, "ac", 1)
                b.wait(nc.tensor, "ac")
                b.wait(nc.tensor, "dv")
                nc.tensor.matmul(ps_h[:, 0, :], blk2[:], qb[:, ot, :], start=True, stop=True)
                ins = nc.tensor.matmul(ps_h[:, 1, :], blk2[:], sqq[:, par, :], start=True, stop=True)
                b.inc(ins, "mm", 1)
                b.wait(nc.vector, "mm")
                hr = hrows[:, par, :, :]
                nc.vector.tensor_scalar_mul(hr[:, MU, :], ps_h[:, 0, :], 1.0 / 64)
                nc.vector.tensor_scalar_mul(hr[:, E2, :], ps_h[:, 1, :], 1.0 / 64)
                nc.vector.tensor_tensor(hr[:, VAR, :], hr[:, MU, :], hr[:, MU, :], ALU.mult)
                ins = nc.vector.tensor_tensor(hr[:, VAR, :], hr[:, E2, :], hr[:, VAR, :], ALU.subtract)
                b.inc(ins, "dv", 1)
                b.wait(nc.scalar, "dv")
                ins = nc.scalar.activation(hr[:, RSTD, :], hr[:, VAR, :], AF.Sqrt, bias=EPS_HLN)
                b.inc(ins, "ac", 1)
                b.wait(nc.vector, "ac")
                nc.vector.reciprocal(hr[:, RSTD, :], hr[:, RSTD, :])
                ins = nc.vector.tensor_tensor(hr[:, MR, :], hr[:, MU, :], hr[:, RSTD, :], ALU.mult)
                b.inc(ins, "dv", 1)
                b.wait(nc.tensor, "dv")
                if b.cnt.get("qnev", 0) > 0:
                    b.wait_at(nc.tensor, "qnev", b.cnt["qnev"])
                nc.tensor.matmul(ps_b[:, 0, :], hpat[:], hr[:, RSTD, :], start=True, stop=True)
                ins = nc.tensor.matmul(ps_b[:, 1, :], hpat[:], hr[:, MR, :], start=True, stop=True)
                b.inc(ins, "mm", 1)
                b.wait(nc.vector, "mm")
                nc.vector.tensor_tensor(tmp[:, 2 * par, :], qb[:, ot, :], ps_b[:, 0, :], ALU.mult)
                nc.vector.tensor_tensor(tmp[:, 2 * par + 1, :], tmp[:, 2 * par, :],
                                        ps_b[:, 1, :], ALU.subtract)
                ins = nc.vector.tensor_scalar(qn[:, ot, :], tmp[:, 2 * par + 1, :],
                                              qkl[:, SC : SC + 1], qkl[:, BI : BI + 1],
                                              ALU.mult, ALU.add)
                b.inc(ins, "qnev", 1)
            b.drain_inc(nc.vector, "vdr")
            b.wait(nc.gpsimd, "vdr")
            ins = nc.gpsimd.dma_start(dst[:, :, t0 : t0 + SL].rearrange("c p t -> p c t"), qn[:])
            b.inc(ins, "qst", 16)
        b.wait(nc.tensor, "qnev")
        for tt in range(4):
            par = tt % 2
            pv = ps_y if par == 0 else ps_b
            if tt >= 2:
                b.wait_at(nc.tensor, "vev", b.cnt["vev"] - 1)
            for half in range(2):
                for c in range(NCHUNK):
                    ins = nc.tensor.matmul(pv[:, half, :],
                                           xn[:, c, tt * 128 : (tt + 1) * 128],
                                           wsb[:, 2, c, half * SL : (half + 1) * SL],
                                           start=(c == 0), stop=(c == 7))
            b.inc(ins, "mm", 1)
            b.wait(nc.vector, "mm")
            ins = nc.vector.tensor_tensor(vsl[:, tt, :], pv.ap().rearrange("p a t -> p (a t)"),
                                          bvb[:], ALU.add)
            b.inc(ins, "vev", 1)
        b.drain_inc(nc.vector, "vdr")
        b.wait(nc.gpsimd, "vdr")
        ins = nc.gpsimd.dma_start(vtok[t0 : t0 + SL, :].rearrange("(q p) d -> p q d", p=128), vsl[:])
        b.inc(ins, "qst", 16)
        b.wait(nc.gpsimd, "qst")
        b.barrier()
    b.restore(m0)


def build_p1():
    nc = bass.Bass("TRN2", target_bir_lowering=False, debug=False)
    xs = nc.dram_tensor("xs", [NCHUNK, 128, T], F32, kind="ExternalInput")
    wqT = nc.dram_tensor("wqT", [D, D], F32, kind="ExternalInput")
    wkTa = nc.dram_tensor("wkTa", [D, D], F32, kind="ExternalInput")
    wvTa = nc.dram_tensor("wvTa", [D, D], F32, kind="ExternalInput")
    qkvbias = nc.dram_tensor("qkvbias", [128, 3, NCHUNK], F32, kind="ExternalInput")
    qkln = nc.dram_tensor("qkln", [128, 4], F32, kind="ExternalInput")
    bvrow = nc.dram_tensor("bvrow", [1, D], F32, kind="ExternalInput")
    hpatd = nc.dram_tensor("hpatd", [2, 128], F32, kind="ExternalInput")
    qT = nc.dram_tensor("qT", [NCHUNK, 128, T], BF16, kind="ExternalOutput")
    kT = nc.dram_tensor("kT", [NCHUNK, 128, T], BF16, kind="ExternalOutput")
    vtok = nc.dram_tensor("vtok", [T, D], BF16, kind="ExternalOutput")
    b = Bld(nc)
    b.setup_consts([EPS_LN, EPS_HLN, -0.5])
    emit_qkv(b, xs, wqT, wkTa, wvTa, qkvbias, qkln, bvrow, hpatd, qT, kT, vtok)
    return nc


def _p1_inputs(inputs, x_stage):
    def tile8(v):
        return np.ascontiguousarray(v.reshape(NCHUNK, 128).T)
    bias = np.stack([tile8(inputs["L0_attn_bq"]), tile8(inputs["L0_attn_bk"]),
                     tile8(inputs["L0_attn_bv"])], axis=1)
    qkln = np.stack([np.tile(inputs["L0_attn_qn_scale"], 2), np.tile(inputs["L0_attn_qn_bias"], 2),
                     np.tile(inputs["L0_attn_kn_scale"], 2), np.tile(inputs["L0_attn_kn_bias"], 2)],
                    axis=1)
    common = {
        "wqT": np.ascontiguousarray(inputs["L0_attn_Wq"]),
        "wkTa": np.ascontiguousarray(inputs["L0_attn_Wk"]),
        "wvTa": np.ascontiguousarray(inputs["L0_attn_Wv"]),
        "qkvbias": np.ascontiguousarray(bias),
        "qkln": np.ascontiguousarray(qkln),
        "bvrow": np.ascontiguousarray(inputs["L0_attn_bv"][None, :]),
        "hpatd": np.repeat(np.eye(2, dtype=np.float32), 64, axis=1),
    }
    maps = []
    for bc in range(8):
        d = dict(common)
        d["xs"] = np.ascontiguousarray(x_stage[bc].T).reshape(NCHUNK, 128, T)
        maps.append(d)
    return maps



def emit_qkv2(b, xs, wqR, wkR, wvR, qkvbias, qkln, bvrow, hpatd, qT, kT, vtok, tag=""):
    nc = b.nc
    m0 = b.mark()
    MU, E2i, VAR, RSTD, MR = 0, 1, 2, 3, 4
    wsb = b.sb(f"q2w{tag}", [128, 3, 8, 1024], BF16)
    bias = b.sb(f"q2b{tag}", [128, 3, 8], F32)
    qkl = b.sb(f"q2l{tag}", [128, 4], F32)
    bvb = b.sb(f"q2bvb{tag}", [128, 1024], F32)
    bvr = b.sb(f"q2bvr{tag}", [1, 1024], F32)
    xsl = b.sb(f"q2x{tag}", [128, 8, SL], F32)
    xb = b.sb(f"q2xb{tag}", [128, 8, SL], BF16)
    sqb = b.sb(f"q2sq{tag}", [128, 8, SL], BF16)
    xn = b.sb(f"q2xn{tag}", [128, 8, SL], BF16)
    qb = b.sb(f"q2qb{tag}", [128, 2, 8, SL], BF16)
    sqq = b.sb(f"q2sqq{tag}", [128, 2, SL], BF16)
    qn2 = b.sb(f"q2qn{tag}", [128, 2, 8, SL], BF16)
    vsl = b.sb(f"q2v{tag}", [128, 4, 1024], BF16)
    tmp = b.sb(f"q2tmp{tag}", [128, 4, SL], F32)
    rows = b.sb(f"q2rows{tag}", [1, 8, SL], F32)
    hrows = b.sb(f"q2hr{tag}", [2, 2, 5, SL], F32)
    onesb = b.sb(f"q2ones{tag}", [128, 1], BF16)
    ones1 = b.sb(f"q2one1{tag}", [1, 128], F32)
    blk2 = b.sb(f"q2blk{tag}", [128, 2], BF16)
    hpat = b.sb(f"q2hp{tag}", [2, 128], F32)
    ps_y = b.ps(f"q2psy{tag}", [128, 2, SL], F32)
    ps_h = b.ps(f"q2psh{tag}", [128, 2, SL], F32)
    ps_b = b.ps(f"q2psb{tag}", [128, 4, SL], F32)

    if getattr(b, "wcs_flat", 0) > 0:
        b.wait_at(nc.sync, "wcs", b.wcs_flat)
    if getattr(b, "w2s_flat", 0) > 0:
        b.wait_at(nc.sync, "w2s", b.w2s_flat)
    for wi, w in enumerate((wqR, wkR, wvR)):
        ins = nc.sync.dma_start(wsb[:, wi, :, :], w.ap().rearrange("(c p) m -> p c m", p=128))
        b.inc(ins, "wld", 16)
    for src, dst in ((qkvbias, bias), (qkln, qkl), (bvrow, bvr), (hpatd, hpat)):
        ins = nc.sync.dma_start(dst[:], src.ap())
        b.inc(ins, "wld", 16)
    b.wait(nc.vector, "wld")
    nc.vector.memset(onesb[:], 1.0)
    nc.vector.memset(ones1[:], 1.0)
    nc.vector.memset(blk2[:, :], 0.0)
    nc.vector.memset(blk2[0:64, 0:1], 1.0)
    ins = nc.vector.memset(blk2[64:128, 1:2], 1.0)
    b.inc(ins, "dv", 1)
    b.wait(nc.tensor, "dv")
    nc.tensor.matmul(ps_b[:, 0, :], ones1[:], bvr[:, 0:SL], start=True, stop=True)
    ins = nc.tensor.matmul(ps_b[:, 1, :], ones1[:], bvr[:, SL:D], start=True, stop=True)
    b.inc(ins, "mm", 1)
    b.wait(nc.vector, "mm")
    nc.vector.tensor_copy(bvb[:, 0:SL], ps_b[:, 0, :])
    ins = nc.vector.tensor_copy(bvb[:, SL:D], ps_b[:, 1, :])
    b.inc(ins, "dv", 1)
    b.barrier()

    for s in range(NSL):
        t0 = s * SL
        ins = nc.sync.dma_start(xsl[:], xs[:, :, t0 : t0 + SL].rearrange("c p t -> p c t"))
        b.inc(ins, "xld", 16)
        b.wait(nc.vector, "xld")
        _ln_to_xn(b, xsl, xn, xb, sqb, rows, onesb, ones1,
                  ps_h[0:1, 0, :], ps_h[64:65, 0, :], ps_b[:, 0:2, :], tmp[:, 0, :])
        b.wait(nc.tensor, "dv")
        qmm0 = b.cnt.get("qmm", 0)
        qbe0 = b.cnt.get("qbe", 0)
        qsq0 = b.cnt.get("qsq", 0)
        smm0 = b.cnt.get("smm", 0)
        dvv0 = b.cnt.get("dvv", 0)
        acs0 = b.cnt.get("acs", 0)
        hrm0 = b.cnt.get("hrm", 0)
        bmm0 = b.cnt.get("bmm", 0)
        apl0 = b.cnt.get("apl", 0)

        def P1(u):
            qk, ot = u // 8, u % 8
            if u >= 2:
                b.wait_at(nc.tensor, "qbe", qbe0 + u - 1)
            for c in range(NCHUNK):
                ins = nc.tensor.matmul(ps_y[:, u % 2, :], wsb[:, qk, c, ot * 128 : (ot + 1) * 128],
                                       xn[:, c, :], start=(c == 0), stop=(c == 7),
                                       skip_group_check=True)
            b.inc(ins, "qmm", 1)

        def P2(u):
            qk, ot = u // 8, u % 8
            b.wait_at(nc.tensor, "qsq", qsq0 + u + 1)
            if u >= 2:
                b.wait_at(nc.tensor, "dvv", dvv0 + u - 1)
            nc.tensor.matmul(ps_h[0:2, u % 2, :], blk2[:], qb[:, qk, ot, :],
                             start=True, stop=True, skip_group_check=True)
            ins = nc.tensor.matmul(ps_h[64:66, u % 2, :], blk2[:], sqq[:, u % 2, :],
                                   start=True, stop=True, skip_group_check=True)
            b.inc(ins, "smm", 1)

        def P3(u):
            hr = hrows[:, u % 2, :, :]
            b.wait_at(nc.tensor, "hrm", hrm0 + u + 1)
            if u >= 2:
                b.wait_at(nc.tensor, "apl", apl0 + u - 1)
            nc.tensor.matmul(ps_b[:, 2 * (u % 2), :], hpat[:], hr[:, RSTD, :],
                             start=True, stop=True, skip_group_check=True)
            ins = nc.tensor.matmul(ps_b[:, 2 * (u % 2) + 1, :], hpat[:], hr[:, MR, :],
                                   start=True, stop=True, skip_group_check=True)
            b.inc(ins, "bmm", 1)

        def E1(u):
            qk, ot = u // 8, u % 8
            b.wait_at(nc.vector, "qmm", qmm0 + u + 1)
            ins = nc.vector.tensor_scalar_add(qb[:, qk, ot, :], ps_y[:, u % 2, :],
                                              bias[:, qk, ot : ot + 1])
            b.inc(ins, "qbe", 1)

        def E2(u):
            qk, ot = u // 8, u % 8
            b.wait_at(nc.scalar, "qbe", qbe0 + u + 1)
            if u >= 2:
                b.wait_at(nc.scalar, "smm", smm0 + u - 1)
            ins = nc.scalar.activation(sqq[:, u % 2, :], qb[:, qk, ot, :], AF.Square)
            b.inc(ins, "qsq", 1)

        def E3(u):
            hr = hrows[:, u % 2, :, :]
            b.wait_at(nc.vector, "smm", smm0 + u + 1)
            if u >= 2:
                b.wait_at(nc.vector, "bmm", bmm0 + u - 1)
            nc.vector.tensor_scalar_mul(hr[:, MU, :], ps_h[0:2, u % 2, :], 1.0 / 64)
            nc.vector.tensor_scalar_mul(hr[:, E2i, :], ps_h[64:66, u % 2, :], 1.0 / 64)
            nc.vector.tensor_tensor(hr[:, VAR, :], hr[:, MU, :], hr[:, MU, :], ALU.mult)
            ins = nc.vector.tensor_tensor(hr[:, VAR, :], hr[:, E2i, :], hr[:, VAR, :], ALU.subtract)
            b.inc(ins, "dvv", 1)

        def A1(u):
            hr = hrows[:, u % 2, :, :]
            b.wait_at(nc.scalar, "dvv", dvv0 + u + 1)
            nc.scalar.activation(hr[:, E2i, :], hr[:, VAR, :], AF.Ln, bias=EPS_HLN)
            ins = nc.scalar.activation(hr[:, RSTD, :], hr[:, E2i, :], AF.Exp, scale=-0.5)
            b.inc(ins, "acs", 1)

        def E4(u):
            hr = hrows[:, u % 2, :, :]
            b.wait_at(nc.vector, "acs", acs0 + u + 1)
            ins = nc.vector.tensor_tensor(hr[:, MR, :], hr[:, MU, :], hr[:, RSTD, :], ALU.mult)
            b.inc(ins, "hrm", 1)

        def E5(u):
            qk, ot = u // 8, u % 8
            SC, BI = (0, 1) if qk == 0 else (2, 3)
            b.wait_at(nc.vector, "bmm", bmm0 + u + 1)
            nc.vector.tensor_tensor(tmp[:, 2, :], qb[:, qk, ot, :], ps_b[:, 2 * (u % 2), :],
                                    ALU.mult)
            nc.vector.tensor_tensor(tmp[:, 3, :], tmp[:, 2, :], ps_b[:, 2 * (u % 2) + 1, :],
                                    ALU.subtract)
            ins = nc.vector.tensor_scalar(qn2[:, qk, ot, :], tmp[:, 3, :],
                                          qkl[:, SC : SC + 1], qkl[:, BI : BI + 1],
                                          ALU.mult, ALU.add)
            b.inc(ins, "apl", 1)

        for u in range(18):
            if u < 16:
                P1(u)
                E1(u)
                E2(u)
            if 1 <= u <= 16:
                P2(u - 1)
                E3(u - 1)
                A1(u - 1)
                E4(u - 1)
            if u >= 2:
                P3(u - 2)
                E5(u - 2)

        b.wait(nc.tensor, "apl")
        for tt in range(4):
            pv = ps_y if tt % 2 == 0 else ps_b
            if tt >= 2:
                b.wait_at(nc.tensor, "vev", b.cnt["vev"] - 1)
            for half in range(2):
                for c in range(NCHUNK):
                    ins = nc.tensor.matmul(pv[:, half, :], xn[:, c, tt * 128 : (tt + 1) * 128],
                                           wsb[:, 2, c, half * SL : (half + 1) * SL],
                                           start=(c == 0), stop=(c == 7), skip_group_check=True)
            b.inc(ins, "mm", 1)
            b.wait(nc.vector, "mm")
            nc.vector.tensor_tensor(vsl[:, tt, 0:SL], pv[:, 0, :], bvb[:, 0:SL], ALU.add)
            ins = nc.vector.tensor_tensor(vsl[:, tt, SL:D], pv[:, 1, :], bvb[:, SL:D], ALU.add)
            b.inc(ins, "vev", 1)
        b.drain_inc(nc.vector, "vdr")
        b.wait(nc.gpsimd, "vdr")
        ins = nc.gpsimd.dma_start(qT[:, :, t0 : t0 + SL].rearrange("c p t -> p c t"), qn2[:, 0, :, :])
        b.inc(ins, "qst", 16)
        ins = nc.gpsimd.dma_start(kT[:, :, t0 : t0 + SL].rearrange("c p t -> p c t"), qn2[:, 1, :, :])
        b.inc(ins, "qst", 16)
        ins = nc.gpsimd.dma_start(vtok[t0 : t0 + SL, :].rearrange("(q p) d -> p q d", p=128), vsl[:])
        b.inc(ins, "qst", 16)
        b.wait(nc.gpsimd, "qst")
        b.barrier()
    b.restore(m0)


def emit_attn(b, xs, qT, kT, vtok, woT, bo8, gam8, ibt, tag=""):
    nc = b.nc
    m0 = b.mark()
    wsb = b.sb(f"p2w{tag}", [128, 8, 1024], BF16)
    wf = b.sb(f"p2wf{tag}", [128, 8, 1024], F32)
    kTs = b.sb(f"p2k{tag}", [128, 8, T], BF16)
    qTs = b.sb(f"p2q{tag}", [128, 8, SL], BF16)
    vts = b.sb(f"p2v{tag}", [128, 2, 1024], BF16)
    ssb = b.sb(f"p2s{tag}", [128, 2, SL], BF16)
    att = b.sb(f"p2a{tag}", [128, 8, SL], BF16)
    xsl = b.sb(f"p2x{tag}", [128, 8, SL], F32)
    tmp = b.sb(f"p2t{tag}", [128, SL], F32)
    bo = b.sb(f"p2bo{tag}", [128, 8], F32)
    gam = b.sb(f"p2g{tag}", [128, 8], F32)
    ibb = b.sb(f"p2ib{tag}", [128, 1], F32)
    ps_s = b.ps(f"p2pss{tag}", [128, 2, SL], F32)
    ps_o = b.ps(f"p2pso{tag}", [128, 4, SL], F32)
    ps_y = b.ps(f"p2psy{tag}", [128, 2, SL], F32)

    ins = nc.sync.dma_start(wf[:], woT.ap().rearrange("(c p) m -> p c m", p=128))
    b.inc(ins, "wld", 16)
    b.wait(nc.vector, "wld")
    ins = nc.vector.tensor_copy(wsb[:], wf[:])
    b.inc(ins, "wfree", 1)
    ins = nc.sync.dma_start(kTs[:], kT.ap().rearrange("c p t -> p c t"))
    b.inc(ins, "wld", 16)
    for src, dst in ((bo8, bo), (gam8, gam), (ibt, ibb)):
        ins = nc.sync.dma_start(dst[:], src.ap())
        b.inc(ins, "wld", 16)
    b.barrier()

    for s in range(NSL):
        t0 = s * SL
        ins = nc.sync.dma_start(qTs[:], qT[:, :, t0 : t0 + SL].rearrange("c p t -> p c t"))
        b.inc(ins, "xld", 16)
        ins = nc.sync.dma_start(xsl[:], xs[:, :, t0 : t0 + SL].rearrange("c p t -> p c t"))
        b.inc(ins, "xld", 16)
        b.wait(nc.tensor, "xld")
        for hp in range(2):
            if b.cnt.get("aev", 0) > 0:
                b.wait(nc.tensor, "aev")
            vld_at = {}
            for j in range(T // 128):
                par = j % 2
                if j == 0:
                    b.wait(nc.sync, "ov")
                    ins = nc.sync.dma_start(vts[:, 0, :], vtok[0:128, :])
                    b.inc(ins, "vld", 16)
                    vld_at[0] = b.cnt["vld"]
                if j + 1 < T // 128:
                    if j >= 1:
                        b.wait(nc.sync, "ov")
                    ins = nc.sync.dma_start(vts[:, (j + 1) % 2, :], vtok[(j + 1) * 128 : (j + 2) * 128, :])
                    b.inc(ins, "vld", 16)
                    vld_at[j + 1] = b.cnt["vld"]
                b.wait_at(nc.tensor, "vld", vld_at[j])
                sg0 = b.cnt.get("sg", 0)
                ov0 = b.cnt.get("ov", 0)
                for h8 in range(8):
                    h = hp * 8 + h8
                    hc, hr = h // 2, (h % 2) * 64
                    spar = h8 % 2
                    if sg0 + h8 - 1 > 0:
                        b.wait_at(nc.tensor, "sg", sg0 + h8 - 1)
                    ins = nc.tensor.matmul(ps_s[:, spar, :],
                                           kTs[hr : hr + 64, hc, j * 128 : (j + 1) * 128],
                                           qTs[hr : hr + 64, hc, :], start=True, stop=True,
                                           skip_group_check=True)
                    b.inc(ins, "sm", 1)
                    b.wait(nc.scalar, "sm")
                    if ov0 + h8 - 1 > 0:
                        b.wait_at(nc.scalar, "ov", ov0 + h8 - 1)
                    ins = nc.scalar.activation(ssb[:, spar, :], ps_s[:, spar, :], AF.Sigmoid,
                                               bias=ibb[:, 0:1])
                    b.inc(ins, "sg", 1)
                    b.wait(nc.tensor, "sg")
                    ins = nc.tensor.matmul(ps_o[(h8 % 2) * 64 : (h8 % 2) * 64 + 64, h8 // 2, :],
                                           vts[:, par, h * 64 : (h + 1) * 64],
                                           ssb[:, spar, :], start=(j == 0), stop=(j == T // 128 - 1),
                                           skip_group_check=True)
                    b.inc(ins, "ov", 1)
            b.wait(nc.vector, "ov")
            for h8 in range(8):
                h = hp * 8 + h8
                hc, hr = h // 2, (h % 2) * 64
                ins = nc.vector.tensor_copy(att[hr : hr + 64, hc, :],
                                            ps_o[(h8 % 2) * 64 : (h8 % 2) * 64 + 64, h8 // 2, :])
                b.inc(ins, "aev", 1)
        b.wait(nc.tensor, "aev")
        for ot in range(NCHUNK):
            par = ot % 2
            if b.cnt.get("pev", 0) >= 2:
                b.wait_at(nc.tensor, "pev", b.cnt["pev"] - 2)
            for c in range(NCHUNK):
                ins = nc.tensor.matmul(ps_y[:, par, :], wsb[:, c, ot * 128 : (ot + 1) * 128],
                                       att[:, c, :], start=(c == 0), stop=(c == 7))
            b.inc(ins, "mm", 1)
            b.wait(nc.vector, "mm")
            nc.vector.tensor_scalar(tmp[:], ps_y[:, par, :], bo[:, ot : ot + 1],
                                    gam[:, ot : ot + 1], ALU.add, ALU.mult)
            ins = nc.vector.tensor_tensor(xsl[:, ot, :], tmp[:], xsl[:, ot, :], ALU.add)
            b.inc(ins, "pev", 2)
        b.drain_inc(nc.vector, "vdr")
        b.wait(nc.gpsimd, "vdr")
        ins = nc.gpsimd.dma_start(xs[:, :, t0 : t0 + SL].rearrange("c p t -> p c t"), xsl[:])
        b.inc(ins, "xst", 16)
        b.wait(nc.gpsimd, "xst")
        b.barrier()
    b.restore(m0)


def build_p2():
    nc = bass.Bass("TRN2", target_bir_lowering=False, debug=False)
    xs = nc.dram_tensor("xs", [NCHUNK, 128, T], F32, kind="ExternalInput")
    xso = nc.dram_tensor("xso", [NCHUNK, 128, T], F32, kind="ExternalOutput")
    qT = nc.dram_tensor("qT", [NCHUNK, 128, T], BF16, kind="ExternalInput")
    kT = nc.dram_tensor("kT", [NCHUNK, 128, T], BF16, kind="ExternalInput")
    vtok = nc.dram_tensor("vtok", [T, D], BF16, kind="ExternalInput")
    woT = nc.dram_tensor("woT", [D, D], F32, kind="ExternalInput")
    bo8 = nc.dram_tensor("bo8", [128, NCHUNK], F32, kind="ExternalInput")
    gam8 = nc.dram_tensor("gam8", [128, NCHUNK], F32, kind="ExternalInput")
    ibt = nc.dram_tensor("ibt", [128, 1], F32, kind="ExternalInput")
    b = Bld(nc)
    b.setup_consts([EPS_LN, EPS_HLN, -0.5])
    emit_dram_copy(b, xso, xs)
    emit_attn(b, xso, qT, kT, vtok, woT, bo8, gam8, ibt)
    return nc


def _p2_inputs(inputs, x_stage, qn, kn, v):
    def tile8(vv):
        return np.ascontiguousarray(vv.reshape(NCHUNK, 128).T)
    common = {
        "woT": np.ascontiguousarray(inputs["L0_attn_Wo"]),
        "bo8": tile8(inputs["L0_attn_bo"]),
        "gam8": tile8(inputs["L0_attn_gamma"]),
        "ibt": np.ascontiguousarray(np.tile(inputs["L0_attn_init_bias"].reshape(1, 1), (128, 1)).astype(np.float32)),
    }
    maps = []
    for bc in range(8):
        d = dict(common)
        d["xs"] = np.ascontiguousarray(x_stage[bc].T).reshape(NCHUNK, 128, T)
        d["qT"] = np.ascontiguousarray(qn[bc].T.astype(BF)).reshape(NCHUNK, 128, T)
        d["kT"] = np.ascontiguousarray(kn[bc].T.astype(BF)).reshape(NCHUNK, 128, T)
        d["vtok"] = np.ascontiguousarray(v[bc].astype(BF))
        maps.append(d)
    return maps



def _ln_to_xn(b, xsl, xn, xb, sqb, rows, onesb, ones1, ps_s, ps_q, ps_b, tmpc):
    nc = b.nc
    MU, E2, VAR, RSTD, MR = 0, 1, 2, 3, 4
    nc.vector.tensor_copy(xb[:], xsl[:])
    ins = nc.vector.tensor_tensor(sqb[:], xb[:], xb[:], ALU.mult)
    b.inc(ins, "dv", 1)
    b.wait(nc.tensor, "dv")
    for c in range(NCHUNK):
        nc.tensor.matmul(ps_s[:], onesb[:], xb[:, c, :], start=(c == 0), stop=(c == 7))
    for c in range(NCHUNK):
        ins = nc.tensor.matmul(ps_q[:], onesb[:], sqb[:, c, :], start=(c == 0), stop=(c == 7))
    b.inc(ins, "mm", 1)
    b.wait(nc.vector, "mm")
    nc.vector.tensor_scalar_mul(rows[:, MU, :], ps_s[:], 1.0 / D)
    nc.vector.tensor_scalar_mul(rows[:, E2, :], ps_q[:], 1.0 / D)
    nc.vector.tensor_tensor(rows[:, VAR, :], rows[:, MU, :], rows[:, MU, :], ALU.mult)
    ins = nc.vector.tensor_tensor(rows[:, VAR, :], rows[:, E2, :], rows[:, VAR, :], ALU.subtract)
    b.inc(ins, "dv", 1)
    b.wait(nc.scalar, "dv")
    nc.scalar.activation(rows[:, E2, :], rows[:, VAR, :], AF.Ln, bias=EPS_LN)
    ins = nc.scalar.activation(rows[:, RSTD, :], rows[:, E2, :], AF.Exp, scale=-0.5)
    b.inc(ins, "ac", 1)
    b.wait(nc.vector, "ac")
    ins = nc.vector.tensor_tensor(rows[:, MR, :], rows[:, MU, :], rows[:, RSTD, :], ALU.mult)
    b.inc(ins, "dv", 1)
    b.wait(nc.tensor, "dv")
    nc.tensor.matmul(ps_b[:, 0, :], ones1[:], rows[:, RSTD, :], start=True, stop=True)
    ins = nc.tensor.matmul(ps_b[:, 1, :], ones1[:], rows[:, MR, :], start=True, stop=True)
    b.inc(ins, "mm", 1)
    b.wait(nc.vector, "mm")
    for c in range(NCHUNK):
        nc.vector.tensor_tensor(tmpc[:], xsl[:, c, :], ps_b[:, 0, :], ALU.mult)
        ins = nc.vector.tensor_tensor(xn[:, c, :], tmpc[:], ps_b[:, 1, :], ALU.subtract)
    b.inc(ins, "dv", 1)


def emit_wcast(b, pairs, tag=""):
    nc = b.nc
    m0 = b.mark()
    wf = b.sb(f"wcf{tag}", [128, 2, 4096], F32)
    wb = b.sb(f"wcb{tag}", [128, 2, 4096], BF16)
    for src, dst, Kd, Md in pairs:
        for kc in range(Kd // 128):
            par = kc % 2
            if b.cnt.get("wcv", 0) >= 2:
                b.wait_at(nc.scalar, "wcv", b.cnt["wcv"] - 1)
            ins = nc.scalar.dma_start(wf[:, par, 0:Md], src[kc * 128 : (kc + 1) * 128, :])
            b.inc(ins, "wcl", 16)
            b.wait(nc.vector, "wcl")
            if b.cnt.get("wcs", 0) >= 2:
                b.wait_at(nc.vector, "wcs", b.cnt["wcs"] - 16)
            ins = nc.vector.tensor_copy(wb[:, par, 0:Md], wf[:, par, 0:Md])
            b.inc(ins, "wcv", 1)
            b.drain_inc(nc.vector, "wcd")
            b.wait(nc.gpsimd, "wcd")
            ins = nc.gpsimd.dma_start(
                dst[:, :, kc * 128 : (kc + 1) * 128].rearrange("o p m -> p o m"),
                wb[:, par, 0:Md].rearrange("p (o m) -> p o m", m=128))
            b.inc(ins, "wcs", 16)
    b.restore(m0)


def emit_cast_flat(b, pairs, tag=""):
    nc = b.nc
    m0 = b.mark()
    wf = b.sb(f"cff{tag}", [128, 2, 4096], F32)
    wb = b.sb(f"cfb{tag}", [128, 2, 4096], BF16)
    for src, dst, Kd, Md in pairs:
        for kc in range(Kd // 128):
            par = kc % 2
            if b.cnt.get("wcv", 0) >= 2:
                b.wait_at(nc.scalar, "wcv", b.cnt["wcv"] - 1)
            ins = nc.scalar.dma_start(wf[:, par, 0:Md], src[kc * 128 : (kc + 1) * 128, :])
            b.inc(ins, "wcl", 16)
            b.wait(nc.vector, "wcl")
            if b.cnt.get("wcs", 0) >= 2:
                b.wait_at(nc.vector, "wcs", b.cnt["wcs"] - 16)
            ins = nc.vector.tensor_copy(wb[:, par, 0:Md], wf[:, par, 0:Md])
            b.inc(ins, "wcv", 1)
            b.drain_inc(nc.vector, "wcd")
            b.wait(nc.gpsimd, "wcd")
            ins = nc.gpsimd.dma_start(dst[kc * 128 : (kc + 1) * 128, :], wb[:, par, 0:Md])
            b.inc(ins, "wcs", 16)
    b.restore(m0)


def emit_casts(b, bufs, flat_pairs, blk_pairs, tag=""):
    nc = b.nc
    wfA, wbA, wfB, wbB = bufs
    MT = 1024
    jobs = []
    for src, dst, Kd, Md in flat_pairs:
        for kc in range(Kd // 128):
            for mc in range(Md // MT):
                jobs.append((src, dst, kc, mc, False))
    for src, dst, Kd, Md in blk_pairs:
        for kc in range(Kd // 128):
            for mc in range(Md // MT):
                jobs.append((src, dst, kc, mc, True))
    for gi, (src, dst, kc, mc, blk) in enumerate(jobs):
        lane = 0
        i = b.cnt.get("wcv", 0) if lane == 0 else b.cnt.get("w2l", 0) // 16
        par = i % 2
        sl = src[kc * 128 : (kc + 1) * 128, mc * MT : (mc + 1) * MT]
        if blk:
            dsl = dst[mc * (MT // 128) : (mc + 1) * (MT // 128), :,
                      kc * 128 : (kc + 1) * 128].rearrange("o p m -> p o m")
            wv = None
        else:
            dsl = dst[kc * 128 : (kc + 1) * 128, mc * MT : (mc + 1) * MT]
        if lane == 0:
            if i >= 2:
                b.wait_at(nc.scalar, "wcv", b.cnt["wcv"] - 1)
            ins = nc.scalar.dma_start(wfA[:, par, :], sl)
            b.inc(ins, "wcl", 16)
            b.wait(nc.vector, "wcl")
            if i >= 2:
                b.wait_at(nc.vector, "wcs", b.cnt["wcs"] - 16)
            ins = nc.vector.tensor_copy(wbA[:, par, :], wfA[:, par, :])
            b.inc(ins, "wcv", 1)
            b.drain_inc(nc.vector, "wcd")
            b.wait(nc.gpsimd, "wcd")
            ins = nc.gpsimd.dma_start(dsl, wbA[:, par, :].rearrange("p (o m) -> p o m", m=128)
                                      if blk else wbA[:, par, :])
            b.inc(ins, "wcs", 16)
        else:
            if i >= 2:
                b.wait_at(nc.gpsimd, "w2s", b.cnt["w2s"] - 16)
            ins = nc.gpsimd.dma_start(wfB[:, par, :], sl)
            b.inc(ins, "w2l", 16)
            b.wait(nc.gpsimd, "w2l")
            ins = nc.gpsimd.tensor_copy(wbB[:, par, :], wfB[:, par, :])
            b.drain_inc(nc.gpsimd, "w2d")
            ins = nc.gpsimd.dma_start(dsl, wbB[:, par, :].rearrange("p (o m) -> p o m", m=128)
                                      if blk else wbB[:, par, :])
            b.inc(ins, "w2s", 16)


def emit_attn2(b, xs, qT, kT, vtok, woR, bo8, gam8, ibt, tag=""):
    nc = b.nc
    m0 = b.mark()
    NJ = T // 128
    kTs = b.sb(f"a2k{tag}", [128, 8, T], BF16)
    vres = b.sb(f"a2v{tag}", [128, NJ, 1024], BF16)
    wsb = b.sb(f"a2w{tag}", [128, 8, 1024], BF16)
    qTs = b.sb(f"a2q{tag}", [128, 8, SL], BF16)
    ssb = b.sb(f"a2s{tag}", [128, 2, 1024], BF16)
    att = b.sb(f"a2a{tag}", [128, 8, SL], BF16)
    xsl = b.sb(f"a2x{tag}", [128, 8, SL], F32)
    tmp = b.sb(f"a2t{tag}", [128, SL], F32)
    bo = b.sb(f"a2bo{tag}", [128, 8], F32)
    gam = b.sb(f"a2g{tag}", [128, 8], F32)
    ibb = b.sb(f"a2ib{tag}", [128, 1], F32)
    ps_s = b.ps(f"a2pss{tag}", [128, 4, SL], F32)
    ps_o = b.ps(f"a2pso{tag}", [128, 2, SL], F32)
    ps_y = b.ps(f"a2psy{tag}", [128, 2, SL], F32)

    if getattr(b, "wcs_flat", 0) > 0:
        b.wait_at(nc.sync, "wcs", b.wcs_flat)
    if getattr(b, "w2s_flat", 0) > 0:
        b.wait_at(nc.sync, "w2s", b.w2s_flat)
    ins = nc.sync.dma_start(kTs[:], kT.ap().rearrange("c p t -> p c t"))
    b.inc(ins, "wld", 16)
    for j0 in range(0, NJ, 8):
        ins = nc.sync.dma_start(
            vres[:, j0 : j0 + 8, :],
            vtok[j0 * 128 : (j0 + 8) * 128, :].rearrange("(j p) d -> p j d", p=128))
        b.inc(ins, "wld", 16)
    ins = nc.sync.dma_start(wsb[:], woR.ap().rearrange("(c p) m -> p c m", p=128))
    b.inc(ins, "wld", 16)
    for src, dst in ((bo8, bo), (gam8, gam), (ibt, ibb)):
        ins = nc.sync.dma_start(dst[:], src.ap())
        b.inc(ins, "wld", 16)
    b.barrier()

    U = 8 * NJ
    for s in range(NSL):
        t0 = s * SL
        ins = nc.sync.dma_start(qTs[:], qT[:, :, t0 : t0 + SL].rearrange("c p t -> p c t"))
        b.inc(ins, "xld", 16)
        ins = nc.sync.dma_start(xsl[:], xs[:, :, t0 : t0 + SL].rearrange("c p t -> p c t"))
        b.inc(ins, "xld", 16)
        amm0 = b.cnt.get("amm", 0)
        asg0 = b.cnt.get("asg", 0)
        aov0 = b.cnt.get("aov", 0)
        aevc0 = b.cnt.get("aevc", 0)
        b.wait(nc.tensor, "xld")

        def logits(u):
            p, j = u // NJ, u % NJ
            sp = u % 2
            if u >= 2:
                b.wait_at(nc.tensor, "asg", asg0 + u - 1)
            nc.tensor.matmul(ps_s[:, 2 * sp, :], kTs[0:64, p, j * 128 : (j + 1) * 128],
                             qTs[0:64, p, :], start=True, stop=True, skip_group_check=True)
            ins = nc.tensor.matmul(ps_s[:, 2 * sp + 1, :], kTs[64:128, p, j * 128 : (j + 1) * 128],
                                   qTs[64:128, p, :], start=True, stop=True, skip_group_check=True)
            b.inc(ins, "amm", 1)

        def attv(u):
            p, j = u // NJ, u % NJ
            sp = u % 2
            po = p % 2
            b.wait_at(nc.tensor, "asg", asg0 + u + 1)
            if j == 0 and p >= 2:
                b.wait_at(nc.tensor, "aevc", aevc0 + p - 1)
            nc.tensor.matmul(ps_o[0:64, po, :], vres[:, j, p * 128 : p * 128 + 64],
                             ssb[:, sp, 0:SL], start=(j == 0), stop=(j == NJ - 1),
                             skip_group_check=True)
            ins = nc.tensor.matmul(ps_o[64:128, po, :], vres[:, j, p * 128 + 64 : p * 128 + 128],
                                   ssb[:, sp, SL : 2 * SL], start=(j == 0), stop=(j == NJ - 1),
                                   skip_group_check=True)
            b.inc(ins, "aov", 1)
            if j == NJ - 1:
                b.wait(nc.vector, "aov")
                ins = nc.vector.tensor_copy(att[:, p, :], ps_o[:, po, :])
                b.inc(ins, "aevc", 1)

        def sig(u):
            sp = u % 2
            b.wait_at(nc.scalar, "amm", amm0 + u + 1)
            if u >= 2:
                b.wait_at(nc.scalar, "aov", aov0 + u - 1)
            ins = nc.scalar.activation(ssb[:, sp, :],
                                       ps_s[:, 2 * sp : 2 * sp + 2, :].rearrange("p a t -> p (a t)"),
                                       AF.Sigmoid, bias=ibb[:, 0:1])
            b.inc(ins, "asg", 1)

        logits(0)
        sig(0)
        for u in range(1, U):
            logits(u)
            sig(u)
            attv(u - 1)
        attv(U - 1)

        b.wait(nc.tensor, "aevc")
        for ot in range(NCHUNK):
            par = ot % 2
            if b.cnt.get("pev", 0) >= 2:
                b.wait_at(nc.tensor, "pev", b.cnt["pev"] - 2)
            for c in range(NCHUNK):
                ins = nc.tensor.matmul(ps_y[:, par, :], wsb[:, c, ot * 128 : (ot + 1) * 128],
                                       att[:, c, :], start=(c == 0), stop=(c == 7))
            b.inc(ins, "mm", 1)
            b.wait(nc.vector, "mm")
            nc.vector.tensor_scalar(tmp[:], ps_y[:, par, :], bo[:, ot : ot + 1],
                                    gam[:, ot : ot + 1], ALU.add, ALU.mult)
            ins = nc.vector.tensor_tensor(xsl[:, ot, :], tmp[:], xsl[:, ot, :], ALU.add)
            b.inc(ins, "pev", 2)
        b.drain_inc(nc.vector, "vdr")
        b.wait(nc.gpsimd, "vdr")
        ins = nc.gpsimd.dma_start(xs[:, :, t0 : t0 + SL].rearrange("c p t -> p c t"), xsl[:])
        b.inc(ins, "xst", 16)
        b.wait(nc.gpsimd, "xst")
        b.barrier()
    b.restore(m0)


def emit_l1ffn(b, xs, w1R, w2R, h1R, h2R, w3R, scl, tag=""):
    nc = b.nc
    m0 = b.mark()
    NH = DHID // 128
    xsl = b.sb(f"p4x{tag}", [128, 2, 8, SL], F32)
    xb = b.sb(f"p4xb{tag}", [128, 8, SL], BF16)
    sqb = b.sb(f"p4sq{tag}", [128, 8, SL], BF16)
    xn = b.sb(f"p4xn{tag}", [128, 2, 8, SL], BF16)
    hb = b.sb(f"p4h{tag}", [128, NH, SL], BF16)
    hb2 = b.sb(f"p4h2{tag}", [128, NH, SL], BF16)
    sil = b.sb(f"p4sil{tag}", [128, 2, SL], F32)
    tmp = b.sb(f"p4t{tag}", [128, 2, SL], F32)
    rows = b.sb(f"p4rows{tag}", [1, 8, SL], F32)
    onesb = b.sb(f"p4ones{tag}", [128, 1], BF16)
    ones1 = b.sb(f"p4one1{tag}", [1, 128], F32)
    sc = b.sb(f"p4sc{tag}", [128, 5, NH], F32)
    w12 = b.sb(f"p4w12{tag}", [128, 2, 2, 8, 128], BF16)
    wh = b.sb(f"p4wh{tag}", [128, 2, 32, 128], BF16)
    ps_s = b.ps(f"p4pss{tag}", [1, SL], F32)
    ps_q = b.ps(f"p4psq{tag}", [1, SL], F32)
    ps_b = b.ps(f"p4psb{tag}", [128, 2, SL], F32)
    ps_g = b.ps(f"p4psg{tag}", [128, 2, SL], F32)
    ps_h = b.ps(f"p4psh{tag}", [128, 2, SL], F32)

    if b.cnt.get("wcs", 0) > 0:
        b.wait(nc.sync, "wcs")
    if b.cnt.get("w2s", 0) > 0:
        b.wait(nc.sync, "w2s")
    ins = nc.sync.dma_start(sc[:], scl.ap())
    b.inc(ins, "wld", 16)
    b.wait(nc.vector, "wld")
    nc.vector.memset(onesb[:], 1.0)
    nc.vector.memset(ones1[:], 1.0)
    b.barrier()

    xst0 = b.cnt.get("xst", 0)
    hevA_prev = None
    for s in range(NSL):
        t0 = s * SL
        sp = s % 2
        xsl_s = xsl[:, sp]
        xn_s = xn[:, sp]
        if s >= 2:
            b.wait_at(nc.sync, "xst", xst0 + 16 * (s - 1))
        ins = nc.sync.dma_start(xsl_s[:], xs[:, :, t0 : t0 + SL].rearrange("c p t -> p c t"))
        b.inc(ins, "xld", 16)
        b.wait(nc.vector, "xld")
        _ln_to_xn(b, xsl_s, xn_s, xb, sqb, rows, onesb, ones1, ps_s, ps_q, ps_b, tmp[:, 0, :])
        b.wait(nc.tensor, "dv")
        if b.cnt.get("dev", 0) > 0:
            b.wait(nc.tensor, "dev")
        wat = {}
        for ot in range(NH):
            par = ot % 2
            if ot == 0:
                if hevA_prev is not None:
                    b.wait_at(nc.sync, "hev", hevA_prev)
                ins = nc.sync.dma_start(w12[:, 0, 0, :, :], w1R[0, :, :])
                b.inc(ins, "wld", 16)
                ins = nc.sync.dma_start(w12[:, 0, 1, :, :], w2R[0, :, :])
                b.inc(ins, "wld", 16)
                wat[0] = b.cnt["wld"]
            if ot + 1 < NH:
                if ot >= 1:
                    b.wait_at(nc.sync, "hev", b.cnt["hev"])
                ins = nc.sync.dma_start(w12[:, (ot + 1) % 2, 0, :, :], w1R[ot + 1, :, :])
                b.inc(ins, "wld", 16)
                ins = nc.sync.dma_start(w12[:, (ot + 1) % 2, 1, :, :], w2R[ot + 1, :, :])
                b.inc(ins, "wld", 16)
                wat[ot + 1] = b.cnt["wld"]
            b.wait_at(nc.tensor, "wld", wat[ot])
            if b.cnt.get("hev", 0) >= 2:
                b.wait_at(nc.tensor, "hev", b.cnt["hev"] - 1)
            for c in range(NCHUNK):
                nc.tensor.matmul(ps_g[:, par, :], w12[:, par, 0, c, :], xn_s[:, c, :],
                                 start=(c == 0), stop=(c == 7), skip_group_check=True)
            for c in range(NCHUNK):
                ins = nc.tensor.matmul(ps_h[:, par, :], w12[:, par, 1, c, :], xn_s[:, c, :],
                                       start=(c == 0), stop=(c == 7), skip_group_check=True)
            b.inc(ins, "mm", 1)
            b.wait(nc.scalar, "mm")
            ins = nc.scalar.activation(sil[:, par, :], ps_g[:, par, :], AF.Silu,
                                       scale=sc[:, 0, ot : ot + 1])
            b.inc(ins, "ac", 1)
            b.wait(nc.vector, "ac")
            ins = nc.vector.scalar_tensor_tensor(hb[:, ot, :], ps_h[:, par, :],
                                                 sc[:, 1, ot : ot + 1], sil[:, par, :],
                                                 ALU.mult, ALU.mult)
            b.inc(ins, "hev", 1)
        hevA_prev = b.cnt["hev"]
        for stage, (wR, hsrc, hdst, sci) in enumerate(
                ((h1R, hb, hb2, 2), (h2R, hb2, hb, 3))):
            b.wait(nc.tensor, "hev")
            wat = {}
            for ot in range(NH):
                par = ot % 2
                if ot == 0:
                    if stage == 0:
                        if b.cnt.get("dev", 0) > 0:
                            b.wait(nc.sync, "dev")
                    else:
                        b.wait(nc.sync, "hev")
                    ins = nc.sync.dma_start(wh[:, 0, :, :], wR[0, :, :])
                    b.inc(ins, "wld", 16)
                    wat[0] = b.cnt["wld"]
                if ot + 1 < NH:
                    if ot >= 1:
                        b.wait_at(nc.sync, "hev", b.cnt["hev"])
                    ins = nc.sync.dma_start(wh[:, (ot + 1) % 2, :, :], wR[ot + 1, :, :])
                    b.inc(ins, "wld", 16)
                    wat[ot + 1] = b.cnt["wld"]
                b.wait_at(nc.tensor, "wld", wat[ot])
                if b.cnt.get("hev", 0) >= 2:
                    b.wait_at(nc.tensor, "hev", b.cnt["hev"] - 1)
                for kc in range(NH):
                    ins = nc.tensor.matmul(ps_g[:, par, :], wh[:, par, kc, :], hsrc[:, kc, :],
                                           start=(kc == 0), stop=(kc == NH - 1),
                                           skip_group_check=True)
                b.inc(ins, "mm", 1)
                b.wait(nc.scalar, "mm")
                ins = nc.scalar.activation(hdst[:, ot, :], ps_g[:, par, :], AF.Gelu,
                                           scale=sc[:, sci, ot : ot + 1])
                b.inc(ins, "hev", 1)
        b.wait(nc.tensor, "hev")
        wat = {}
        for ot in range(NCHUNK):
            par = ot % 2
            if ot == 0:
                b.wait(nc.sync, "dev")
                b.wait(nc.sync, "hev")
                ins = nc.sync.dma_start(wh[:, 0, :, :], w3R[0, :, :])
                b.inc(ins, "wld", 16)
                wat[0] = b.cnt["wld"]
            if ot + 1 < NCHUNK:
                if ot >= 1:
                    b.wait_at(nc.sync, "dev", b.cnt["dev"])
                ins = nc.sync.dma_start(wh[:, (ot + 1) % 2, :, :], w3R[ot + 1, :, :])
                b.inc(ins, "wld", 16)
                wat[ot + 1] = b.cnt["wld"]
            b.wait_at(nc.tensor, "wld", wat[ot])
            if b.cnt.get("dev", 0) >= 2:
                b.wait_at(nc.tensor, "dev", b.cnt["dev"] - 1)
            for kc in range(NH):
                ins = nc.tensor.matmul(ps_g[:, par, :], wh[:, par, kc, :], hb[:, kc, :],
                                       start=(kc == 0), stop=(kc == NH - 1),
                                       skip_group_check=True)
            b.inc(ins, "mm", 1)
            b.wait(nc.vector, "mm")
            nc.vector.tensor_scalar(tmp[:, 1, :], ps_g[:, par, :], sc[:, 4, ot : ot + 1],
                                    0.5, ALU.mult, ALU.mult)
            ins = nc.vector.tensor_tensor(xsl_s[:, ot, :], tmp[:, 1, :], xsl_s[:, ot, :], ALU.add)
            b.inc(ins, "dev", 1)
        b.drain_inc(nc.vector, "vdr")
        b.wait(nc.gpsimd, "vdr")
        ins = nc.gpsimd.dma_start(xs[:, :, t0 : t0 + SL].rearrange("c p t -> p c t"), xsl_s[:])
        b.inc(ins, "xst", 16)
    b.wait(nc.gpsimd, "xst")
    b.barrier()
    b.restore(m0)


def build_wprep():
    nc = bass.Bass("TRN2", target_bir_lowering=False, debug=False)
    w1T = nc.dram_tensor("w1T", [D, DHID], F32, kind="ExternalInput")
    w2T = nc.dram_tensor("w2T", [D, DHID], F32, kind="ExternalInput")
    h1T = nc.dram_tensor("h1T", [DHID, DHID], F32, kind="ExternalInput")
    h2T = nc.dram_tensor("h2T", [DHID, DHID], F32, kind="ExternalInput")
    w3T = nc.dram_tensor("w3T", [DHID, D], F32, kind="ExternalInput")
    w1R = nc.dram_tensor("w1R", [32, 128, D], BF16, kind="ExternalOutput")
    w2R = nc.dram_tensor("w2R", [32, 128, D], BF16, kind="ExternalOutput")
    h1R = nc.dram_tensor("h1R", [32, 128, DHID], BF16, kind="ExternalOutput")
    h2R = nc.dram_tensor("h2R", [32, 128, DHID], BF16, kind="ExternalOutput")
    w3R = nc.dram_tensor("w3R", [8, 128, DHID], BF16, kind="ExternalOutput")
    b = Bld(nc)
    emit_wcast(b, [(w1T, w1R, D, DHID), (w2T, w2R, D, DHID), (h1T, h1R, DHID, DHID),
                   (h2T, h2R, DHID, DHID), (w3T, w3R, DHID, D)])
    return nc


def build_p4():
    nc = bass.Bass("TRN2", target_bir_lowering=False, debug=False)
    xs = nc.dram_tensor("xs", [NCHUNK, 128, T], F32, kind="ExternalInput")
    xso = nc.dram_tensor("xso", [NCHUNK, 128, T], F32, kind="ExternalOutput")
    w1R = nc.dram_tensor("w1R", [32, 128, D], BF16, kind="ExternalInput")
    w2R = nc.dram_tensor("w2R", [32, 128, D], BF16, kind="ExternalInput")
    h1R = nc.dram_tensor("h1R", [32, 128, DHID], BF16, kind="ExternalInput")
    h2R = nc.dram_tensor("h2R", [32, 128, DHID], BF16, kind="ExternalInput")
    w3R = nc.dram_tensor("w3R", [8, 128, DHID], BF16, kind="ExternalInput")
    scl = nc.dram_tensor("scl", [128, 5, 32], F32, kind="ExternalInput")
    b = Bld(nc)
    b.setup_consts([EPS_LN, EPS_HLN, -0.5])
    emit_dram_copy(b, xso, xs)
    emit_l1ffn(b, xso, w1R, w2R, h1R, h2R, w3R, scl)
    return nc


def _wprep_inputs(inputs):
    return {
        "w1T": np.ascontiguousarray(inputs["L1_ffn_w1_w"].T),
        "w2T": np.ascontiguousarray(inputs["L1_ffn_w2_w"].T),
        "h1T": np.ascontiguousarray(inputs["L1_ffn_h1_w"].T),
        "h2T": np.ascontiguousarray(inputs["L1_ffn_h2_w"].T),
        "w3T": np.ascontiguousarray(inputs["L1_ffn_w3_w"].T),
    }


def _p4_scl(inputs):
    scl = np.zeros((128, 5, 32), np.float32)
    scl[:, 0, :] = inputs["L1_ffn_w1_s"].reshape(32, 128).T
    scl[:, 1, :] = inputs["L1_ffn_w2_s"].reshape(32, 128).T
    scl[:, 2, :] = inputs["L1_ffn_h1_s"].reshape(32, 128).T
    scl[:, 3, :] = inputs["L1_ffn_h2_s"].reshape(32, 128).T
    scl[:, 4, 0:8] = inputs["L1_ffn_w3_s"].reshape(8, 128).T
    return np.ascontiguousarray(scl)



def emit_bu(b, xs, bT, BuT, tag=""):
    nc = b.nc
    m0 = b.mark()
    wsb = b.sb(f"p5w{tag}", [128, 8, 1024], BF16)
    wf = b.sb(f"p5wf{tag}", [128, 8, 1024], F32)
    xsl = b.sb(f"p5x{tag}", [128, 8, SL], F32)
    xb = b.sb(f"p5xb{tag}", [128, 8, SL], BF16)
    sqb = b.sb(f"p5sq{tag}", [128, 8, SL], BF16)
    xn = b.sb(f"p5xn{tag}", [128, 8, SL], BF16)
    bub = b.sb(f"p5bu{tag}", [128, 8, SL], BF16)
    tmp = b.sb(f"p5t{tag}", [128, SL], F32)
    rows = b.sb(f"p5rows{tag}", [1, 8, SL], F32)
    onesb = b.sb(f"p5ones{tag}", [128, 1], BF16)
    ones1 = b.sb(f"p5one1{tag}", [1, 128], F32)
    ps_s = b.ps(f"p5pss{tag}", [1, SL], F32)
    ps_q = b.ps(f"p5psq{tag}", [1, SL], F32)
    ps_b = b.ps(f"p5psb{tag}", [128, 2, SL], F32)
    ps_y = b.ps(f"p5psy{tag}", [128, 2, SL], F32)

    ins = nc.sync.dma_start(wf[:], bT.ap().rearrange("(c p) m -> p c m", p=128))
    b.inc(ins, "wld", 16)
    b.wait(nc.vector, "wld")
    nc.vector.tensor_copy(wsb[:], wf[:])
    nc.vector.memset(onesb[:], 1.0)
    nc.vector.memset(ones1[:], 1.0)
    b.barrier()
    for s in range(NSL):
        t0 = s * SL
        ins = nc.sync.dma_start(xsl[:], xs[:, :, t0 : t0 + SL].rearrange("c p t -> p c t"))
        b.inc(ins, "xld", 16)
        b.wait(nc.vector, "xld")
        _ln_to_xn(b, xsl, xn, xb, sqb, rows, onesb, ones1, ps_s, ps_q, ps_b, tmp)
        b.wait(nc.tensor, "dv")
        for ot in range(NCHUNK):
            par = ot % 2
            if b.cnt.get("bev", 0) >= 2:
                b.wait_at(nc.tensor, "bev", b.cnt["bev"] - 1)
            for c in range(NCHUNK):
                ins = nc.tensor.matmul(ps_y[:, par, :], wsb[:, c, ot * 128 : (ot + 1) * 128],
                                       xn[:, c, :], start=(c == 0), stop=(c == 7))
            b.inc(ins, "mm", 1)
            b.wait(nc.vector, "mm")
            ins = nc.vector.tensor_copy(bub[:, ot, :], ps_y[:, par, :])
            b.inc(ins, "bev", 1)
        b.drain_inc(nc.vector, "vdr")
        b.wait(nc.gpsimd, "vdr")
        ins = nc.gpsimd.dma_start(BuT[:, :, t0 : t0 + SL].rearrange("c p t -> p c t"), bub[:])
        b.inc(ins, "bst", 16)
        b.wait(nc.gpsimd, "bst")
        b.barrier()
    b.restore(m0)


def emit_rnn(b, BuT, aT, idn, hT, tag=""):
    nc = b.nc
    m0 = b.mark()
    asb = b.sb(f"p6a{tag}", [128, 8, 1024], BF16)
    af = b.sb(f"p6af{tag}", [128, 8, 1024], F32)
    ids = b.sb(f"p6id{tag}", [128, 128], BF16)
    bub = b.sb(f"p6bu{tag}", [128, 8, T + 16], BF16)
    ht = b.sb(f"p6ht{tag}", [128, 2, 8], BF16)
    hst = b.sb(f"p6h{tag}", [128, 8, T], BF16)
    ps_a = b.ps(f"p6psa{tag}", [128, 8], F32)
    ps_b2 = b.ps(f"p6psb{tag}", [128, 8], F32)

    ins = nc.sync.dma_start(af[:], aT.ap().rearrange("(c p) m -> p c m", p=128))
    b.inc(ins, "wld", 16)
    ins = nc.sync.dma_start(ids[:], idn.ap())
    b.inc(ins, "wld", 16)
    ins = nc.sync.dma_start(bub[:, :, 0:T], BuT.ap().rearrange("c p t -> p c t"))
    b.inc(ins, "wld", 16)
    b.wait(nc.vector, "wld")
    nc.vector.tensor_copy(asb[:], af[:])
    nc.vector.memset(ht[:, 0, :], 0.0)
    nc.vector.memset(bub[:, :, T : T + 16], 0.0)
    b.barrier()

    G = b.cnt.get("gl", 0)
    M = b.cnt.get("mm", 0)
    C = b.cnt.get("hcp", 0)
    UN = 8
    nc.tensor.matmul(ps_a[:], ids[:], bub[:, :, 0:1], start=True, stop=False,
                     skip_group_check=True)
    with nc.Fori(0, T // UN, engines=[ET.PE, ET.Activation, ET.DVE]) as j:
        for sub in range(UN):
            cur, nxt = sub % 2, 1 - sub % 2
            tv = j * UN + sub
            pcur = ps_a if cur == 0 else ps_b2
            pnxt = ps_b2 if cur == 0 else ps_a
            nc.tensor.wait_ge(b.sem("gl"), G + tv)
            for m in range(8):
                for k in range(8):
                    ins = nc.tensor.matmul(
                        pcur[:, m : m + 1],
                        asb[:, k, m * 128 : (m + 1) * 128],
                        ht[:, cur, k : k + 1],
                        start=False, stop=(k == 7), skip_group_check=True)
                    if m == 7 and k == 7:
                        ins.then_inc(b.sem("mm"), 1)
            nc.tensor.matmul(pnxt[:], ids[:], bub[:, :, bass.ds(tv + 1, 1)],
                             start=True, stop=False, skip_group_check=True)
            nc.scalar.wait_ge(b.sem("mm"), M + tv + 1)
            ins = nc.scalar.activation(ht[:, nxt, :], pcur[:], AF.Gelu)
            ins.then_inc(b.sem("gl"), 1)
            nc.vector.wait_ge(b.sem("gl"), G + tv + 1)
            ins = nc.vector.tensor_copy(hst[:, :, bass.ds(tv, 1)], ht[:, nxt, :])
            ins.then_inc(b.sem("hcp"), 1)
    b.cnt["gl"] = G + T
    b.cnt["mm"] = M + T
    b.cnt["hcp"] = C + T
    b.wait(nc.vector, "hcp")
    b.drain_inc(nc.vector, "vdr")
    b.wait(nc.gpsimd, "vdr")
    ins = nc.gpsimd.dma_start(hT.ap().rearrange("c p t -> p c t"), hst[:])
    b.inc(ins, "hst", 16)
    b.wait(nc.gpsimd, "hst")
    b.barrier()
    b.restore(m0)


def emit_cout(b, xs, hT, cT, gb8, tag=""):
    nc = b.nc
    m0 = b.mark()
    wsb = b.sb(f"p7w{tag}", [128, 8, 1024], BF16)
    wf = b.sb(f"p7wf{tag}", [128, 8, 1024], F32)
    gb = b.sb(f"p7gb{tag}", [128, 2, 8], F32)
    xsl = b.sb(f"p7x{tag}", [128, 8, SL], F32)
    hsl = b.sb(f"p7h{tag}", [128, 8, SL], BF16)
    yf = b.sb(f"p7y{tag}", [128, 8, SL], F32)
    yb = b.sb(f"p7yb{tag}", [128, 8, SL], BF16)
    sqb = b.sb(f"p7sq{tag}", [128, 8, SL], BF16)
    tmp = b.sb(f"p7t{tag}", [128, 2, SL], F32)
    rows = b.sb(f"p7rows{tag}", [1, 8, SL], F32)
    onesb = b.sb(f"p7ones{tag}", [128, 1], BF16)
    ones1 = b.sb(f"p7one1{tag}", [1, 128], F32)
    ps_s = b.ps(f"p7pss{tag}", [1, SL], F32)
    ps_q = b.ps(f"p7psq{tag}", [1, SL], F32)
    ps_b = b.ps(f"p7psb{tag}", [128, 2, SL], F32)
    ps_y = b.ps(f"p7psy{tag}", [128, 2, SL], F32)
    MU, E2, VAR, RSTD, MR = 0, 1, 2, 3, 4

    ins = nc.sync.dma_start(wf[:], cT.ap().rearrange("(c p) m -> p c m", p=128))
    b.inc(ins, "wld", 16)
    ins = nc.sync.dma_start(gb[:], gb8.ap())
    b.inc(ins, "wld", 16)
    b.wait(nc.vector, "wld")
    nc.vector.tensor_copy(wsb[:], wf[:])
    nc.vector.memset(onesb[:], 1.0)
    nc.vector.memset(ones1[:], 1.0)
    b.barrier()
    for s in range(NSL):
        t0 = s * SL
        ins = nc.sync.dma_start(xsl[:], xs[:, :, t0 : t0 + SL].rearrange("c p t -> p c t"))
        b.inc(ins, "xld", 16)
        ins = nc.sync.dma_start(hsl[:], hT[:, :, t0 : t0 + SL].rearrange("c p t -> p c t"))
        b.inc(ins, "xld", 16)
        b.wait(nc.tensor, "xld")
        for ot in range(NCHUNK):
            par = ot % 2
            if b.cnt.get("cev", 0) >= 2:
                b.wait_at(nc.tensor, "cev", b.cnt["cev"] - 2)
            for c in range(NCHUNK):
                ins = nc.tensor.matmul(ps_y[:, par, :], wsb[:, c, ot * 128 : (ot + 1) * 128],
                                       hsl[:, c, :], start=(c == 0), stop=(c == 7))
            b.inc(ins, "mm", 1)
            b.wait(nc.vector, "mm")
            nc.vector.tensor_copy(yf[:, ot, :], ps_y[:, par, :])
            ins = nc.vector.tensor_copy(yb[:, ot, :], ps_y[:, par, :])
            b.inc(ins, "cev", 2)
        ins = nc.vector.tensor_tensor(sqb[:], yb[:], yb[:], ALU.mult)
        b.inc(ins, "ac", 1)
        b.wait(nc.tensor, "ac")
        for c in range(NCHUNK):
            nc.tensor.matmul(ps_s[:], onesb[:], yb[:, c, :], start=(c == 0), stop=(c == 7))
        for c in range(NCHUNK):
            ins = nc.tensor.matmul(ps_q[:], onesb[:], sqb[:, c, :], start=(c == 0), stop=(c == 7))
        b.inc(ins, "mm", 1)
        b.wait(nc.vector, "mm")
        nc.vector.tensor_scalar_mul(rows[:, MU, :], ps_s[:], 1.0 / D)
        nc.vector.tensor_scalar_mul(rows[:, E2, :], ps_q[:], 1.0 / D)
        nc.vector.tensor_tensor(rows[:, VAR, :], rows[:, MU, :], rows[:, MU, :], ALU.mult)
        ins = nc.vector.tensor_tensor(rows[:, VAR, :], rows[:, E2, :], rows[:, VAR, :], ALU.subtract)
        b.inc(ins, "dv", 1)
        b.wait(nc.scalar, "dv")
        nc.scalar.activation(rows[:, E2, :], rows[:, VAR, :], AF.Ln, bias=EPS_LN)
        ins = nc.scalar.activation(rows[:, RSTD, :], rows[:, E2, :], AF.Exp, scale=-0.5)
        b.inc(ins, "ac", 1)
        b.wait(nc.vector, "ac")
        ins = nc.vector.tensor_tensor(rows[:, MR, :], rows[:, MU, :], rows[:, RSTD, :], ALU.mult)
        b.inc(ins, "dv", 1)
        b.wait(nc.tensor, "dv")
        nc.tensor.matmul(ps_b[:, 0, :], ones1[:], rows[:, RSTD, :], start=True, stop=True)
        ins = nc.tensor.matmul(ps_b[:, 1, :], ones1[:], rows[:, MR, :], start=True, stop=True)
        b.inc(ins, "mm", 1)
        b.wait(nc.vector, "mm")
        for c in range(NCHUNK):
            nc.vector.tensor_tensor(tmp[:, 0, :], yf[:, c, :], ps_b[:, 0, :], ALU.mult)
            nc.vector.tensor_tensor(tmp[:, 1, :], tmp[:, 0, :], ps_b[:, 1, :], ALU.subtract)
            nc.vector.tensor_scalar(tmp[:, 0, :], tmp[:, 1, :], gb[:, 0, c : c + 1],
                                    gb[:, 1, c : c + 1], ALU.mult, ALU.add)
            ins = nc.vector.tensor_tensor(xsl[:, c, :], tmp[:, 0, :], xsl[:, c, :], ALU.add)
        b.inc(ins, "dv", 1)
        b.drain_inc(nc.vector, "vdr")
        b.wait(nc.gpsimd, "vdr")
        ins = nc.gpsimd.dma_start(xs[:, :, t0 : t0 + SL].rearrange("c p t -> p c t"), xsl[:])
        b.inc(ins, "xst", 16)
        b.wait(nc.gpsimd, "xst")
        b.barrier()
    b.restore(m0)


def build_p5():
    nc = bass.Bass("TRN2", target_bir_lowering=False, debug=False)
    xs = nc.dram_tensor("xs", [NCHUNK, 128, T], F32, kind="ExternalInput")
    bT = nc.dram_tensor("bT", [D, D], F32, kind="ExternalInput")
    aT = nc.dram_tensor("aT", [D, D], F32, kind="ExternalInput")
    idn = nc.dram_tensor("idn", [128, 128], BF16, kind="ExternalInput")
    BuT = nc.dram_tensor("BuT", [NCHUNK, 128, T], BF16, kind="ExternalOutput")
    hT = nc.dram_tensor("hT", [NCHUNK, 128, T], BF16, kind="ExternalOutput")
    b = Bld(nc)
    b.setup_consts([EPS_LN, EPS_HLN, -0.5])
    emit_bu(b, xs, bT, BuT)
    emit_rnn(b, BuT, aT, idn, hT)
    return nc


def build_p6():
    nc = bass.Bass("TRN2", target_bir_lowering=False, debug=False)
    xs = nc.dram_tensor("xs", [NCHUNK, 128, T], F32, kind="ExternalInput")
    xso = nc.dram_tensor("xso", [NCHUNK, 128, T], F32, kind="ExternalOutput")
    hT = nc.dram_tensor("hT", [NCHUNK, 128, T], BF16, kind="ExternalInput")
    cT = nc.dram_tensor("cT", [D, D], F32, kind="ExternalInput")
    gb8 = nc.dram_tensor("gb8", [128, 2, NCHUNK], F32, kind="ExternalInput")
    b = Bld(nc)
    b.setup_consts([EPS_LN, EPS_HLN, -0.5])
    emit_dram_copy(b, xso, xs)
    emit_cout(b, xso, hT, cT, gb8)
    return nc



def _core_inputs(inputs, bcore):
    x = inputs["L0_attn_x"][bcore]
    xT = np.ascontiguousarray(x.T).reshape(NCHUNK, 128, T)
    d = {"xs": xT}
    d["wrT"] = np.ascontiguousarray(inputs["L0_ffn_W_r"].T)
    d["wkT"] = np.ascontiguousarray(inputs["L0_ffn_W_k"].T)
    d["wvT"] = np.ascontiguousarray(inputs["L0_ffn_W_v"].T)
    murk = np.stack([inputs["L0_ffn_mu_r"].reshape(NCHUNK, 128).T,
                     inputs["L0_ffn_mu_k"].reshape(NCHUNK, 128).T], axis=1)
    d["murk"] = np.ascontiguousarray(murk)
    return d


def emit_dram_copy(b, dst, src, nbytes_hint=None):
    nc = b.nc
    m0 = b.mark()
    cp = b.sb("dcp", [128, NCHUNK, T // 4], F32)
    for i in range(4):
        if i > 0:
            b.wait(nc.sync, "dcp")
        ins = nc.sync.dma_start(cp[:], src[:, :, i * (T // 4) : (i + 1) * (T // 4)].rearrange("c p t -> p c t"))
        b.inc(ins, "dcp", 16)
        b.wait(nc.sync, "dcp")
        ins = nc.sync.dma_start(dst[:, :, i * (T // 4) : (i + 1) * (T // 4)].rearrange("c p t -> p c t"), cp[:])
        b.inc(ins, "dcp", 16)
    b.wait(nc.sync, "dcp")
    b.barrier()
    b.restore(m0)


def build_p0():
    nc = bass.Bass("TRN2", target_bir_lowering=False, debug=False)
    xs = nc.dram_tensor("xs", [NCHUNK, 128, T], F32, kind="ExternalInput")
    xso = nc.dram_tensor("xso", [NCHUNK, 128, T], F32, kind="ExternalOutput")
    wrT = nc.dram_tensor("wrT", [D, D], F32, kind="ExternalInput")
    wkT = nc.dram_tensor("wkT", [D, D], F32, kind="ExternalInput")
    wvT = nc.dram_tensor("wvT", [D, D], F32, kind="ExternalInput")
    murk = nc.dram_tensor("murk", [128, 2, NCHUNK], F32, kind="ExternalInput")
    b = Bld(nc)
    b.setup_consts([EPS_LN, EPS_HLN, -0.5])
    emit_dram_copy(b, xso, xs)
    emit_l0ffn(b, xso, wrT, wkT, wvT, murk)
    return nc


def build_fused():
    nc = bass.Bass("TRN2", target_bir_lowering=False, debug=False)
    xin = nc.dram_tensor("xin", [NCHUNK, 128, T], F32, kind="ExternalInput")
    xout = nc.dram_tensor("xout", [NCHUNK, 128, T], F32, kind="ExternalOutput")
    wrT = nc.dram_tensor("wrT", [D, D], F32, kind="ExternalInput")
    wkT = nc.dram_tensor("wkT", [D, D], F32, kind="ExternalInput")
    wvT = nc.dram_tensor("wvT", [D, D], F32, kind="ExternalInput")
    murk = nc.dram_tensor("murk", [128, 2, NCHUNK], F32, kind="ExternalInput")
    wqT = nc.dram_tensor("wqT", [D, D], F32, kind="ExternalInput")
    wkTa = nc.dram_tensor("wkTa", [D, D], F32, kind="ExternalInput")
    wvTa = nc.dram_tensor("wvTa", [D, D], F32, kind="ExternalInput")
    qkvbias = nc.dram_tensor("qkvbias", [128, 3, NCHUNK], F32, kind="ExternalInput")
    qkln = nc.dram_tensor("qkln", [128, 4], F32, kind="ExternalInput")
    bvrow = nc.dram_tensor("bvrow", [1, D], F32, kind="ExternalInput")
    hpatd = nc.dram_tensor("hpatd", [2, 128], F32, kind="ExternalInput")
    woT = nc.dram_tensor("woT", [D, D], F32, kind="ExternalInput")
    bo8 = nc.dram_tensor("bo8", [128, NCHUNK], F32, kind="ExternalInput")
    gam8 = nc.dram_tensor("gam8", [128, NCHUNK], F32, kind="ExternalInput")
    ibt = nc.dram_tensor("ibt", [128, 1], F32, kind="ExternalInput")
    w1T = nc.dram_tensor("w1T", [D, DHID], F32, kind="ExternalInput")
    w2T = nc.dram_tensor("w2T", [D, DHID], F32, kind="ExternalInput")
    h1T = nc.dram_tensor("h1T", [DHID, DHID], F32, kind="ExternalInput")
    h2T = nc.dram_tensor("h2T", [DHID, DHID], F32, kind="ExternalInput")
    w3T = nc.dram_tensor("w3T", [DHID, D], F32, kind="ExternalInput")
    scl = nc.dram_tensor("scl", [128, 5, 32], F32, kind="ExternalInput")
    bT = nc.dram_tensor("bT", [D, D], F32, kind="ExternalInput")
    aT = nc.dram_tensor("aT", [D, D], F32, kind="ExternalInput")
    cT = nc.dram_tensor("cT", [D, D], F32, kind="ExternalInput")
    idn = nc.dram_tensor("idn", [128, 128], BF16, kind="ExternalInput")
    gb8 = nc.dram_tensor("gb8", [128, 2, NCHUNK], F32, kind="ExternalInput")
    qT = nc.dram_tensor("qT", [NCHUNK, 128, T], BF16, kind="Internal")
    kT = nc.dram_tensor("kT", [NCHUNK, 128, T], BF16, kind="Internal")
    vtok = nc.dram_tensor("vtok", [T, D], BF16, kind="Internal")
    BuT = nc.dram_tensor("BuT", [NCHUNK, 128, T], BF16, kind="Internal")
    hT = nc.dram_tensor("hT", [NCHUNK, 128, T], BF16, kind="Internal")
    w1R = nc.dram_tensor("w1R", [32, 128, D], BF16, kind="Internal")
    w2R = nc.dram_tensor("w2R", [32, 128, D], BF16, kind="Internal")
    h1R = nc.dram_tensor("h1R", [32, 128, DHID], BF16, kind="Internal")
    h2R = nc.dram_tensor("h2R", [32, 128, DHID], BF16, kind="Internal")
    w3R = nc.dram_tensor("w3R", [8, 128, DHID], BF16, kind="Internal")
    woR = nc.dram_tensor("woR", [D, D], BF16, kind="Internal")
    wqR = nc.dram_tensor("wqR", [D, D], BF16, kind="Internal")
    wkR = nc.dram_tensor("wkR", [D, D], BF16, kind="Internal")
    wvR = nc.dram_tensor("wvR", [D, D], BF16, kind="Internal")
    wrR = nc.dram_tensor("wrR", [D, D], BF16, kind="Internal")
    wkfR = nc.dram_tensor("wkfR", [D, D], BF16, kind="Internal")
    wvfR = nc.dram_tensor("wvfR", [D, D], BF16, kind="Internal")

    b = Bld(nc)
    b.setup_consts([EPS_LN, EPS_HLN, -0.5])
    emit_cast_flat(b, [(wqT, wqR, D, D), (wkTa, wkR, D, D), (wvTa, wvR, D, D),
                       (woT, woR, D, D)])
    b.wcs_flat = b.cnt.get("wcs", 0)
    emit_wcast(b, [(w1T, w1R, D, DHID), (w2T, w2R, D, DHID), (h1T, h1R, DHID, DHID),
                   (h2T, h2R, DHID, DHID), (w3T, w3R, DHID, D)])
    emit_l0ffn(b, xout, wrT, wkT, wvT, murk, tag="a", xsrc=xin)
    emit_qkv2(b, xout, wqR, wkR, wvR, qkvbias, qkln, bvrow, hpatd, qT, kT, vtok)
    emit_attn2(b, xout, qT, kT, vtok, woR, bo8, gam8, ibt)
    emit_l0ffn(b, xout, wrT, wkT, wvT, murk, tag="b")
    emit_l1ffn(b, xout, w1R, w2R, h1R, h2R, w3R, scl, tag="a")
    emit_bu(b, xout, bT, BuT)
    emit_rnn(b, BuT, aT, idn, hT)
    emit_cout(b, xout, hT, cT, gb8)
    emit_l1ffn(b, xout, w1R, w2R, h1R, h2R, w3R, scl, tag="b")
    return nc


def _fused_inputs(inputs):
    def tile8(v):
        return np.ascontiguousarray(v.reshape(NCHUNK, 128).T)
    qkln = np.stack([np.tile(inputs["L0_attn_qn_scale"], 2), np.tile(inputs["L0_attn_qn_bias"], 2),
                     np.tile(inputs["L0_attn_kn_scale"], 2), np.tile(inputs["L0_attn_kn_bias"], 2)],
                    axis=1).astype(np.float32)
    common = {
        "wrT": np.ascontiguousarray(inputs["L0_ffn_W_r"].T),
        "wkT": np.ascontiguousarray(inputs["L0_ffn_W_k"].T),
        "wvT": np.ascontiguousarray(inputs["L0_ffn_W_v"].T),
        "murk": np.ascontiguousarray(np.stack(
            [inputs["L0_ffn_mu_r"].reshape(NCHUNK, 128).T,
             inputs["L0_ffn_mu_k"].reshape(NCHUNK, 128).T], axis=1)),
        "wqT": np.ascontiguousarray(inputs["L0_attn_Wq"]),
        "wkTa": np.ascontiguousarray(inputs["L0_attn_Wk"]),
        "wvTa": np.ascontiguousarray(inputs["L0_attn_Wv"]),
        "qkvbias": np.ascontiguousarray(np.stack(
            [tile8(inputs["L0_attn_bq"]), tile8(inputs["L0_attn_bk"]),
             tile8(inputs["L0_attn_bv"])], axis=1)),
        "qkln": np.ascontiguousarray(qkln),
        "bvrow": np.ascontiguousarray(inputs["L0_attn_bv"][None, :]),
        "hpatd": np.repeat(np.eye(2, dtype=np.float32), 64, axis=1),
        "woT": np.ascontiguousarray(inputs["L0_attn_Wo"]),
        "bo8": tile8(inputs["L0_attn_bo"]),
        "gam8": tile8(inputs["L0_attn_gamma"]),
        "ibt": np.ascontiguousarray(np.tile(inputs["L0_attn_init_bias"].reshape(1, 1),
                                            (128, 1)).astype(np.float32)),
        "w1T": np.ascontiguousarray(inputs["L1_ffn_w1_w"].T),
        "w2T": np.ascontiguousarray(inputs["L1_ffn_w2_w"].T),
        "h1T": np.ascontiguousarray(inputs["L1_ffn_h1_w"].T),
        "h2T": np.ascontiguousarray(inputs["L1_ffn_h2_w"].T),
        "w3T": np.ascontiguousarray(inputs["L1_ffn_w3_w"].T),
        "scl": _p4_scl(inputs),
        "bT": np.ascontiguousarray(inputs["L1_attn_B"].T),
        "aT": np.ascontiguousarray(inputs["L1_attn_A"].T),
        "cT": np.ascontiguousarray(inputs["L1_attn_C"].T),
        "idn": np.eye(128).astype(BF),
        "gb8": np.ascontiguousarray(np.stack(
            [inputs["L1_attn_gamma"].reshape(NCHUNK, 128).T,
             inputs["L1_attn_beta"].reshape(NCHUNK, 128).T], axis=1)),
    }
    return common


def kernel(**inputs):
    nc = build_fused()
    common = _fused_inputs(inputs)
    x = inputs["L0_attn_x"].astype(np.float32, copy=False)
    in_maps = [dict(common, xin=_x_to_dev(x[bc])) for bc in range(8)]
    res = run_bass_kernel_spmd(nc, in_maps, core_ids=list(range(8)))
    out = np.stack([_x_from_dev(res.results[bc]["xout"]) for bc in range(8)])
    return out.astype(np.float32)


def _x_to_dev(x2d):
    return np.ascontiguousarray(x2d.T).reshape(NCHUNK, 128, T)


def _x_from_dev(xdev):
    return xdev.reshape(D, T).T


